# version 07
# baseline (speedup 1.0000x reference)
import sys as _sys
for _p in ("/opt/trn_rl_repo",):
    if _p not in _sys.path:
        _sys.path.insert(0, _p)
"""Shared raw-Bass builder helpers for the mk_seq_preln_d2 kernel.

Conventions:
- Residual stream and most activations are FEATURE-major: [d (128-chunks on
  partitions) x tokens (free)].
- Weights arrive in HBM pre-transposed to [d_in, d_out] fp32/int32 (host
  layout only); device casts matmul operands to bf16.
- Phase-A sync style: sequential nc.Block() phases (Block exit = all-engine
  barrier) + explicit semaphores inside a phase where pipelining needs them.
- PSUM pool (8 banks total), shared by every helper:
    psA, psB      [128, 512]  (1 bank each)
    psX, psY      [128, 1024] (2 banks each)
    psR1, psR2    [1, 512]    (1 bank each)
"""
import contextlib
import numpy as np
import concourse.bass as bass
import concourse.mybir as mybir
from concourse.bass import IndirectOffsetOnAxis
from concourse.bass_utils import run_bass_kernel_spmd

F32 = mybir.dt.float32
BF16 = mybir.dt.bfloat16
I32 = mybir.dt.int32
U8 = mybir.dt.uint8
AF = mybir.ActivationFunctionType
ALU = mybir.AluOpType

T = 2048
D = 1024
NCHUNK = D // 128

LN_EPS = 1e-5


class Ctx:
    def __init__(self, nc, es):
        self.nc = nc
        self.es = es
        self.n_sems = 0
        self._named = {}
        self.semc = {}

    def sbuf(self, name, shape, dtype):
        return self.es.enter_context(self.nc.sbuf_tensor(name, shape, dtype))

    def psum(self, name, shape, dtype=F32):
        return self.es.enter_context(self.nc.psum_tensor(name, shape, dtype))

    def sem(self, name=None):
        self.n_sems += 1
        return self.es.enter_context(
            self.nc.semaphore(f"{name or 's'}{self.n_sems}"))

    def named_sem(self, tag):
        if tag not in self._named:
            self._named[tag] = self.es.enter_context(self.nc.semaphore(tag))
            self.semc[tag] = 0
        return self._named[tag]


def make_ctx(dma_scratch=16384):
    nc = bass.Bass("TRN2", target_bir_lowering=False,
                   dynamic_dma_scratch_size=dma_scratch)
    es = contextlib.ExitStack()
    return Ctx(nc, es)


def load_consts(ctx, pool=True):
    nc = ctx.nc
    ctx.ones_col = ctx.sbuf("ones_col", [1, 128], F32)
    ctx.ones_col_bf = ctx.sbuf("ones_col_bf", [1, 128], BF16)
    ctx.ones128 = ctx.sbuf("ones128", [128, 1], F32)
    ctx.ones128_bf = ctx.sbuf("ones128_bf", [128, 1], BF16)
    ctx.eps_col = ctx.sbuf("eps_col", [128, 1], F32)
    if pool:
        ctx.psA = ctx.psum("psA", [128, 512])
        ctx.psB = ctx.psum("psB", [128, 512])
        ctx.psX = ctx.psum("psX", [128, 1024])
        ctx.psY = ctx.psum("psY", [128, 1024])
        ctx.psR1 = ctx.psum("psR1", [1, 512])
        ctx.psR2 = ctx.psum("psR2", [1, 512])
    with nc.Block() as block:
        @block.gpsimd
        def _(g):
            g.memset(ctx.ones_col[:], 1.0)
            g.memset(ctx.ones_col_bf[:], 1.0)
            g.memset(ctx.ones128[:], 1.0)
            g.memset(ctx.ones128_bf[:], 1.0)
            g.memset(ctx.eps_col[:], LN_EPS)


def dma_load(ctx, pairs, engines=("sync", "gpsimd", "scalar")):
    nc = ctx.nc
    per = {}
    for i, (dst, src) in enumerate(pairs):
        per.setdefault(engines[i % len(engines)], []).append((dst, src))
    waits = []
    with nc.Block() as block:
        def mk(items, sem, base):
            def f(eng):
                for dst, src in items:
                    eng.dma_start(out=dst, in_=src).then_inc(sem, 16)
            return f
        for eng_name, items in per.items():
            tag = "dma_ld_sw" if eng_name == "gpsimd" else f"dma_ld_{eng_name}"
            sem = ctx.named_sem(tag)
            base = ctx.semc[tag]
            ctx.semc[tag] = base + 16 * len(items)
            waits.append((sem, ctx.semc[tag]))
            getattr(block, eng_name)(mk(items, sem, base))
    with nc.Block() as block:
        @block.sync
        def _(s):
            for sem, val in waits:
                s.wait_ge(sem, val)


def dma_store(ctx, pairs):
    nc = ctx.nc
    sem = ctx.named_sem("dma_st")
    base = ctx.semc["dma_st"]
    n = len(pairs)
    ctx.semc["dma_st"] = base + 16 * n
    with nc.Block() as block:
        @block.gpsimd
        def _(g):
            for dst, src in pairs:
                g.dma_start(out=dst, in_=src).then_inc(sem, 16)
            g.wait_ge(sem, base + 16 * n)


def free_slices(Tn, step=512):
    return [slice(f, min(f + step, Tn)) for f in range(0, Tn, step)]


def emit_ln(ctx, x_chunks, out_chunks, aux, g_row=None, b_row=None,
            out2_chunks=None):
    nc = ctx.nc
    Tn = x_chunks[0].shape[-1]
    fsl = free_slices(Tn)
    scr = aux["scr"]
    mean, r, mr = aux["mean"], aux["r"], aux["mr"]

    for i, sl in enumerate(fsl):
        sqd = ctx.named_sem("ln_sq")
        mmd = ctx.named_sem("ln_mm")
        sqb = ctx.semc["ln_sq"]
        mmb = ctx.semc["ln_mm"]
        ctx.semc["ln_sq"] = sqb + NCHUNK
        ctx.semc["ln_mm"] = mmb + NCHUNK
        with nc.Block() as block:
            @block.scalar
            def _(s):
                for c in range(NCHUNK):
                    if c >= 2:
                        s.wait_ge(mmd, mmb + c - 1)
                    s.activation(scr[c % 2][:], x_chunks[c][:, sl], AF.Square)\
                        .then_inc(sqd, 1)

            @block.tensor
            def _(t):
                for c in range(NCHUNK):
                    t.matmul(ctx.psR1[0:1, :], ctx.ones128[:], x_chunks[c][:, sl],
                             start=(c == 0), stop=(c == NCHUNK - 1))
                    t.wait_ge(sqd, sqb + c + 1)
                    t.matmul(ctx.psR2[0:1, :], ctx.ones128[:], scr[c % 2][:],
                             start=(c == 0), stop=(c == NCHUNK - 1))\
                        .then_inc(mmd, 1)
        with nc.Block() as block:
            @block.scalar
            def _(s):
                s.activation(mean[0:1, sl], ctx.psR1[0:1, :], AF.Copy,
                             scale=1.0 / D)
        with nc.Block() as block:
            @block.scalar
            def _(s):
                s.activation(scr[0][0:1, 0:len_sl(sl)], mean[0:1, sl], AF.Square)
        with nc.Block() as block:
            @block.vector
            def _(v):
                v.scalar_tensor_tensor(scr[1][0:1, 0:len_sl(sl)],
                                       ctx.psR2[0:1, 0:len_sl(sl)], 1.0 / D,
                                       scr[0][0:1, 0:len_sl(sl)],
                                       op0=ALU.mult, op1=ALU.subtract)
        with nc.Block() as block:
            @block.scalar
            def _(s):
                s.activation(scr[0][0:1, 0:len_sl(sl)],
                             scr[1][0:1, 0:len_sl(sl)], AF.Ln, bias=ctx.eps_col[0:1, :])
        with nc.Block() as block:
            @block.scalar
            def _(s):
                s.activation(r[0:1, sl], scr[0][0:1, 0:len_sl(sl)], AF.Exp,
                             scale=-0.5)
    if g_row is not None:
        with nc.Block() as block:
            @block.vector
            def _(v):
                v.tensor_mul(mr[:], mean[0:1, :], r[0:1, :])

    for i, sl in enumerate(fsl):
        n = len_sl(sl)
        with nc.Block() as block:
            @block.tensor
            def _(t):
                if g_row is None:
                    t.matmul(ctx.psA[:, 0:n], ctx.ones_col[:], mean[0:1, sl],
                             start=True, stop=True)
                    t.matmul(ctx.psB[:, 0:n], ctx.ones_col[:], r[0:1, sl],
                             start=True, stop=True)
        for c in range(NCHUNK):
            if g_row is not None:
                gsl = g_row[0:1, c * 128:(c + 1) * 128]
                bsl = b_row[0:1, c * 128:(c + 1) * 128]
                with nc.Block() as block:
                    @block.tensor
                    def _(t):
                        t.matmul(ctx.psA[:, 0:n], gsl, r[0:1, sl],
                                 start=True, stop=True)
                        t.matmul(ctx.psB[:, 0:n], gsl, mr[0:1, sl],
                                 start=True, stop=False)
                        t.matmul(ctx.psB[:, 0:n], bsl,
                                 aux["negones"][0:1, 0:n],
                                 start=False, stop=True)
                with nc.Block() as block:
                    @block.vector
                    def _(v):
                        v.tensor_mul(scr[0][:, 0:n], x_chunks[c][:, sl],
                                     ctx.psA[:, 0:n])
                with nc.Block() as block:
                    @block.vector
                    def _(v):
                        v.tensor_sub(out_chunks[c][:, sl], scr[0][:, 0:n],
                                     ctx.psB[:, 0:n])
                        if out2_chunks is not None:
                            v.tensor_sub(out2_chunks[c][:, sl],
                                         scr[0][:, 0:n], ctx.psB[:, 0:n])
            else:
                with nc.Block() as block:
                    @block.vector
                    def _(v):
                        v.tensor_sub(scr[0][:, 0:n], x_chunks[c][:, sl],
                                     ctx.psA[:, 0:n])
                with nc.Block() as block:
                    @block.vector
                    def _(v):
                        v.tensor_mul(out_chunks[c][:, sl], scr[0][:, 0:n],
                                     ctx.psB[:, 0:n])
                        if out2_chunks is not None:
                            v.tensor_mul(out2_chunks[c][:, sl],
                                         scr[0][:, 0:n], ctx.psB[:, 0:n])


def len_sl(sl):
    return sl.stop - sl.start


def alloc_ln_aux(ctx, Tn, gb=False):
    aux = {}
    for k in ("mean", "r"):
        aux[k] = ctx.sbuf(f"ln_{k}", [1, Tn], F32)
    aux["scr"] = [ctx.sbuf(f"ln_scr{i}", [128, 512], F32) for i in range(2)]
    if gb:
        aux["mr"] = ctx.sbuf("ln_mr", [1, Tn], F32)
        aux["negones"] = ctx.sbuf("ln_negones", [1, 512], F32)
        nc = ctx.nc
        with nc.Block() as block:
            @block.gpsimd
            def _(g):
                g.memset(aux["negones"][:], -1.0)
    else:
        aux["mr"] = None
        aux["negones"] = None
    return aux

NHEADS = 16
NT = 16






def build_l0attn(ctx, x_dram, wqkv_dram, bqkv_cols_dram, bv_row_dram,
                 wout_dram, bout_cols_dram, ident_dram, y_dram):
    nc = ctx.nc
    A = [ctx.sbuf(f"A{c}", [128, T], F32) for c in range(NCHUNK)]
    q_sb = [ctx.sbuf(f"q{c}", [128, T], BF16) for c in range(NCHUNK)]
    k_sb = [ctx.sbuf(f"k{c}", [128, T], BF16) for c in range(NCHUNK)]
    v_sb = [ctx.sbuf(f"v{i}", [128, D], BF16) for i in range(NT)]
    ln_mean = ctx.sbuf("lnmean", [1, T], F32)
    ln_r = ctx.sbuf("lnr", [1, T], F32)
    lnscr = [ctx.sbuf(f"lnscr{i}", [128, 512], F32) for i in range(2)]
    bqkv_cols = ctx.sbuf("bqkv_cols_sb", [128, 24], F32)
    bout_cols = ctx.sbuf("bout_cols_sb", [128, 8], F32)
    bv_row = ctx.sbuf("bv_row_sb", [1, D], F32)
    bv_row_bf = ctx.sbuf("bv_row_bf", [1, D], BF16)
    ident = ctx.sbuf("ident_sb", [128, 128], F32)
    maxbuf = ctx.sbuf("maxbuf", [128, 16], F32)
    mrow = ctx.sbuf("mrow", [1, 130], F32)
    negM_col = ctx.sbuf("negM_col", [128, 1], F32)
    z_row = ctx.sbuf("z_row", [1, 1024], F32)
    invz_row = ctx.sbuf("invz_row", [1, 1024], F32)
    invzbc = ctx.sbuf("invzbc", [64, 1024], F32)

    aux = {"mean": ln_mean, "r": ln_r, "mr": None, "negones": None,
           "scr": lnscr}

    Abf = [a[:].bitcast(BF16) for a in A]
    Af32hi = [a[:, 1024:2048] for a in A]
    xln = [Abf[c][:, 0:2048] for c in range(NCHUNK)]
    attn = xln
    wst_f32 = Af32hi[0]
    wst_bf = Abf[1][:, 2048:3072]
    wv_bf = [Abf[4 + j][:, 2048:4096] for j in range(4)]
    p_k0 = Abf[2][:, 2048:4096]
    p_strA = Abf[3][:, 2048:4096]
    p_strB = Abf[0][:, 2048:4096]

    def p_strip(s):
        if s < 16:
            return p_strA[:, s * 128:(s + 1) * 128]
        return p_strB[:, (s - 16) * 128:(s - 15) * 128]

    def strip_kq(s):
        if s < 15:
            return (s + 1, s + 1)
        return (s - 14, 0)

    dma_load(ctx, [(A[c][:], x_dram[c * 128:(c + 1) * 128, :])
                   for c in range(NCHUNK)]
             + [(bqkv_cols[:], bqkv_cols_dram[:, :]),
                (bout_cols[:], bout_cols_dram[:, :]),
                (bv_row[:], bv_row_dram[:, :]),
                (ident[:], ident_dram[:, :])])
    with nc.Block() as block:
        @block.vector
        def _(v):
            v.tensor_copy(bv_row_bf[:], bv_row[:])

    emit_ln(ctx, [a[:] for a in A], xln, aux)

    wqkv_r = wqkv_dram.rearrange("(c p) n -> p c n", p=128)
    for qk in range(2):
        dst = q_sb if qk == 0 else k_sb
        for oc in range(NCHUNK):
            ocol = qk * 1024 + oc * 128
            dma_load(ctx, [(wst_f32.rearrange("p (c n) -> p c n", n=128),
                            wqkv_r[:, :, ocol:ocol + 128])])
            with nc.Block() as block:
                @block.vector
                def _(v):
                    v.tensor_copy(wst_bf, wst_f32)
            with nc.Block() as block:
                @block.tensor
                def _(t):
                    for dc in range(NCHUNK):
                        lhsT = wst_bf.rearrange("p (c n) -> p c n", n=128)[:, dc, :]
                        for f in range(4):
                            ps = [ctx.psX, ctx.psY][f // 2]
                            psl = slice((f % 2) * 512, (f % 2 + 1) * 512)
                            t.matmul(ps[:, psl], lhsT,
                                     xln[dc][:, f * 512:(f + 1) * 512],
                                     start=(dc == 0), stop=(dc == NCHUNK - 1))
            with nc.Block() as block:
                @block.scalar
                def _(s):
                    bcol = bqkv_cols[:, qk * 8 + oc:qk * 8 + oc + 1]
                    s.activation(dst[oc][:, 0:1024], ctx.psX[:], AF.Identity,
                                 bias=bcol)
                    s.activation(dst[oc][:, 1024:2048], ctx.psY[:], AF.Identity,
                                 bias=bcol)

    for dc in range(NCHUNK):
        dma_load(ctx, [(wst_f32, wqkv_r[:, dc, 2048:3072])])
        with nc.Block() as block:
            @block.vector
            def _(v):
                half = wv_bf[dc // 2]
                v.tensor_copy(half[:, (dc % 2) * 1024:(dc % 2 + 1) * 1024],
                              wst_f32)
    for i in range(NT):
        tsl = slice(i * 128, (i + 1) * 128)
        with nc.Block() as block:
            @block.tensor
            def _(t):
                for dc in range(NCHUNK):
                    wv_dc = wv_bf[dc // 2][:, (dc % 2) * 1024:(dc % 2 + 1) * 1024]
                    for f in range(2):
                        t.matmul(ctx.psX[:, f * 512:(f + 1) * 512],
                                 xln[dc][:, tsl],
                                 wv_dc[:, f * 512:(f + 1) * 512],
                                 start=(dc == 0), stop=False)
                for f in range(2):
                    t.matmul(ctx.psX[:, f * 512:(f + 1) * 512],
                             ctx.ones_col_bf[:],
                             bv_row_bf[0:1, f * 512:(f + 1) * 512],
                             start=False, stop=True)
        with nc.Block() as block:
            @block.scalar
            def _(s):
                s.copy(v_sb[i][:], ctx.psX[:])

    for h in range(NHEADS):
        ch, off = h // 2, (h % 2) * 64
        qh = q_sb[ch][off:off + 64, :]
        kh = k_sb[ch][off:off + 64, :]
        hsl = slice(h * 64, (h + 1) * 64)

        def emit_k0(t, f):
            ps = [ctx.psX, ctx.psY][f // 2]
            psl = slice((f % 2) * 512, (f % 2 + 1) * 512)
            t.matmul(ps[:, psl], kh[:, 0:128],
                     qh[:, f * 512:(f + 1) * 512], start=True, stop=True)

        def emit_batch(t, b):
            ps = [ctx.psA, ctx.psB][b % 2]
            for u in range(4):
                s_id = b * 4 + u
                if s_id >= 30:
                    break
                ki, qi = strip_kq(s_id)
                t.matmul(ps[:, u * 128:(u + 1) * 128],
                         kh[:, ki * 128:(ki + 1) * 128],
                         qh[:, qi * 128:(qi + 1) * 128],
                         start=True, stop=True)

        with nc.Block() as block:
            @block.tensor
            def _(t):
                for f in range(4):
                    emit_k0(t, f)
        with nc.Block() as block:
            @block.vector
            def _(v):
                for f in range(4):
                    ps = [ctx.psX, ctx.psY][f // 2]
                    psl = slice((f % 2) * 512, (f % 2 + 1) * 512)
                    v.reduce_max(maxbuf[:, f:f + 1], ps[:, psl],
                                 axis=mybir.AxisListType.X)
        for b in range(8):
            with nc.Block() as block:
                @block.tensor
                def _(t):
                    emit_batch(t, b)
            with nc.Block() as block:
                @block.vector
                def _(v):
                    ps = [ctx.psA, ctx.psB][b % 2]
                    n = min(4, 30 - b * 4) * 128
                    v.reduce_max(maxbuf[:, 4 + b:5 + b], ps[:, 0:n],
                                 axis=mybir.AxisListType.X)
        with nc.Block() as block:
            @block.vector
            def _(v):
                v.reduce_max(maxbuf[:, 12:13], maxbuf[:, 0:12],
                             axis=mybir.AxisListType.X)
        with nc.Block() as block:
            @block.tensor
            def _(t):
                t.matmul(ctx.psR1[0:1, 0:128], maxbuf[:, 12:13], ident[:],
                         start=True, stop=True)
        with nc.Block() as block:
            @block.vector
            def _(v):
                v.reduce_max(mrow[0:1, 128:129], ctx.psR1[0:1, 0:128],
                             axis=mybir.AxisListType.X)
        with nc.Block() as block:
            @block.scalar
            def _(s):
                s.activation(mrow[0:1, 129:130], mrow[0:1, 128:129],
                             AF.Copy, scale=-0.125)
        with nc.Block() as block:
            @block.tensor
            def _(t):
                t.matmul(ctx.psA[:, 0:1], ctx.ones_col[:], mrow[0:1, 129:130],
                         start=True, stop=True)
        with nc.Block() as block:
            @block.scalar
            def _(s):
                s.copy(negM_col[:], ctx.psA[:, 0:1])

        with nc.Block() as block:
            @block.tensor
            def _(t):
                for f in range(4):
                    emit_k0(t, f)
        with nc.Block() as block:
            @block.scalar
            def _(s):
                for f in range(4):
                    ps = [ctx.psX, ctx.psY][f // 2]
                    psl = slice((f % 2) * 512, (f % 2 + 1) * 512)
                    s.activation(p_k0[:, f * 512:(f + 1) * 512], ps[:, psl],
                                 AF.Exp, bias=negM_col[:], scale=0.125)
        for b in range(8):
            with nc.Block() as block:
                @block.tensor
                def _(t):
                    emit_batch(t, b)
            with nc.Block() as block:
                @block.scalar
                def _(s):
                    ps = [ctx.psA, ctx.psB][b % 2]
                    for u in range(min(4, 30 - b * 4)):
                        s.activation(p_strip(b * 4 + u),
                                     ps[:, u * 128:(u + 1) * 128],
                                     AF.Exp, bias=negM_col[:], scale=0.125)

        for half in range(2):
            for grp in range(2):
                with nc.Block() as block:
                    @block.tensor
                    def _(t):
                        for u in range(4):
                            qi = half * 8 + grp * 4 + u
                            dst = ctx.psR1[0:1, u * 128:(u + 1) * 128]
                            t.matmul(dst, ctx.ones128_bf[:],
                                     p_k0[:, qi * 128:(qi + 1) * 128],
                                     start=True, stop=False)
                            if qi >= 1:
                                t.matmul(dst, ctx.ones128_bf[:],
                                         p_strip(qi - 1),
                                         start=False, stop=True)
                            else:
                                for j in range(15):
                                    t.matmul(dst, ctx.ones128_bf[:],
                                             p_strip(15 + j),
                                             start=False, stop=(j == 14))
                with nc.Block() as block:
                    @block.scalar
                    def _(s):
                        s.copy(z_row[0:1, grp * 512:(grp + 1) * 512],
                               ctx.psR1[0:1, :])
            with nc.Block() as block:
                @block.vector
                def _(v):
                    v.reciprocal(invz_row[:], z_row[:])
            for f in range(2):
                with nc.Block() as block:
                    @block.tensor
                    def _(t):
                        t.matmul(ctx.psA[0:64, 0:512], ctx.ones_col[0:1, 0:64],
                                 invz_row[0:1, f * 512:(f + 1) * 512],
                                 start=True, stop=True)
                with nc.Block() as block:
                    @block.scalar
                    def _(s):
                        s.copy(invzbc[:, f * 512:(f + 1) * 512],
                               ctx.psA[0:64, 0:512])
            with nc.Block() as block:
                @block.tensor
                def _(t):
                    for u in range(8):
                        qi = half * 8 + u
                        ps = ctx.psX
                        dst = ps[0:64, u * 128:(u + 1) * 128]
                        t.matmul(dst, v_sb[0][:, hsl],
                                 p_k0[:, qi * 128:(qi + 1) * 128],
                                 start=True, stop=False)
                        if qi >= 1:
                            t.matmul(dst, v_sb[qi][:, hsl], p_strip(qi - 1),
                                     start=False, stop=True)
                        else:
                            for j in range(15):
                                t.matmul(dst, v_sb[1 + j][:, hsl],
                                         p_strip(15 + j),
                                         start=False, stop=(j == 14))
            with nc.Block() as block:
                @block.vector
                def _(v):
                    v.tensor_mul(attn[ch][off:off + 64,
                                          half * 1024:(half + 1) * 1024],
                                 ctx.psX[0:64, :], invzbc[:, :])

    wo_f32 = q_sb[1][:].bitcast(F32)[:, 0:1024]
    wo_bf = q_sb[2][:, 0:1024]
    xre = q_sb[0][:].bitcast(F32)[:, 0:1024]
    yst = k_sb[0][:].bitcast(F32)[:, 0:1024]
    wout_r = wout_dram.rearrange("(c p) n -> p c n", p=128)
    for oc in range(NCHUNK):
        dma_load(ctx, [(wo_f32.rearrange("p (c n) -> p c n", n=128),
                        wout_r[:, :, oc * 128:(oc + 1) * 128])])
        with nc.Block() as block:
            @block.vector
            def _(v):
                v.tensor_copy(wo_bf, wo_f32)
        with nc.Block() as block:
            @block.tensor
            def _(t):
                for dc in range(NCHUNK):
                    lhsT = wo_bf.rearrange("p (c n) -> p c n", n=128)[:, dc, :]
                    for f in range(4):
                        ps = [ctx.psX, ctx.psY][f // 2]
                        psl = slice((f % 2) * 512, (f % 2 + 1) * 512)
                        t.matmul(ps[:, psl], lhsT,
                                 attn[dc][:, f * 512:(f + 1) * 512],
                                 start=(dc == 0), stop=(dc == NCHUNK - 1))
        for half in range(2):
            tsl = slice(half * 1024, (half + 1) * 1024)
            dma_load(ctx, [(xre, x_dram[oc * 128:(oc + 1) * 128, tsl])])
            with nc.Block() as block:
                @block.vector
                def _(v):
                    bcol = bout_cols[:, oc:oc + 1]
                    v.scalar_tensor_tensor(
                        yst, [ctx.psX, ctx.psY][half][:], bcol, xre,
                        op0=ALU.add, op1=ALU.add)
            dma_store(ctx, [(y_dram[oc * 128:(oc + 1) * 128, tsl], yst)])






SCL = 0.125
CAP = 768
NF = 32
BIGI = 1.0e5
NEGBIG = -1e9


def _pieces512(a, b):
    out = []
    while a < b:
        c = min(b, (a // 512 + 1) * 512)
        out.append((a, c))
        a = c
    return out


def zav_regions(kc, h):
    qs = max(0, (kc - 8 * h) * 128)
    out = []
    if kc >= 8 * h:
        out.append((slice(qs, qs + 128), True))
        if qs + 128 < 1024:
            out.append((slice(qs + 128, 1024), False))
    else:
        out.append((slice(0, 1024), False))
    return out


def build_a1(ctx, x_dram, wq_dram, wk_dram, wv_dram, wproj_dram,
             g1_dram, b1_dram, cos_dram, sin_dram, ident_dram, maskt_dram,
             xf_dram):
    nc = ctx.nc
    xs = [ctx.sbuf(f"xs{i}", [128, 1024], F32) for i in range(2)]
    h1 = [ctx.sbuf(f"h1_{c}", [128, T], BF16) for c in range(NCHUNK)]
    attn = h1
    q_sb = [ctx.sbuf(f"q{c}", [128, T], BF16) for c in range(NCHUNK)]
    k_sb = [ctx.sbuf(f"k{c}", [128, T], BF16) for c in range(NCHUNK)]
    v_sb = [ctx.sbuf(f"v{i}", [128, D], BF16) for i in range(NT)]
    m0 = ctx.sbuf("m0", [1, T], F32)
    r0 = ctx.sbuf("r0", [1, T], F32)
    r01 = ctx.sbuf("r01", [1, T], F32)
    scr = [ctx.sbuf(f"scr{i}", [128, 512], F32) for i in range(2)]
    g1_row = ctx.sbuf("g1_row_sb", [1, D], F32)
    b1_row = ctx.sbuf("b1_row_sb", [1, D], F32)
    negones = ctx.sbuf("negones", [1, 512], F32)
    ident = ctx.sbuf("ident_sb", [128, 128], F32)
    ident_bf = ctx.sbuf("ident_bf", [128, 128], BF16)
    maskt = ctx.sbuf("maskt_sb", [128, 128], BF16)
    ctab = ctx.sbuf("ctab", [128, 1024], F32)
    stab = ctx.sbuf("stab", [128, 1024], F32)
    swb = ctx.sbuf("swb", [128, 1024], BF16)
    t1b = ctx.sbuf("t1b", [128, 1024], BF16)
    wst_f32 = ctx.sbuf("wst_f32", [128, 1024], F32)
    wst_bf = ctx.sbuf("wst_bf", [128, 1024], BF16)
    wvh_bf = ctx.sbuf("wvh_bf", [128, 4096], BF16)
    pbuf = ctx.sbuf("pbuf", [128, 1024], BF16)
    maxbuf = ctx.sbuf("maxbuf", [128, 16], F32)
    mrow = ctx.sbuf("mrow", [1, 130], F32)
    negM_col = ctx.sbuf("negM_col", [128, 1], F32)

    t2b = pbuf
    z_row = stab[0:1, 0:1024]
    invz_row = wst_f32[0:1, 0:1024]
    invzbc = wvh_bf[:].bitcast(F32)[0:64, 0:1024]
    xf_st = ctab
    fsl = free_slices(T)

    dma_load(ctx, [(g1_row[:], g1_dram[:, :]), (b1_row[:], b1_dram[:, :]),
                   (ident[:], ident_dram[:, :]),
                   (wst_f32[:, 0:128], maskt_dram[:, :])])
    with nc.Block() as block:
        @block.gpsimd
        def _(g):
            g.memset(negones[:], -1.0)
    with nc.Block() as block:
        @block.vector
        def _(v):
            v.tensor_copy(ident_bf[:], ident[:])
            v.tensor_copy(maskt[:], wst_f32[:, 0:128])

    for i, sl in enumerate(fsl):
        sqd = ctx.named_sem("ln_sq"); mmd = ctx.named_sem("ln_mm")
        sqb = ctx.semc["ln_sq"]; mmb = ctx.semc["ln_mm"]
        ctx.semc["ln_sq"] = sqb + NCHUNK
        ctx.semc["ln_mm"] = mmb + NCHUNK
        lds = [ctx.named_sem(f"xstr{c}") for c in range(NCHUNK)]
        ldb = [ctx.semc[f"xstr{c}"] for c in range(NCHUNK)]
        for c in range(NCHUNK):
            ctx.semc[f"xstr{c}"] = ldb[c] + 16
        with nc.Block() as block:
            @block.sync
            def _(sy):
                for c in range(NCHUNK):
                    if c >= 2:
                        sy.wait_ge(mmd, mmb + c - 1)
                    sy.dma_start(out=xs[c % 2][:, 0:512],
                                 in_=x_dram[c * 128:(c + 1) * 128, sl])\
                        .then_inc(lds[c], 16)

            @block.scalar
            def _(s):
                for c in range(NCHUNK):
                    s.wait_ge(lds[c], ldb[c] + 16)
                    if c >= 2:
                        s.wait_ge(mmd, mmb + c - 1)
                    s.activation(scr[c % 2][:], xs[c % 2][:, 0:512], AF.Square)\
                        .then_inc(sqd, 1)

            @block.tensor
            def _(t):
                for c in range(NCHUNK):
                    t.wait_ge(sqd, sqb + c + 1)
                    t.matmul(ctx.psR1[0:1, :], ctx.ones128[:],
                             xs[c % 2][:, 0:512],
                             start=(c == 0), stop=(c == NCHUNK - 1))
                    t.matmul(ctx.psR2[0:1, :], ctx.ones128[:], scr[c % 2][:],
                             start=(c == 0), stop=(c == NCHUNK - 1))\
                        .then_inc(mmd, 1)
        with nc.Block() as block:
            @block.scalar
            def _(s):
                s.activation(m0[0:1, sl], ctx.psR1[0:1, :], AF.Copy,
                             scale=1.0 / D)
        with nc.Block() as block:
            @block.scalar
            def _(s):
                s.activation(scr[0][0:1, 0:512], m0[0:1, sl], AF.Square)
        with nc.Block() as block:
            @block.vector
            def _(v):
                v.scalar_tensor_tensor(scr[1][0:1, 0:512],
                                       ctx.psR2[0:1, 0:512], 1.0 / D,
                                       scr[0][0:1, 0:512],
                                       op0=ALU.mult, op1=ALU.subtract)
        with nc.Block() as block:
            @block.scalar
            def _(s):
                s.activation(scr[0][0:1, 0:512], scr[1][0:1, 0:512],
                             AF.Ln, bias=ctx.eps_col[0:1, :])
        with nc.Block() as block:
            @block.scalar
            def _(s):
                s.activation(r0[0:1, sl], scr[0][0:1, 0:512], AF.Exp,
                             scale=-0.5)
        with nc.Block() as block:
            @block.vector
            def _(v):
                v.tensor_mul(scr[0][0:1, 0:512], scr[1][0:1, 0:512],
                             r0[0:1, sl])
        with nc.Block() as block:
            @block.vector
            def _(v):
                v.tensor_mul(scr[1][0:1, 0:512], scr[0][0:1, 0:512],
                             r0[0:1, sl])
        with nc.Block() as block:
            @block.scalar
            def _(s):
                s.activation(scr[0][0:1, 0:512], scr[1][0:1, 0:512],
                             AF.Ln, bias=ctx.eps_col[0:1, :])
        with nc.Block() as block:
            @block.scalar
            def _(s):
                s.activation(scr[1][0:1, 0:512], scr[0][0:1, 0:512], AF.Exp,
                             scale=-0.5)
        with nc.Block() as block:
            @block.vector
            def _(v):
                v.tensor_mul(r01[0:1, sl], scr[1][0:1, 0:512], r0[0:1, sl])

    for i, sl in enumerate(fsl):
        with nc.Block() as block:
            @block.vector
            def _(v):
                v.tensor_mul(scr[1][0:1, 0:512], m0[0:1, sl], r01[0:1, sl])
        for c in range(NCHUNK):
            gsl = g1_row[0:1, c * 128:(c + 1) * 128]
            bsl = b1_row[0:1, c * 128:(c + 1) * 128]
            dma_load(ctx, [(xs[0][:, 0:512], x_dram[c * 128:(c + 1) * 128, sl])])
            with nc.Block() as block:
                @block.tensor
                def _(t):
                    t.matmul(ctx.psA[:, :], gsl, r01[0:1, sl],
                             start=True, stop=True)
                    t.matmul(ctx.psB[:, :], gsl, scr[1][0:1, 0:512],
                             start=True, stop=False)
                    t.matmul(ctx.psB[:, :], bsl, negones[0:1, :],
                             start=False, stop=True)
            with nc.Block() as block:
                @block.vector
                def _(v):
                    v.tensor_mul(scr[0][:], xs[0][:, 0:512], ctx.psA[:, :])
            with nc.Block() as block:
                @block.vector
                def _(v):
                    v.tensor_sub(h1[c][:, sl], scr[0][:], ctx.psB[:, :])

    for which, wd, dst in ((0, wq_dram, q_sb), (1, wk_dram, k_sb)):
        w_r = wd.rearrange("(c p) n -> p c n", p=128)
        for oc in range(NCHUNK):
            dma_load(ctx, [(wst_f32[:].rearrange("p (c n) -> p c n", n=128),
                            w_r[:, :, oc * 128:(oc + 1) * 128])])
            with nc.Block() as block:
                @block.vector
                def _(v):
                    v.tensor_copy(wst_bf[:], wst_f32[:])
            with nc.Block() as block:
                @block.tensor
                def _(t):
                    for dc in range(NCHUNK):
                        lhsT = wst_bf[:].rearrange("p (c n) -> p c n", n=128)[:, dc, :]
                        for f in range(4):
                            ps = [ctx.psX, ctx.psY][f // 2]
                            psl = slice((f % 2) * 512, (f % 2 + 1) * 512)
                            t.matmul(ps[:, psl], lhsT,
                                     h1[dc][:, f * 512:(f + 1) * 512],
                                     start=(dc == 0), stop=(dc == NCHUNK - 1))
            with nc.Block() as block:
                @block.scalar
                def _(s):
                    s.copy(dst[oc][:, 0:1024], ctx.psX[:])
                    s.copy(dst[oc][:, 1024:2048], ctx.psY[:])
    wv_r = wv_dram.rearrange("(c p) n -> p c n", p=128)
    for hh in range(2):
        csl = slice(hh * 512, (hh + 1) * 512)
        for dc in range(NCHUNK):
            dma_load(ctx, [(wst_f32[:, 0:512], wv_r[:, dc, csl])])
            with nc.Block() as block:
                @block.vector
                def _(v):
                    v.tensor_copy(wvh_bf[:, dc * 512:(dc + 1) * 512],
                                  wst_f32[:, 0:512])
        for i in range(NT):
            tsl = slice(i * 128, (i + 1) * 128)
            with nc.Block() as block:
                @block.tensor
                def _(t):
                    for dc in range(NCHUNK):
                        t.matmul(ctx.psX[:, 0:512],
                                 h1[dc][:, tsl],
                                 wvh_bf[:, dc * 512:(dc + 1) * 512],
                                 start=(dc == 0), stop=(dc == NCHUNK - 1))
            with nc.Block() as block:
                @block.scalar
                def _(s):
                    s.copy(v_sb[i][:, csl], ctx.psX[:, 0:512])

    SHUF = []
    for j in range(16):
        SHUF += [2 * j + 1, 2 * j]
    for half in range(2):
        tsl = slice(half * 1024, (half + 1) * 1024)
        dma_load(ctx, [(ctab[:], cos_dram[:, tsl]), (stab[:], sin_dram[:, tsl])])
        for arr in (q_sb, k_sb):
            for c in range(NCHUNK):
                with nc.Block() as block:
                    @block.vector
                    def _(v):
                        v.stream_shuffle(swb[:], arr[c][:, tsl], SHUF)
                with nc.Block() as block:
                    @block.vector
                    def _(v):
                        v.tensor_mul(t1b[:], arr[c][:, tsl], ctab[:])
                with nc.Block() as block:
                    @block.vector
                    def _(v):
                        v.tensor_mul(t2b[:], swb[:], stab[:])
                with nc.Block() as block:
                    @block.vector
                    def _(v):
                        v.tensor_add(arr[c][:, tsl], t1b[:], t2b[:])

    for h in range(NHEADS):
        ch, off = h // 2, (h % 2) * 64
        qh = q_sb[ch][off:off + 64, :]
        kh = k_sb[ch][off:off + 64, :]
        hsl = slice(h * 64, (h + 1) * 64)
        for half in range(2):
            q0 = half * 1024
            nkc = 8 * half + 8
            for kc in range(nkc):
                qs = max(0, (kc - 8 * half) * 128)
                with nc.Block() as block:
                    @block.tensor
                    def _(t):
                        for f0, f1 in _pieces512(qs, 1024):
                            t.matmul(ctx.psX[:, f0:f1],
                                     kh[:, kc * 128:(kc + 1) * 128],
                                     qh[:, q0 + f0:q0 + f1],
                                     start=True, stop=True)
                with nc.Block() as block:
                    @block.vector
                    def _(v):
                        v.reduce_max(maxbuf[:, kc:kc + 1], ctx.psX[:, qs:1024],
                                     axis=mybir.AxisListType.X)
            with nc.Block() as block:
                @block.vector
                def _(v):
                    v.reduce_max(maxbuf[:, 0:1], maxbuf[:, 0:nkc],
                                 axis=mybir.AxisListType.X)
            with nc.Block() as block:
                @block.tensor
                def _(t):
                    t.matmul(ctx.psR1[0:1, 0:128], maxbuf[:, 0:1], ident[:],
                             start=True, stop=True)
            with nc.Block() as block:
                @block.vector
                def _(v):
                    v.reduce_max(mrow[0:1, 128:129], ctx.psR1[0:1, 0:128],
                                 axis=mybir.AxisListType.X)
            with nc.Block() as block:
                @block.scalar
                def _(s):
                    s.activation(mrow[0:1, 129:130], mrow[0:1, 128:129],
                                 AF.Copy, scale=-SCL)
            with nc.Block() as block:
                @block.tensor
                def _(t):
                    t.matmul(ctx.psA[:, 0:1], ctx.ones_col[:],
                             mrow[0:1, 129:130], start=True, stop=True)
            with nc.Block() as block:
                @block.scalar
                def _(s):
                    s.copy(negM_col[:], ctx.psA[:, 0:1])
            for kc in range(nkc):
                qs = max(0, (kc - 8 * half) * 128)
                with nc.Block() as block:
                    @block.tensor
                    def _(t):
                        if kc >= 8 * half:
                            t.matmul(ctx.psX[:, qs:qs + 128], ident_bf[:],
                                     maskt[:], start=True, stop=False)
                            t.matmul(ctx.psX[:, qs:qs + 128],
                                     kh[:, kc * 128:(kc + 1) * 128],
                                     qh[:, q0 + qs:q0 + qs + 128],
                                     start=False, stop=True)
                            rest = qs + 128
                        else:
                            rest = qs
                        for f0, f1 in _pieces512(rest, 1024):
                            t.matmul(ctx.psX[:, f0:f1],
                                     kh[:, kc * 128:(kc + 1) * 128],
                                     qh[:, q0 + f0:q0 + f1],
                                     start=True, stop=True)
                with nc.Block() as block:
                    @block.scalar
                    def _(s):
                        s.activation(pbuf[:, qs:1024], ctx.psX[:, qs:1024],
                                     AF.Exp, bias=negM_col[:], scale=SCL)
                with nc.Block() as block:
                    @block.tensor
                    def _(t):
                        for rsl, stop in zav_regions(kc, half):
                            pieces = []
                            a = rsl.start
                            while a < rsl.stop:
                                b2 = min(rsl.stop, (a // 512 + 1) * 512)
                                pieces.append((a, b2))
                                a = b2
                            for f0, f1 in pieces:
                                psr = ctx.psR1 if f0 < 512 else ctx.psR2
                                o0 = f0 % 512
                                t.matmul(psr[0:1, o0:o0 + (f1 - f0)],
                                         ctx.ones128_bf[:], pbuf[:, f0:f1],
                                         start=(kc == 0), stop=stop)
                            for f0, f1 in pieces:
                                t.matmul(ctx.psY[0:64, f0:f1],
                                         v_sb[kc][:, hsl], pbuf[:, f0:f1],
                                         start=(kc == 0), stop=stop)
            with nc.Block() as block:
                @block.scalar
                def _(s):
                    s.copy(z_row[0:1, 0:512], ctx.psR1[0:1, :])
                    s.copy(z_row[0:1, 512:1024], ctx.psR2[0:1, :])
            with nc.Block() as block:
                @block.vector
                def _(v):
                    v.reciprocal(invz_row[:], z_row[:])
            for f in range(2):
                with nc.Block() as block:
                    @block.tensor
                    def _(t):
                        t.matmul(ctx.psA[0:64, 0:512], ctx.ones_col[0:1, 0:64],
                                 invz_row[0:1, f * 512:(f + 1) * 512],
                                 start=True, stop=True)
                with nc.Block() as block:
                    @block.scalar
                    def _(s):
                        s.copy(invzbc[:, f * 512:(f + 1) * 512],
                               ctx.psA[0:64, 0:512])
            with nc.Block() as block:
                @block.vector
                def _(v):
                    v.tensor_mul(attn[ch][off:off + 64, q0:q0 + 1024],
                                 ctx.psY[0:64, :], invzbc[:, :])

    wp_r = wproj_dram.rearrange("(c p) n -> p c n", p=128)
    for oc in range(NCHUNK):
        dma_load(ctx, [(wst_f32[:].rearrange("p (c n) -> p c n", n=128),
                        wp_r[:, :, oc * 128:(oc + 1) * 128])])
        with nc.Block() as block:
            @block.vector
            def _(v):
                v.tensor_copy(wst_bf[:], wst_f32[:])
        with nc.Block() as block:
            @block.tensor
            def _(t):
                for dc in range(NCHUNK):
                    lhsT = wst_bf[:].rearrange("p (c n) -> p c n", n=128)[:, dc, :]
                    for f in range(4):
                        ps = [ctx.psX, ctx.psY][f // 2]
                        psl = slice((f % 2) * 512, (f % 2 + 1) * 512)
                        t.matmul(ps[:, psl], lhsT,
                                 attn[dc][:, f * 512:(f + 1) * 512],
                                 start=(dc == 0), stop=(dc == NCHUNK - 1))
        for half in range(2):
            tsl = slice(half * 1024, (half + 1) * 1024)
            dma_load(ctx, [(xs[0][:, 0:1024], x_dram[oc * 128:(oc + 1) * 128, tsl])])
            for f in range(2):
                sl = slice(half * 1024 + f * 512, half * 1024 + (f + 1) * 512)
                lsl = slice(f * 512, (f + 1) * 512)
                with nc.Block() as block:
                    @block.tensor
                    def _(t):
                        t.matmul(ctx.psA[:, :], ctx.ones_col[:], r0[0:1, sl],
                                 start=True, stop=True)
                    @block.vector
                    def _(v):
                        v.tensor_mul(scr[1][0:1, :], m0[0:1, sl], r0[0:1, sl])
                with nc.Block() as block:
                    @block.tensor
                    def _(t):
                        t.matmul(ctx.psB[:, :], ctx.ones_col[:],
                                 scr[1][0:1, 0:512], start=True, stop=True)
                with nc.Block() as block:
                    @block.vector
                    def _(v):
                        v.tensor_mul(scr[0][:], xs[0][:, lsl], ctx.psA[:, :])
                with nc.Block() as block:
                    @block.vector
                    def _(v):
                        v.tensor_sub(xf_st[:, lsl], scr[0][:], ctx.psB[:, :])
            with nc.Block() as block:
                @block.vector
                def _(v):
                    v.tensor_add(xf_st[:, 0:1024], xf_st[:, 0:1024],
                                 [ctx.psX, ctx.psY][half][:])
            dma_store(ctx, [(xf_dram[oc * 128:(oc + 1) * 128, tsl],
                             xf_st[:, 0:1024])])



def host_prep(Wqkv, ln1_g, ln1_b):
    Dm, H, hd = 1024, 16, 64
    perm = np.empty(64, np.int64)
    perm[0::2] = np.arange(32)
    perm[1::2] = np.arange(32, 64)
    wq = np.empty((Dm, Dm), np.float32)
    wk = np.empty((Dm, Dm), np.float32)
    wv = np.empty((Dm, Dm), np.float32)
    for h in range(H):
        base = h * 192
        wq[:, h * 64:(h + 1) * 64] = Wqkv[:, base:base + 64][:, perm]
        wk[:, h * 64:(h + 1) * 64] = Wqkv[:, base + 64:base + 128][:, perm]
        wv[:, h * 64:(h + 1) * 64] = Wqkv[:, base + 128:base + 192]
    inv_freq = 1.0 / 10000.0 ** (np.arange(0, hd, 2, dtype=np.float32) / hd)
    ang = np.arange(2048, dtype=np.float32)[:, None] * inv_freq
    sin = np.sin(ang).T
    cos = np.cos(ang).T
    C = np.empty((128, 2048), np.float32)
    S = np.empty((128, 2048), np.float32)
    for g in range(2):
        o = g * 64
        C[o + 0:o + 64:2] = cos
        C[o + 1:o + 64:2] = cos
        S[o + 0:o + 64:2] = -sin
        S[o + 1:o + 64:2] = sin
    maskt = np.where(np.arange(128)[:, None] > np.arange(128)[None, :],
                     np.float32(NEGBIG * 8.0), np.float32(0.0))
    return wq, wk, wv, C, S, maskt







def build_a2(ctx, xf_dram, x1_dram, g2_dram, b2_dram, wg_dram, ident_dram,
             h2tok_dram, basetok_dram, gates_dram, mask_dram, pos_dram,
             counts_dram):
    nc = ctx.nc
    XF = [ctx.sbuf(f"XF{c}", [128, T], F32) for c in range(NCHUNK)]
    h2 = [ctx.sbuf(f"h2_{c}", [128, T], BF16) for c in range(NCHUNK)]
    g2_row = ctx.sbuf("g2_row_sb", [1, D], F32)
    b2_row = ctx.sbuf("b2_row_sb", [1, D], F32)
    negones = ctx.sbuf("negones_sb", [1, 512], F32)
    ident = ctx.sbuf("ident_f", [128, 128], F32)
    ident_bf = ctx.sbuf("ident_b", [128, 128], BF16)
    m_row = ctx.sbuf("m_row", [1, T], F32)
    r_row = ctx.sbuf("r_row", [1, T], F32)
    scr = [ctx.sbuf(f"scrA{i}", [128, 512], F32) for i in range(2)]
    h2f = ctx.sbuf("h2f", [128, T], F32)
    wg_sb = ctx.sbuf("wg_sb", [128, 64], F32)
    gatesf = ctx.sbuf("gatesf", [128, 128], F32)
    gsm = ctx.sbuf("gsm", [128, 128], F32)
    maskf = ctx.sbuf("maskf", [128, 128], F32)
    maskb = ctx.sbuf("maskb", [128, 128], BF16)
    mask_T = ctx.sbuf("mask_T", [8, T], F32)
    pos_T = ctx.sbuf("pos_T", [8, T], F32)
    postok = ctx.sbuf("postok", [128, 128], F32)
    gaux = ctx.sbuf("gaux", [128, 64], F32)
    counts_sb = ctx.sbuf("counts_sb", [1, 8], F32)
    tt_st = ctx.sbuf("tt_st", [128, 1024], F32)
    x1b = ctx.sbuf("x1b", [128, T], F32)

    aux = {"mean": m_row, "r": r_row, "mr": None, "negones": None, "scr": scr}

    dma_load(ctx, [(XF[c][:], xf_dram[c * 128:(c + 1) * 128, :])
                   for c in range(NCHUNK)]
             + [(g2_row[:], g2_dram[:, :]), (b2_row[:], b2_dram[:, :]),
                (ident[:], ident_dram[:, :]),
                (wg_sb[:].rearrange("p (c n) -> p c n", n=8),
                 wg_dram.rearrange("(c p) n -> p c n", p=128))])
    with nc.Block() as block:
        @block.gpsimd
        def _(g):
            g.memset(negones[:], -1.0)
    with nc.Block() as block:
        @block.vector
        def _(v):
            v.tensor_copy(ident_bf[:], ident[:])

    aux["mr"] = ctx.sbuf("mrA", [1, T], F32)
    aux["negones"] = negones
    emit_ln(ctx, [x[:] for x in XF], [o[:] for o in h2], aux,
            g_row=g2_row[:], b_row=b2_row[:])

    wg_bf = ctx.sbuf("wg_bf", [128, 64], BF16)
    with nc.Block() as block:
        @block.vector
        def _(v):
            v.tensor_copy(wg_bf[:], wg_sb[:])
    with nc.Block() as block:
        @block.tensor
        def _(t):
            for dc in range(NCHUNK):
                for f in range(4):
                    ps = [ctx.psX, ctx.psY][f // 2]
                    psl = slice((f % 2) * 512, (f % 2 + 1) * 512)
                    t.matmul(ps[0:8, psl], wg_bf[:, dc * 8:(dc + 1) * 8],
                             h2[dc][:, f * 512:(f + 1) * 512],
                             start=(dc == 0), stop=(dc == NCHUNK - 1))
    gates_T = mask_T
    with nc.Block() as block:
        @block.scalar
        def _(s):
            s.copy(gates_T[:, 0:1024], ctx.psX[0:8, :])
            s.copy(gates_T[:, 1024:2048], ctx.psY[0:8, :])
    for i in range(NT):
        with nc.Block() as block:
            @block.tensor
            def _(t):
                t.transpose(ctx.psA[0:128, 0:8],
                            gates_T[:, i * 128:(i + 1) * 128], ident[0:8, 0:8])
        with nc.Block() as block:
            @block.scalar
            def _(s):
                s.copy(gatesf[:, i * 8:(i + 1) * 8], ctx.psA[0:128, 0:8])

    for i in range(NT):
        gg = gatesf[:, i * 8:(i + 1) * 8]
        with nc.Block() as block:
            @block.vector
            def _(v):
                v.reduce_max(gaux[:, i:i + 1], gg, axis=mybir.AxisListType.X)
        with nc.Block() as block:
            @block.vector
            def _(v):
                v.tensor_scalar(gsm[:, i * 8:(i + 1) * 8], gg,
                                gaux[:, i:i + 1], 0.0,
                                op0=ALU.subtract, op1=ALU.add)
        with nc.Block() as block:
            @block.scalar
            def _(s):
                s.activation(gsm[:, i * 8:(i + 1) * 8],
                             gsm[:, i * 8:(i + 1) * 8], AF.Exp,
                             accum_out=gaux[:, 16 + i:17 + i])
    with nc.Block() as block:
        @block.vector
        def _(v):
            v.reciprocal(gaux[:, 32:48], gaux[:, 16:32])
    for i in range(NT):
        with nc.Block() as block:
            @block.vector
            def _(v):
                v.tensor_scalar_mul(gsm[:, i * 8:(i + 1) * 8],
                                    gsm[:, i * 8:(i + 1) * 8],
                                    gaux[:, 32 + i:33 + i])
        with nc.Block() as block:
            @block.vector
            def _(v):
                v.max(out=gaux[:, 48:56], in_=gsm[:, i * 8:(i + 1) * 8])
        with nc.Block() as block:
            @block.vector
            def _(v):
                v.tensor_scalar(maskf[:, i * 8:(i + 1) * 8],
                                gsm[:, i * 8:(i + 1) * 8],
                                gaux[:, 49:50], 0.0,
                                op0=ALU.is_ge, op1=ALU.add)
    with nc.Block() as block:
        @block.vector
        def _(v):
            v.tensor_copy(maskb[:], maskf[:])

    for i in range(NT):
        with nc.Block() as block:
            @block.tensor
            def _(t):
                t.transpose(ctx.psA[:, :].bitcast(BF16)[0:8, 0:128],
                            maskb[:, i * 8:(i + 1) * 8], ident_bf[:])
        with nc.Block() as block:
            @block.scalar
            def _(s):
                s.copy(mask_T[:, i * 128:(i + 1) * 128],
                       ctx.psA[:, :].bitcast(BF16)[0:8, 0:128])
    with nc.Block() as block:
        @block.vector
        def _(v):
            v.tensor_tensor_scan(pos_T[:], mask_T[:], mask_T[:], 0.0,
                                 op0=ALU.add, op1=ALU.bypass)
    with nc.Block() as block:
        @block.tensor
        def _(t):
            t.matmul(ctx.psR1[0:1, 0:8], pos_T[:, T - 1:T], ident[0:8, 0:8],
                     start=True, stop=True)
    with nc.Block() as block:
        @block.scalar
        def _(s):
            s.copy(counts_sb[:], ctx.psR1[0:1, 0:8])
    dma_store(ctx, [(counts_dram[:, :], counts_sb[:])])
    for i in range(NT):
        with nc.Block() as block:
            @block.tensor
            def _(t):
                t.transpose(ctx.psA[0:128, 0:8],
                            pos_T[:, i * 128:(i + 1) * 128], ident[0:8, 0:8])
        with nc.Block() as block:
            @block.scalar
            def _(s):
                s.copy(postok[:, i * 8:(i + 1) * 8], ctx.psA[0:128, 0:8])
    g_r = gates_dram.rearrange("(i p) e -> p i e", p=128)
    m_r = mask_dram.rearrange("(i p) e -> p i e", p=128)
    p_r = pos_dram.rearrange("(i p) e -> p i e", p=128)
    dma_store(ctx, [(g_r[:, i, :], gsm[:, i * 8:(i + 1) * 8])
                    for i in range(NT)])
    dma_store(ctx, [(m_r[:, i, :], maskf[:, i * 8:(i + 1) * 8])
                    for i in range(NT)])
    dma_store(ctx, [(p_r[:, i, :], postok[:, i * 8:(i + 1) * 8])
                    for i in range(NT)])

    h_r = h2tok_dram.rearrange("(i p) d -> p i d", p=128)
    for i in range(NT):
        with nc.Block() as block:
            @block.tensor
            def _(t):
                for dc in range(NCHUNK):
                    t.transpose(
                        ctx.psX[:, :].bitcast(BF16)[:, dc * 128:(dc + 1) * 128],
                        h2[dc][:, i * 128:(i + 1) * 128], ident_bf[:])
        with nc.Block() as block:
            @block.vector
            def _(v):
                v.tensor_copy(tt_st[:].bitcast(BF16)[:, 0:1024],
                              ctx.psX[:, :].bitcast(BF16)[:, 0:1024])
        dma_store(ctx, [(h_r[:, i, :], tt_st[:].bitcast(BF16)[:, 0:1024])])

    b_r = basetok_dram.rearrange("(i p) d -> p i d", p=128)
    for dc in range(NCHUNK):
        dma_load(ctx, [(x1b[:], x1_dram[dc * 128:(dc + 1) * 128, :])])
        with nc.Block() as block:
            @block.vector
            def _(v):
                v.tensor_add(XF[dc][:], XF[dc][:], x1b[:])
    for i in range(NT):
        with nc.Block() as block:
            @block.tensor
            def _(t):
                for dc in range(NCHUNK):
                    t.transpose(ctx.psX[:, dc * 128:(dc + 1) * 128],
                                XF[dc][:, i * 128:(i + 1) * 128], ident[:])
        with nc.Block() as block:
            @block.scalar
            def _(s):
                s.copy(tt_st[:, 0:1024], ctx.psX[:, :])
        dma_store(ctx, [(b_r[:, i, :], tt_st[:, 0:1024])])




def build_b(ctx, counts_dram, pref_dram, h2_dram, base_dram, gates_dram,
            mask_dram, pos_dram, w1_dram, b1c_dram, w2_dram, b2r_dram,
            iota_dram, tokidx_dram, ident_dram, y_dram):
    nc = ctx.nc
    psM = [ctx.psum(f"psM{i}", [128, 1024]) for i in range(3)]
    psS = ctx.psum("psS", [128, 1024])

    HC = CAP // 2
    midb = ctx.sbuf("midb", [128, NF * HC], BF16)
    dispT = [ctx.sbuf(f"dispT{c}", [128, CAP], BF16) for c in range(NCHUNK)]
    disp_tok = ctx.sbuf("disp_tok", [128, 1024], BF16)
    w1st_f = ctx.sbuf("w1st_f", [128, 1024], F32)
    w1st_b = ctx.sbuf("w1st_b", [128, 1024], BF16)
    w2st_f = ctx.sbuf("w2st_f", [128, 1024], F32)
    w2st_b = ctx.sbuf("w2st_b", [128, 1024], BF16)
    b1c_sb = ctx.sbuf("b1c_sb", [128, 32], F32)
    b2r_sb = ctx.sbuf("b2r_sb", [1, 1024], F32)
    b2r_bf = ctx.sbuf("b2r_bf", [1, 1024], BF16)
    ident = ctx.sbuf("ident_f", [128, 128], F32)
    ident_bf = ctx.sbuf("ident_b", [128, 128], BF16)
    iota_sb = ctx.sbuf("iota_sb", [128, CAP], F32)
    toki_sb = ctx.sbuf("toki_sb", [128, 16], I32)
    counts_sb = ctx.sbuf("counts_sb2", [8, 8], F32)
    pref_sb = ctx.sbuf("pref_sb", [8, 1], F32)
    offs_col = ctx.sbuf("offs_col", [8, 1], F32)
    offs_row = ctx.sbuf("offs_row", [1, 8], F32)
    offsbc = ctx.sbuf("offsbc", [128, 8], F32)
    iota8 = ctx.sbuf("iota8", [128, 8], F32)
    gt = ctx.sbuf("gt", [128, 8], F32)
    mt = ctx.sbuf("mt", [128, 8], F32)
    pt = ctx.sbuf("pt", [128, 8], F32)
    gp = ctx.sbuf("gp", [128, 8], F32)
    kp = ctx.sbuf("kp", [128, 8], F32)
    sv = ctx.sbuf("sv", [128, 8], F32)
    sl8 = ctx.sbuf("sl8", [128, 8], F32)
    seq = ctx.sbuf("seq", [128, 8], F32)
    seq2 = ctx.sbuf("seq2", [128, 8], F32)
    seqi = ctx.sbuf("seqi", [128, 8], I32)
    eq1 = ctx.sbuf("eq1", [128, 8], F32)
    eq2 = ctx.sbuf("eq2", [128, 8], F32)
    g2s = ctx.sbuf("g2s", [128, 8], F32)
    prod = ctx.sbuf("prod", [128, 8], F32)
    cc1 = ctx.sbuf("cc1", [128, 16], F32)
    jj = ctx.sbuf("jj", [128, 32], I32)
    gv = ctx.sbuf("gv", [128, 32], F32)
    jf = ctx.sbuf("jf", [128, 2], F32)
    seqeq_all = ctx.sbuf("seqeq_all", [128, 128], F32)
    tokf_sb = ctx.sbuf("tokf_sb", [128, 16], F32)
    eqb = None
    idxrow = ctx.sbuf("idxrow", [1, CAP], F32)
    idxrow_i = ctx.sbuf("idxrow_i", [1, CAP], I32)
    gA = ctx.sbuf("gA", [128, 1024], BF16)
    gB = ctx.sbuf("gB", [128, 1024], BF16)
    yst = ctx.sbuf("yst", [128, 1024], F32)
    idxcol = ctx.sbuf("idxcol", [128, 1], I32)
    seq1 = ctx.sbuf("seq1", [128, 1], I32)
    tok1 = ctx.sbuf("tok1", [128, 1], I32)

    idx_dram = nc.dram_tensor("idx_scratch", [8 * CAP, 1], I32).ap()
    exp_dram = nc.dram_tensor("exp_scratch", [8 * CAP, D], BF16).ap()

    eqb = w1st_f[:, 0:CAP]
    dma_load(ctx, [(counts_sb[:], counts_dram[:, :]),
                   (pref_sb[:], pref_dram[:, :]),
                   (ident[:], ident_dram[:, :]),
                   (iota_sb[:], iota_dram[:, :]),
                   (toki_sb[:], tokidx_dram[:, :])])
    with nc.Block() as block:
        @block.gpsimd
        def _(g):
            g.memset(gA[:], 0.0)
            g.memset(gB[:], 0.0)
    with nc.Block() as block:
        @block.vector
        def _(v):
            v.tensor_copy(ident_bf[:], ident[:])
            v.tensor_copy(iota8[:], iota_sb[:, 0:8])
            v.tensor_copy(tokf_sb[:], toki_sb[:])

    with nc.Block() as block:
        @block.tensor
        def _(t):
            t.matmul(psM[0][0:8, 0:1], counts_sb[:], pref_sb[:],
                     start=True, stop=True)
    with nc.Block() as block:
        @block.scalar
        def _(s):
            s.copy(offs_col[:], psM[0][0:8, 0:1])
    with nc.Block() as block:
        @block.tensor
        def _(t):
            t.matmul(psM[0][0:1, 0:8], offs_col[:], ident[0:8, 0:8],
                     start=True, stop=True)
    with nc.Block() as block:
        @block.scalar
        def _(s):
            s.copy(offs_row[:], psM[0][0:1, 0:8])
    with nc.Block() as block:
        @block.tensor
        def _(t):
            t.matmul(psM[0][0:128, 0:8], ctx.ones_col[:], offs_row[:],
                     start=True, stop=True)
    with nc.Block() as block:
        @block.scalar
        def _(s):
            s.copy(offsbc[:], psM[0][0:128, 0:8])

    for i in range(NT):
        rs = slice(i * 128, (i + 1) * 128)
        dma_load(ctx, [(gt[:], gates_dram[rs, :]), (mt[:], mask_dram[rs, :]),
                       (pt[:], pos_dram[rs, :])])
        with nc.Block() as block:
            @block.vector
            def _(v):
                v.tensor_add(gp[:], pt[:], offsbc[:])
                v.tensor_scalar_add(sl8[:], pt[:], -1.0)
        with nc.Block() as block:
            @block.vector
            def _(v):
                v.tensor_single_scalar(kp[:], gp[:], 4097.0, op=ALU.is_lt)
                v.tensor_single_scalar(sv[:], sl8[:], float(CAP), op=ALU.is_lt)
        with nc.Block() as block:
            @block.vector
            def _(v):
                v.tensor_mul(kp[:], kp[:], mt[:])
        with nc.Block() as block:
            @block.vector
            def _(v):
                v.tensor_mul(kp[:], kp[:], sv[:])
        with nc.Block() as block:
            @block.vector
            def _(v):
                v.scalar_tensor_tensor(seq[:], sl8[:], 5000.0, kp[:],
                                       op0=ALU.add, op1=ALU.mult)
        with nc.Block() as block:
            @block.vector
            def _(v):
                v.tensor_scalar_add(seqeq_all[:, i * 8:(i + 1) * 8], seq[:],
                                    -5000.0)
        with nc.Block() as block:
            @block.vector
            def _(v):
                v.reduce_max(cc1[:, 0:1], gt[:], axis=mybir.AxisListType.X)
        with nc.Block() as block:
            @block.vector
            def _(v):
                v.tensor_scalar(eq1[:], gt[:], cc1[:, 0:1], 0.0,
                                op0=ALU.is_equal, op1=ALU.add)
        with nc.Block() as block:
            @block.vector
            def _(v):
                v.tensor_mul(prod[:], eq1[:], iota8[:])
        with nc.Block() as block:
            @block.vector
            def _(v):
                v.reduce_sum(cc1[:, 1:2], prod[:], axis=mybir.AxisListType.X)
        with nc.Block() as block:
            @block.vector
            def _(v):
                v.scalar_tensor_tensor(g2s[:], eq1[:], -10.0, gt[:],
                                       op0=ALU.mult, op1=ALU.add)
        with nc.Block() as block:
            @block.vector
            def _(v):
                v.reduce_max(cc1[:, 2:3], g2s[:], axis=mybir.AxisListType.X)
        with nc.Block() as block:
            @block.vector
            def _(v):
                v.tensor_scalar(eq2[:], g2s[:], cc1[:, 2:3], 0.0,
                                op0=ALU.is_equal, op1=ALU.add)
        for (eqm, o) in ((eq1, 0), (eq2, 1)):
            with nc.Block() as block:
                @block.vector
                def _(v):
                    v.tensor_mul(prod[:], eqm[:], sl8[:])
            with nc.Block() as block:
                @block.vector
                def _(v):
                    v.reduce_sum(cc1[:, 4 + o:5 + o], prod[:],
                                 axis=mybir.AxisListType.X)
            with nc.Block() as block:
                @block.vector
                def _(v):
                    v.tensor_mul(prod[:], eqm[:], kp[:])
            with nc.Block() as block:
                @block.vector
                def _(v):
                    v.reduce_sum(cc1[:, 6 + o:7 + o], prod[:],
                                 axis=mybir.AxisListType.X)
        if True:
            with nc.Block() as block:
                @block.vector
                def _(v):
                    v.tensor_mul(prod[:], eq2[:], iota8[:])
            with nc.Block() as block:
                @block.vector
                def _(v):
                    v.reduce_sum(cc1[:, 3:4], prod[:],
                                 axis=mybir.AxisListType.X)
        with nc.Block() as block:
            @block.vector
            def _(v):
                v.tensor_mul(gv[:, 2 * i:2 * i + 1], cc1[:, 0:1], cc1[:, 6:7])
        with nc.Block() as block:
            @block.vector
            def _(v):
                v.tensor_mul(gv[:, 2 * i + 1:2 * i + 2], cc1[:, 2:3],
                             cc1[:, 7:8])
        for o in range(2):
            with nc.Block() as block:
                @block.vector
                def _(v):
                    v.scalar_tensor_tensor(jf[:, o:o + 1],
                                           cc1[:, 1 + 2 * o:2 + 2 * o],
                                           float(CAP),
                                           cc1[:, 4 + o:5 + o],
                                           op0=ALU.mult, op1=ALU.add)
            with nc.Block() as block:
                @block.vector
                def _(v):
                    v.scalar_tensor_tensor(jf[:, o:o + 1], jf[:, o:o + 1],
                                           -BIGI, cc1[:, 6 + o:7 + o],
                                           op0=ALU.add, op1=ALU.mult)
            with nc.Block() as block:
                @block.vector
                def _(v):
                    v.tensor_scalar_add(jf[:, o:o + 1], jf[:, o:o + 1], BIGI)
        with nc.Block() as block:
            @block.vector
            def _(v):
                v.tensor_copy(jj[:, 2 * i:2 * i + 2], jf[:, 0:2])

    for e in range(8):
        for i in range(NT):
            with nc.Block() as block:
                @block.vector
                def _(v):
                    v.tensor_scalar(eqb, iota_sb[:],
                                    seqeq_all[:, i * 8 + e:i * 8 + e + 1],
                                    0.0, op0=ALU.is_equal, op1=ALU.add)
            with nc.Block() as block:
                @block.tensor
                def _(t):
                    t.matmul(psM[0][0:1, 0:512], tokf_sb[:, i:i + 1],
                             eqb[:, 0:512], start=(i == 0),
                             stop=(i == NT - 1))
                    t.matmul(psM[0][0:1, 512:768], tokf_sb[:, i:i + 1],
                             eqb[:, 512:768], start=(i == 0),
                             stop=(i == NT - 1))
        with nc.Block() as block:
            @block.scalar
            def _(s):
                s.copy(idxrow[:], psM[0][0:1, 0:768])
        with nc.Block() as block:
            @block.vector
            def _(v):
                v.tensor_copy(idxrow_i[:], idxrow[:])
        dma_store(ctx, [(idx_dram[e * CAP:(e + 1) * CAP, :]
                         .rearrange("r one -> one r"), idxrow_i[:])])

    for e in range(8):
        dma_load(ctx, [(b1c_sb[:], b1c_dram[e]),
                       (b2r_sb[:], b2r_dram[e])])
        with nc.Block() as block:
            @block.vector
            def _(v):
                v.tensor_copy(b2r_bf[:], b2r_sb[:])
        for st in range(6):
            r0 = e * CAP + st * 128
            dma_load(ctx, [(idxcol[:], idx_dram[r0:r0 + 128, :])])
            sem_g = ctx.named_sem("gath")
            base_g = ctx.semc["gath"]
            ctx.semc["gath"] = base_g + 16
            with nc.Block() as block:
                @block.gpsimd
                def _(g):
                    sem, base = sem_g, base_g
                    g.indirect_dma_start(
                        out=disp_tok[:],
                        out_offset=None,
                        in_=h2_dram[:, :],
                        in_offset=IndirectOffsetOnAxis(idxcol[:], 0))\
                        .then_inc(sem, 16)
                    g.wait_ge(sem, base + 16)
            with nc.Block() as block:
                @block.tensor
                def _(t):
                    for dc in range(NCHUNK):
                        t.transpose(
                            psM[0][:, :].bitcast(BF16)[:, dc * 128:(dc + 1) * 128],
                            disp_tok[:, dc * 128:(dc + 1) * 128], ident_bf[:])
            with nc.Block() as block:
                @block.vector
                def _(v):
                    for dc in range(NCHUNK):
                        v.tensor_copy(dispT[dc][:, st * 128:(st + 1) * 128],
                                      psM[0][:, :].bitcast(BF16)
                                      [:, dc * 128:(dc + 1) * 128])
        w1_r = w1_dram[e].rearrange("(c p) n -> p c n", p=128)
        for hc in range(2):
            ssl = slice(hc * HC, (hc + 1) * HC)
            for fc in range(NF):
                dma_load(ctx, [(w1st_f[:].rearrange("p (c n) -> p c n", n=128),
                                w1_r[:, :, fc * 128:(fc + 1) * 128])])
                with nc.Block() as block:
                    @block.vector
                    def _(v):
                        v.tensor_copy(w1st_b[:], w1st_f[:])
                with nc.Block() as block:
                    @block.tensor
                    def _(t):
                        for dc in range(NCHUNK):
                            lhsT = w1st_b[:].rearrange("p (c n) -> p c n",
                                                       n=128)[:, dc, :]
                            t.matmul(psS[:, 0:HC], lhsT, dispT[dc][:, ssl],
                                     start=(dc == 0), stop=(dc == NCHUNK - 1))
                with nc.Block() as block:
                    @block.scalar
                    def _(s):
                        s.activation(midb[:, fc * HC:(fc + 1) * HC],
                                     psS[:, 0:HC], AF.Gelu,
                                     bias=b1c_sb[:, fc:fc + 1])
            for fc in range(NF):
                r0 = fc * 128
                dma_load(ctx, [(w2st_f[:], w2_dram[e][r0:r0 + 128, :])])
                with nc.Block() as block:
                    @block.vector
                    def _(v):
                        v.tensor_copy(w2st_b[:], w2st_f[:])
                with nc.Block() as block:
                    @block.tensor
                    def _(t):
                        for st3 in range(3):
                            lhsT = midb[:, fc * HC + st3 * 128:
                                        fc * HC + (st3 + 1) * 128]
                            t.matmul(psM[st3][:, 0:512], lhsT,
                                     w2st_b[:, 0:512],
                                     start=(fc == 0), stop=False)
                            t.matmul(psM[st3][:, 512:1024], lhsT,
                                     w2st_b[:, 512:1024],
                                     start=(fc == 0), stop=False)
            with nc.Block() as block:
                @block.tensor
                def _(t):
                    for st3 in range(3):
                        t.matmul(psM[st3][:, 0:512], ctx.ones_col_bf[:],
                                 b2r_bf[0:1, 0:512], start=False, stop=True)
                        t.matmul(psM[st3][:, 512:1024], ctx.ones_col_bf[:],
                                 b2r_bf[0:1, 512:1024], start=False, stop=True)
            for st3 in range(3):
                st = hc * 3 + st3
                with nc.Block() as block:
                    @block.scalar
                    def _(s):
                        s.copy(disp_tok[:], psM[st3][:, :])
                r0 = e * CAP + st * 128
                dma_store(ctx, [(exp_dram[r0:r0 + 128, :], disp_tok[:])])

    for i in range(NT):
        rs = slice(i * 128, (i + 1) * 128)
        baset = w1st_f
        t1 = w2st_f
        dma_load(ctx, [(baset[:], base_dram[rs, :])])
        sem_g2 = ctx.named_sem("gath2")
        base_g2 = ctx.semc["gath2"]
        ctx.semc["gath2"] = base_g2 + 32
        with nc.Block() as block:
            @block.gpsimd
            def _(g):
                sem, base = sem_g2, base_g2
                g.indirect_dma_start(
                    out=gA[:], out_offset=None, in_=exp_dram[:, :],
                    in_offset=IndirectOffsetOnAxis(jj[:, 2 * i:2 * i + 1], 0),
                    bounds_check=8 * CAP - 1,
                    oob_is_err=False).then_inc(sem, 16)
                g.indirect_dma_start(
                    out=gB[:], out_offset=None, in_=exp_dram[:, :],
                    in_offset=IndirectOffsetOnAxis(jj[:, 2 * i + 1:2 * i + 2], 0),
                    bounds_check=8 * CAP - 1,
                    oob_is_err=False).then_inc(sem, 16)
                g.wait_ge(sem, base + 32)
        with nc.Block() as block:
            @block.vector
            def _(v):
                v.scalar_tensor_tensor(t1[:], gA[:], gv[:, 2 * i:2 * i + 1],
                                       baset[:], op0=ALU.mult, op1=ALU.add)
        with nc.Block() as block:
            @block.vector
            def _(v):
                v.scalar_tensor_tensor(yst[:], gB[:],
                                       gv[:, 2 * i + 1:2 * i + 2],
                                       t1[:], op0=ALU.mult, op1=ALU.add)
        dma_store(ctx, [(y_dram[rs, :], yst[:])])



def host_consts_b(core):
    pref = np.zeros((8, 1), np.float32)
    pref[:core] = 1.0
    iota = np.broadcast_to(np.arange(CAP, dtype=np.float32), (128, CAP)).copy()
    tki = (np.arange(T, dtype=np.int32).reshape(16, 128).T).copy()
    return pref, iota, tki






TOPK = 614
SIG_THR = float(2 * TOPK - 2048)
NITER = 36
INV32 = 1.0 / 32.0


def build_l1attn(ctx, x_dram, wqkv_dram, bqkv_cols_dram, bv_row_dram,
                 wout_dram, bout_cols_dram, ident_dram, y_dram):
    nc = ctx.nc
    A = [ctx.sbuf(f"A{c}", [128, T], F32) for c in range(NCHUNK)]
    q_sb = [ctx.sbuf(f"q{c}", [128, T], BF16) for c in range(NCHUNK)]
    k_sb = [ctx.sbuf(f"k{c}", [128, T], BF16) for c in range(NCHUNK)]
    v_sb = [ctx.sbuf(f"v{i}", [128, D], BF16) for i in range(NT)]
    ln_mean = ctx.sbuf("lnmean", [1, T], F32)
    ln_r = ctx.sbuf("lnr", [1, T], F32)
    lnscr = [ctx.sbuf(f"lnscr{i}", [128, 512], F32) for i in range(2)]
    bqkv_cols = ctx.sbuf("bqkv_cols_sb", [128, 24], F32)
    bout_cols = ctx.sbuf("bout_cols_sb", [128, 8], F32)
    bv_row = ctx.sbuf("bv_row_sb", [1, D], F32)
    bv_row_bf = ctx.sbuf("bv_row_bf", [1, D], BF16)
    ident = ctx.sbuf("ident_sb", [128, 128], F32)
    z_row = ctx.sbuf("z_row", [1, 128], F32)
    invz_row = ctx.sbuf("invz_row", [1, 128], F32)
    invzbc = ctx.sbuf("invzbc", [64, 128], F32)
    lo_c = ctx.sbuf("lo_c", [128, 4], F32)
    hi_c = ctx.sbuf("hi_c", [128, 4], F32)
    t_c = ctx.sbuf("t_c", [128, 4], F32)
    nt_c = ctx.sbuf("nt_c", [128, 4], F32)
    sig2_c = ctx.sbuf("sig2_c", [128, 8], F32)
    sig_c = ctx.sbuf("sig_c", [128, 4], F32)
    mge_c = ctx.sbuf("mge_c", [128, 4], U8)
    mlt_c = ctx.sbuf("mlt_c", [128, 4], U8)
    nlo_c = ctx.sbuf("nlo_c", [128, 4], F32)
    nt32_c = ctx.sbuf("nt32_c", [128, 4], F32)
    z_cols = ctx.sbuf("z_cols", [128, 16], F32)
    zh_c = ctx.sbuf("zh_c", [128, 2], F32)
    sgnbuf = ctx.sbuf("sgnbuf", [128, 1024], BF16)
    ebuf = ctx.sbuf("ebuf", [128, 1024], BF16)
    pmbuf = ctx.sbuf("pmbuf", [128, 1024], BF16)
    p_sb = ctx.sbuf("p_sb", [128, T], BF16)

    aux = {"mean": ln_mean, "r": ln_r, "mr": None, "negones": None,
           "scr": lnscr}

    Abf = [a[:].bitcast(BF16) for a in A]
    Af32hi = [a[:, 1024:2048] for a in A]
    xln = [Abf[c][:, 0:2048] for c in range(NCHUNK)]
    attn = xln
    wst_f32 = Af32hi[0]
    wst_bf = Abf[1][:, 2048:3072]
    wv_bf = [Abf[4 + j][:, 2048:4096] for j in range(4)]

    def sc_half(tq, hf):
        return Af32hi[2 * tq + hf]

    dma_load(ctx, [(A[c][:], x_dram[c * 128:(c + 1) * 128, :])
                   for c in range(NCHUNK)]
             + [(bqkv_cols[:], bqkv_cols_dram[:, :]),
                (bout_cols[:], bout_cols_dram[:, :]),
                (bv_row[:], bv_row_dram[:, :]),
                (ident[:], ident_dram[:, :])])
    with nc.Block() as block:
        @block.vector
        def _(v):
            v.tensor_copy(bv_row_bf[:], bv_row[:])
            v.tensor_copy(ctx.ident_bf[:], ident[:])

    emit_ln(ctx, [a[:] for a in A], xln, aux)

    wqkv_r = wqkv_dram.rearrange("(c p) n -> p c n", p=128)
    for qk in range(2):
        dst = q_sb if qk == 0 else k_sb
        for oc in range(NCHUNK):
            ocol = qk * 1024 + oc * 128
            dma_load(ctx, [(wst_f32.rearrange("p (c n) -> p c n", n=128),
                            wqkv_r[:, :, ocol:ocol + 128])])
            with nc.Block() as block:
                @block.vector
                def _(v):
                    v.tensor_copy(wst_bf, wst_f32)
            with nc.Block() as block:
                @block.tensor
                def _(t):
                    for dc in range(NCHUNK):
                        lhsT = wst_bf.rearrange("p (c n) -> p c n", n=128)[:, dc, :]
                        for f in range(4):
                            ps = [ctx.psX, ctx.psY][f // 2]
                            psl = slice((f % 2) * 512, (f % 2 + 1) * 512)
                            t.matmul(ps[:, psl], lhsT,
                                     xln[dc][:, f * 512:(f + 1) * 512],
                                     start=(dc == 0), stop=(dc == NCHUNK - 1))
            with nc.Block() as block:
                @block.scalar
                def _(s):
                    bcol = bqkv_cols[:, qk * 8 + oc:qk * 8 + oc + 1]
                    s.activation(dst[oc][:, 0:1024], ctx.psX[:], AF.Identity,
                                 bias=bcol)
                    s.activation(dst[oc][:, 1024:2048], ctx.psY[:], AF.Identity,
                                 bias=bcol)

    for dc in range(NCHUNK):
        dma_load(ctx, [(wst_f32, wqkv_r[:, dc, 2048:3072])])
        with nc.Block() as block:
            @block.vector
            def _(v):
                half = wv_bf[dc // 2]
                v.tensor_copy(half[:, (dc % 2) * 1024:(dc % 2 + 1) * 1024],
                              wst_f32)
    for i in range(NT):
        tsl = slice(i * 128, (i + 1) * 128)
        with nc.Block() as block:
            @block.tensor
            def _(t):
                for dc in range(NCHUNK):
                    wv_dc = wv_bf[dc // 2][:, (dc % 2) * 1024:(dc % 2 + 1) * 1024]
                    for f in range(2):
                        t.matmul(ctx.psX[:, f * 512:(f + 1) * 512],
                                 xln[dc][:, tsl],
                                 wv_dc[:, f * 512:(f + 1) * 512],
                                 start=(dc == 0), stop=False)
                for f in range(2):
                    t.matmul(ctx.psX[:, f * 512:(f + 1) * 512],
                             ctx.ones_col_bf[:],
                             bv_row_bf[0:1, f * 512:(f + 1) * 512],
                             start=False, stop=True)
        with nc.Block() as block:
            @block.scalar
            def _(s):
                s.copy(v_sb[i][:], ctx.psX[:])

    for h in range(NHEADS):
        ch, off = h // 2, (h % 2) * 64
        qh = q_sb[ch][off:off + 64, :]
        kh = k_sb[ch][off:off + 64, :]
        hsl = slice(h * 64, (h + 1) * 64)

        for grp in range(4):
            for tq in range(4):
                qt = grp * 4 + tq
                with nc.Block() as block:
                    @block.tensor
                    def _(t):
                        for f in range(4):
                            ps = [ctx.psX, ctx.psY][f // 2]
                            psl = slice((f % 2) * 512, (f % 2 + 1) * 512)
                            t.matmul(ps[:, psl],
                                     qh[:, qt * 128:(qt + 1) * 128],
                                     kh[:, f * 512:(f + 1) * 512],
                                     start=True, stop=True)
                with nc.Block() as block:
                    @block.scalar
                    def _(s):
                        s.copy(sc_half(tq, 0), ctx.psX[:])
                        s.copy(sc_half(tq, 1), ctx.psY[:])
            with nc.Block() as block:
                @block.vector
                def _(v):
                    for tq in range(4):
                        v.reduce_max(sig2_c[:, 2 * tq:2 * tq + 1],
                                     sc_half(tq, 0), axis=mybir.AxisListType.X)
                        v.reduce_max(sig2_c[:, 2 * tq + 1:2 * tq + 2],
                                     sc_half(tq, 1), axis=mybir.AxisListType.X)
            with nc.Block() as block:
                @block.vector
                def _(v):
                    v.tensor_max(hi_c[:], sig2_c[:, 0:8:2], sig2_c[:, 1:8:2])
            with nc.Block() as block:
                @block.vector
                def _(v):
                    for tq in range(4):
                        v.tensor_reduce(sig2_c[:, 2 * tq:2 * tq + 1],
                                        sc_half(tq, 0),
                                        axis=mybir.AxisListType.X, op=ALU.min)
                        v.tensor_reduce(sig2_c[:, 2 * tq + 1:2 * tq + 2],
                                        sc_half(tq, 1),
                                        axis=mybir.AxisListType.X, op=ALU.min)
            with nc.Block() as block:
                @block.vector
                def _(v):
                    v.tensor_tensor(t_c[:], sig2_c[:, 0:8:2], sig2_c[:, 1:8:2],
                                    op=ALU.min)
            with nc.Block() as block:
                @block.vector
                def _(v):
                    v.tensor_scalar_add(lo_c[:], t_c[:], -1.0)
            VA = ctx.named_sem("bis_va"); SA = ctx.named_sem("bis_sa")
            VS = ctx.named_sem("bis_vs"); SS = ctx.named_sem("bis_ss")
            vab = ctx.semc["bis_va"]; sab = ctx.semc["bis_sa"]
            vsb = ctx.semc["bis_vs"]; ssb = ctx.semc["bis_ss"]
            ctx.semc["bis_va"] = vab + 5 * NITER
            ctx.semc["bis_sa"] = sab + NITER
            ctx.semc["bis_vs"] = vsb + NITER
            ctx.semc["bis_ss"] = ssb + NITER
            with nc.Block() as block:
                @block.vector
                def _(v):
                    for it in range(NITER):
                        A0 = vab + 5 * it
                        v.tensor_add(sig_c[:], lo_c[:], hi_c[:])\
                            .then_inc(VA, 1)
                        v.wait_ge(VA, A0 + 1)
                        v.tensor_scalar_mul(t_c[:], sig_c[:], 0.5)\
                            .then_inc(VS, 1)
                        v.wait_ge(SS, ssb + it + 1)
                        v.tensor_add(sig_c[:], sig2_c[:, 0:8:2],
                                     sig2_c[:, 1:8:2]).then_inc(VA, 1)
                        v.wait_ge(VA, A0 + 2)
                        v.tensor_single_scalar(mge_c[:], sig_c[:], SIG_THR,
                                               op=ALU.is_ge).then_inc(VA, 1)
                        v.tensor_single_scalar(mlt_c[:], sig_c[:], SIG_THR,
                                               op=ALU.is_lt).then_inc(VA, 1)
                        v.wait_ge(VA, A0 + 4)
                        v.copy_predicated(lo_c[:], mge_c[:], t_c[:])
                        v.copy_predicated(hi_c[:], mlt_c[:], t_c[:])\
                            .then_inc(VA, 1)
                        v.wait_ge(VA, A0 + 5)

                @block.scalar
                def _(s):
                    for it in range(NITER):
                        s.wait_ge(VS, vsb + it + 1)
                        s.mul(nt_c[:], t_c[:], -1.0).then_inc(SA, 1)
                        s.wait_ge(SA, sab + it + 1)
                        for tq in range(4):
                            for hf in range(2):
                                ins = s.activation(
                                    pmbuf[:], sc_half(tq, hf), AF.Sign,
                                    bias=nt_c[:, tq:tq + 1],
                                    accum_out=sig2_c[:, 2 * tq + hf:
                                                     2 * tq + hf + 1])
                        ins.then_inc(SS, 1)
            with nc.Block() as block:
                @block.scalar
                def _(s):
                    s.mul(nlo_c[:], lo_c[:], -1.0)
                    s.mul(nt32_c[:], lo_c[:], -INV32)
            for tq in range(4):
                qt = grp * 4 + tq
                for hf in range(2):
                    with nc.Block() as block:
                        @block.scalar
                        def _(s):
                            s.activation(sgnbuf[:], sc_half(tq, hf), AF.Sign,
                                         bias=nlo_c[:, tq:tq + 1])
                            s.activation(ebuf[:], sc_half(tq, hf), AF.Exp,
                                         bias=nt32_c[:, tq:tq + 1],
                                         scale=INV32)
                    with nc.Block() as block:
                        @block.vector
                        def _(v):
                            v.tensor_mul(pmbuf[:], ebuf[:], sgnbuf[:])
                    with nc.Block() as block:
                        @block.scalar
                        def _(s):
                            s.activation(p_sb[:, hf * 1024:(hf + 1) * 1024],
                                         pmbuf[:], AF.Relu,
                                         accum_out=zh_c[:, hf:hf + 1])
                with nc.Block() as block:
                    @block.vector
                    def _(v):
                        v.tensor_add(z_cols[:, qt:qt + 1], zh_c[:, 0:1],
                                     zh_c[:, 1:2])
                psAb = ctx.psA[:, :].bitcast(BF16)
                psBb = ctx.psB[:, :].bitcast(BF16)
                with nc.Block() as block:
                    @block.tensor
                    def _(t):
                        for kc in range(NT):
                            dstp = [psAb, psBb][kc % 2]
                            t.transpose(dstp[:, (kc // 2) * 128:
                                             (kc // 2 + 1) * 128],
                                        p_sb[:, kc * 128:(kc + 1) * 128],
                                        ctx.ident_bf[:])
                with nc.Block() as block:
                    @block.vector
                    def _(v):
                        v.tensor_copy(p_sb[:, 0:1024], psAb[:, :])
                        v.tensor_copy(p_sb[:, 1024:2048], psBb[:, :])
                with nc.Block() as block:
                    @block.tensor
                    def _(t):
                        for kc in range(NT):
                            rhs = p_sb[:, (kc % 2) * 1024 + (kc // 2) * 128:
                                       (kc % 2) * 1024 + (kc // 2) * 128 + 128]
                            t.matmul(ctx.psX[0:64, 0:128], v_sb[kc][:, hsl],
                                     rhs, start=(kc == 0), stop=(kc == NT - 1))
                    @block.tensor
                    def _(t2):
                        pass
                with nc.Block() as block:
                    @block.tensor
                    def _(t):
                        t.matmul(ctx.psR1[0:1, 0:128], z_cols[:, qt:qt + 1],
                                 ident[:], start=True, stop=True)
                with nc.Block() as block:
                    @block.scalar
                    def _(s):
                        s.copy(z_row[0:1, 0:128], ctx.psR1[0:1, 0:128])
                with nc.Block() as block:
                    @block.vector
                    def _(v):
                        v.reciprocal(invz_row[0:1, 0:128], z_row[0:1, 0:128])
                with nc.Block() as block:
                    @block.tensor
                    def _(t):
                        t.matmul(ctx.psB[0:64, 0:128], ctx.ones_col[0:1, 0:64],
                                 invz_row[0:1, 0:128], start=True, stop=True)
                with nc.Block() as block:
                    @block.scalar
                    def _(s):
                        s.copy(invzbc[:, :], ctx.psB[0:64, 0:128])
                with nc.Block() as block:
                    @block.vector
                    def _(v):
                        v.tensor_mul(attn[ch][off:off + 64,
                                              qt * 128:(qt + 1) * 128],
                                     ctx.psX[0:64, 0:128], invzbc[:, :])

    wo_f32 = q_sb[1][:].bitcast(F32)[:, 0:1024]
    wo_bf = q_sb[2][:, 0:1024]
    xre = q_sb[0][:].bitcast(F32)[:, 0:1024]
    yst = k_sb[0][:].bitcast(F32)[:, 0:1024]
    wout_r = wout_dram.rearrange("(c p) n -> p c n", p=128)
    for oc in range(NCHUNK):
        dma_load(ctx, [(wo_f32.rearrange("p (c n) -> p c n", n=128),
                        wout_r[:, :, oc * 128:(oc + 1) * 128])])
        with nc.Block() as block:
            @block.vector
            def _(v):
                v.tensor_copy(wo_bf, wo_f32)
        with nc.Block() as block:
            @block.tensor
            def _(t):
                for dc in range(NCHUNK):
                    lhsT = wo_bf.rearrange("p (c n) -> p c n", n=128)[:, dc, :]
                    for f in range(4):
                        ps = [ctx.psX, ctx.psY][f // 2]
                        psl = slice((f % 2) * 512, (f % 2 + 1) * 512)
                        t.matmul(ps[:, psl], lhsT,
                                 attn[dc][:, f * 512:(f + 1) * 512],
                                 start=(dc == 0), stop=(dc == NCHUNK - 1))
        for half in range(2):
            tsl = slice(half * 1024, (half + 1) * 1024)
            dma_load(ctx, [(xre, x_dram[oc * 128:(oc + 1) * 128, tsl])])
            with nc.Block() as block:
                @block.vector
                def _(v):
                    bcol = bout_cols[:, oc:oc + 1]
                    v.scalar_tensor_tensor(
                        yst, [ctx.psX, ctx.psY][half][:], bcol, xre,
                        op0=ALU.add, op1=ALU.add)
            dma_store(ctx, [(y_dram[oc * 128:(oc + 1) * 128, tsl], yst)])




SCALE = 1.0 / 127.0
HID = 3072
NH = HID // 128


def build_l1ffn(ctx, x_dram, wg_dram, wu_dram, wd_dram, y_dram):
    nc = ctx.nc
    x_sb = [ctx.sbuf(f"x{c}", [128, T], F32) for c in range(NCHUNK)]
    xln = [ctx.sbuf(f"xln{c}", [128, T], BF16) for c in range(NCHUNK)]
    h_all = [ctx.sbuf(f"h{c}", [128, 1024], BF16) for c in range(NH)]
    wi_buf = ctx.sbuf("wi_buf", [128, HID], I32)
    wb_buf = ctx.sbuf("wb_buf", [128, HID], BF16)
    y_sb = [ctx.sbuf("y0", [128, 1024], F32)] * 2
    aux = alloc_ln_aux(ctx, T)
    sg_sb = ctx.sbuf("sg_sb", [128, 1024], F32)

    wg_r = wg_dram.rearrange("(c p) n -> p c n", p=128)
    wu_r = wu_dram.rearrange("(c p) n -> p c n", p=128)
    wd_r = wd_dram.rearrange("(c p) n -> p c n", p=128)

    dma_load(ctx, [(x_sb[c][:], x_dram[c * 128:(c + 1) * 128, :])
                   for c in range(NCHUNK)])
    emit_ln(ctx, [x[:] for x in x_sb], [o[:] for o in xln], aux)

    wiv = wi_buf[:].rearrange("p (c n) -> p c n", n=128)
    wbv = wb_buf[:].rearrange("p (c n) -> p c n", n=128)

    for half in range(2):
        tsl = slice(half * 1024, (half + 1) * 1024)
        for oc in range(NH):
            osl = slice(oc * 128, (oc + 1) * 128)
            dma_load(ctx, [
                (wiv[:, 0:8, :], wg_r[:, :, osl]),
                (wiv[:, 8:16, :], wu_r[:, :, osl]),
            ])
            with nc.Block() as block:
                @block.vector
                def _(v):
                    v.tensor_copy(wbv[:, 0:16, :], wiv[:, 0:16, :])
            with nc.Block() as block:
                @block.tensor
                def _(t):
                    for dc in range(NCHUNK):
                        for f in range(2):
                            qs = slice(half * 1024 + f * 512,
                                       half * 1024 + (f + 1) * 512)
                            ps = slice(f * 512, (f + 1) * 512)
                            t.matmul(ctx.psX[:, ps], wbv[:, dc, :],
                                     xln[dc][:, qs],
                                     start=(dc == 0), stop=(dc == NCHUNK - 1))
                            t.matmul(ctx.psY[:, ps], wbv[:, 8 + dc, :],
                                     xln[dc][:, qs],
                                     start=(dc == 0), stop=(dc == NCHUNK - 1))
            with nc.Block() as block:
                @block.scalar
                def _(s):
                    s.activation(sg_sb[:], ctx.psX[:], AF.Silu,
                                 scale=SCALE)
            with nc.Block() as block:
                @block.vector
                def _(v):
                    v.scalar_tensor_tensor(h_all[oc][:], ctx.psY[:], SCALE,
                                           sg_sb[:],
                                           op0=ALU.mult, op1=ALU.mult)
        for o in range(NCHUNK):
            osl = slice(o * 128, (o + 1) * 128)
            dma_load(ctx, [(wiv[:, 0:24, :], wd_r[:, :, osl])])
            with nc.Block() as block:
                @block.vector
                def _(v):
                    v.tensor_copy(wbv[:, 0:24, :], wiv[:, 0:24, :])
            with nc.Block() as block:
                @block.tensor
                def _(t):
                    for hc in range(NH):
                        for f in range(2):
                            ps = slice(f * 512, (f + 1) * 512)
                            t.matmul(ctx.psX[:, ps], wbv[:, hc, :],
                                     h_all[hc][:, ps],
                                     start=(hc == 0), stop=(hc == NH - 1))
            with nc.Block() as block:
                @block.vector
                def _(v):
                    v.scalar_tensor_tensor(y_sb[half][:], ctx.psX[:],
                                           SCALE, x_sb[o][:, tsl],
                                           op0=ALU.mult, op1=ALU.add)
            dma_store(ctx, [(y_dram[o * 128:(o + 1) * 128, tsl],
                             y_sb[half][:])])





_PROG_CACHE = {}


def _prog_l0attn():
    ctx = make_ctx()
    nc = ctx.nc
    x = nc.dram_tensor("x_T", [D, T], F32, kind="ExternalInput").ap()
    wqkv = nc.dram_tensor("wqkv_T", [D, 3 * D], F32, kind="ExternalInput").ap()
    bqkv = nc.dram_tensor("bqkv_cols", [128, 24], F32, kind="ExternalInput").ap()
    bvr = nc.dram_tensor("bv_row", [1, D], F32, kind="ExternalInput").ap()
    wout = nc.dram_tensor("wout_T", [D, D], F32, kind="ExternalInput").ap()
    bout = nc.dram_tensor("bout_cols", [128, 8], F32, kind="ExternalInput").ap()
    ident = nc.dram_tensor("ident128", [128, 128], F32, kind="ExternalInput").ap()
    y = nc.dram_tensor("y_T", [D, T], F32, kind="ExternalOutput").ap()
    with ctx.es:
        load_consts(ctx)
        build_l0attn(ctx, x, wqkv, bqkv, bvr, wout, bout, ident, y)
    return ctx.nc


def _prog_a1():
    ctx = make_ctx()
    nc = ctx.nc
    x = nc.dram_tensor("x_T", [D, T], F32, kind="ExternalInput").ap()
    wq = nc.dram_tensor("wq_i", [D, D], F32, kind="ExternalInput").ap()
    wk = nc.dram_tensor("wk_i", [D, D], F32, kind="ExternalInput").ap()
    wv = nc.dram_tensor("wv_i", [D, D], F32, kind="ExternalInput").ap()
    wp = nc.dram_tensor("wproj", [D, D], F32, kind="ExternalInput").ap()
    g1 = nc.dram_tensor("g1_row", [1, D], F32, kind="ExternalInput").ap()
    b1 = nc.dram_tensor("b1_row", [1, D], F32, kind="ExternalInput").ap()
    cs = nc.dram_tensor("cos_tab", [128, T], F32, kind="ExternalInput").ap()
    sn = nc.dram_tensor("sin_tab", [128, T], F32, kind="ExternalInput").ap()
    ident = nc.dram_tensor("ident128", [128, 128], F32, kind="ExternalInput").ap()
    mkt = nc.dram_tensor("maskt", [128, 128], F32, kind="ExternalInput").ap()
    xf = nc.dram_tensor("xf_T", [D, T], F32, kind="ExternalOutput").ap()
    with ctx.es:
        load_consts(ctx)
        build_a1(ctx, x, wq, wk, wv, wp, g1, b1, cs, sn, ident, mkt, xf)
    return ctx.nc


def _prog_a2():
    ctx = make_ctx()
    nc = ctx.nc
    xf = nc.dram_tensor("xf_T", [D, T], F32, kind="ExternalInput").ap()
    x1 = nc.dram_tensor("x1_T", [D, T], F32, kind="ExternalInput").ap()
    g2 = nc.dram_tensor("g2_row", [1, D], F32, kind="ExternalInput").ap()
    b2 = nc.dram_tensor("b2_row", [1, D], F32, kind="ExternalInput").ap()
    wg = nc.dram_tensor("wgate", [D, 8], F32, kind="ExternalInput").ap()
    ident = nc.dram_tensor("ident128", [128, 128], F32, kind="ExternalInput").ap()
    h2t = nc.dram_tensor("h2tok", [T, D], BF16, kind="ExternalOutput").ap()
    bt = nc.dram_tensor("base_tok", [T, D], F32, kind="ExternalOutput").ap()
    gd = nc.dram_tensor("gates_sm", [T, 8], F32, kind="ExternalOutput").ap()
    md = nc.dram_tensor("mask_tok", [T, 8], F32, kind="ExternalOutput").ap()
    pd = nc.dram_tensor("pos_tok", [T, 8], F32, kind="ExternalOutput").ap()
    cd = nc.dram_tensor("counts_row", [1, 8], F32, kind="ExternalOutput").ap()
    with ctx.es:
        load_consts(ctx)
        build_a2(ctx, xf, x1, g2, b2, wg, ident, h2t, bt, gd, md, pd, cd)
    return ctx.nc


def _prog_b():
    ctx = make_ctx(dma_scratch=32768)
    nc = ctx.nc
    cnt = nc.dram_tensor("counts_all", [8, 8], F32, kind="ExternalInput").ap()
    prf = nc.dram_tensor("prefmask", [8, 1], F32, kind="ExternalInput").ap()
    h2 = nc.dram_tensor("h2tok", [T, D], BF16, kind="ExternalInput").ap()
    bs = nc.dram_tensor("base_tok", [T, D], F32, kind="ExternalInput").ap()
    gd = nc.dram_tensor("gates_sm", [T, 8], F32, kind="ExternalInput").ap()
    md = nc.dram_tensor("mask_tok", [T, 8], F32, kind="ExternalInput").ap()
    pd = nc.dram_tensor("pos_tok", [T, 8], F32, kind="ExternalInput").ap()
    w1 = nc.dram_tensor("w1_all", [8, D, 4096], F32, kind="ExternalInput").ap()
    b1c = nc.dram_tensor("b1c_all", [8, 128, 32], F32, kind="ExternalInput").ap()
    w2 = nc.dram_tensor("w2_all", [8, 4096, D], F32, kind="ExternalInput").ap()
    b2r = nc.dram_tensor("b2r_all", [8, 1, D], F32, kind="ExternalInput").ap()
    iot = nc.dram_tensor("iota768", [128, CAP], F32, kind="ExternalInput").ap()
    tki = nc.dram_tensor("tokidx_cols", [128, 16], I32, kind="ExternalInput").ap()
    ident = nc.dram_tensor("ident128", [128, 128], F32, kind="ExternalInput").ap()
    y = nc.dram_tensor("y_tok", [T, D], F32, kind="ExternalOutput").ap()
    with ctx.es:
        load_consts(ctx, pool=False)
        build_b(ctx, cnt, prf, h2, bs, gd, md, pd, w1, b1c, w2, b2r,
                iot, tki, ident, y)
    return ctx.nc


def _prog_l1attn():
    ctx = make_ctx()
    nc = ctx.nc
    x = nc.dram_tensor("x_T", [D, T], F32, kind="ExternalInput").ap()
    wqkv = nc.dram_tensor("wqkv_T", [D, 3 * D], F32, kind="ExternalInput").ap()
    bqkv = nc.dram_tensor("bqkv_cols", [128, 24], F32, kind="ExternalInput").ap()
    bvr = nc.dram_tensor("bv_row", [1, D], F32, kind="ExternalInput").ap()
    wout = nc.dram_tensor("wout_T", [D, D], F32, kind="ExternalInput").ap()
    bout = nc.dram_tensor("bout_cols", [128, 8], F32, kind="ExternalInput").ap()
    ident = nc.dram_tensor("ident128", [128, 128], F32, kind="ExternalInput").ap()
    y = nc.dram_tensor("y_T", [D, T], F32, kind="ExternalOutput").ap()
    with ctx.es:
        load_consts(ctx)
        ctx.ident_bf = ctx.sbuf("ident_bf", [128, 128], BF16)
        build_l1attn(ctx, x, wqkv, bqkv, bvr, wout, bout, ident, y)
    return ctx.nc


def _prog_l1ffn():
    ctx = make_ctx()
    nc = ctx.nc
    x = nc.dram_tensor("x_T", [D, T], F32, kind="ExternalInput").ap()
    wg = nc.dram_tensor("wg_T", [D, HID], I32, kind="ExternalInput").ap()
    wu = nc.dram_tensor("wu_T", [D, HID], I32, kind="ExternalInput").ap()
    wd = nc.dram_tensor("wd_T", [HID, D], I32, kind="ExternalInput").ap()
    y = nc.dram_tensor("y_T", [D, T], F32, kind="ExternalOutput").ap()
    with ctx.es:
        load_consts(ctx)
        build_l1ffn(ctx, x, wg, wu, wd, y)
    return ctx.nc


def _get_prog(name):
    if name not in _PROG_CACHE:
        _PROG_CACHE[name] = {
            "l0attn": _prog_l0attn, "a1": _prog_a1, "a2": _prog_a2,
            "b": _prog_b, "l1attn": _prog_l1attn, "l1ffn": _prog_l1ffn,
        }[name]()
    return _PROG_CACHE[name]



def _rope_tables():
    hd = 64
    inv_freq = 1.0 / 10000.0 ** (np.arange(0, hd, 2, dtype=np.float32) / hd)
    ang = np.arange(T, dtype=np.float32)[:, None] * inv_freq
    sin = np.sin(ang).astype(np.float32).T
    cos = np.cos(ang).astype(np.float32).T
    C = np.empty((128, T), np.float32)
    S = np.empty((128, T), np.float32)
    for g in range(2):
        o = g * 64
        C[o + 0:o + 64:2] = cos
        C[o + 1:o + 64:2] = cos
        S[o + 0:o + 64:2] = -sin
        S[o + 1:o + 64:2] = sin
    return C, S


def _perm_inner_qkv(Wqkv):
    perm = np.empty(64, np.int64)
    perm[0::2] = np.arange(32)
    perm[1::2] = np.arange(32, 64)
    wq = np.empty((D, D), np.float32)
    wk = np.empty((D, D), np.float32)
    wv = np.empty((D, D), np.float32)
    for h in range(16):
        base = h * 192
        wq[:, h * 64:(h + 1) * 64] = Wqkv[:, base:base + 64][:, perm]
        wk[:, h * 64:(h + 1) * 64] = Wqkv[:, base + 64:base + 128][:, perm]
        wv[:, h * 64:(h + 1) * 64] = Wqkv[:, base + 128:base + 192]
    return wq, wk, wv



def kernel(L0_attn_x, L0_attn_w_qkv, L0_attn_b_qkv, L0_attn_w_out,
           L0_attn_b_out, L0_ffn_ln1_g, L0_ffn_ln1_b, L0_ffn_Wqkv,
           L0_ffn_Wproj, L0_ffn_ln2_g, L0_ffn_ln2_b, L0_ffn_Wgate,
           L0_ffn_W1, L0_ffn_b1, L0_ffn_W2, L0_ffn_b2,
           L1_attn_W_qkv, L1_attn_b_qkv, L1_attn_W_out, L1_attn_b_out,
           L1_ffn_w_gate, L1_ffn_w_up, L1_ffn_w_down):
    B = 8
    asnp = np.asarray
    ident = np.eye(128, dtype=np.float32)
    cores = list(range(B))

    def run(name, in_maps):
        nc = _get_prog(name)
        return run_bass_kernel_spmd(nc, in_maps, core_ids=cores).results

    x0 = asnp(L0_attn_x)
    wqkv0 = asnp(L0_attn_w_qkv)
    bqkv0 = asnp(L0_attn_b_qkv)
    common0 = {
        "wqkv_T": np.ascontiguousarray(wqkv0.T),
        "bqkv_cols": np.ascontiguousarray(bqkv0.reshape(24, 128).T),
        "bv_row": np.ascontiguousarray(bqkv0[2048:3072].reshape(1, D)),
        "wout_T": np.ascontiguousarray(asnp(L0_attn_w_out).T),
        "bout_cols": np.ascontiguousarray(asnp(L0_attn_b_out).reshape(8, 128).T),
        "ident128": ident,
    }
    r1 = run("l0attn", [dict(common0, x_T=np.ascontiguousarray(x0[b].T))
                        for b in range(B)])
    x1 = [r1[b]["y_T"] for b in range(B)]

    wq_i, wk_i, wv_i = _perm_inner_qkv(asnp(L0_ffn_Wqkv))
    C, S = _rope_tables()
    maskt = np.where(np.arange(128)[:, None] > np.arange(128)[None, :],
                     np.float32(-1e9 * 8.0), np.float32(0.0))
    commonA1 = {
        "wq_i": wq_i, "wk_i": wk_i, "wv_i": wv_i,
        "wproj": np.ascontiguousarray(asnp(L0_ffn_Wproj)),
        "g1_row": asnp(L0_ffn_ln1_g).reshape(1, D),
        "b1_row": asnp(L0_ffn_ln1_b).reshape(1, D),
        "cos_tab": C, "sin_tab": S, "ident128": ident, "maskt": maskt,
    }
    r2 = run("a1", [dict(commonA1, x_T=x1[b]) for b in range(B)])
    xf = [r2[b]["xf_T"] for b in range(B)]

    commonA2 = {
        "g2_row": asnp(L0_ffn_ln2_g).reshape(1, D),
        "b2_row": asnp(L0_ffn_ln2_b).reshape(1, D),
        "wgate": np.ascontiguousarray(asnp(L0_ffn_Wgate)),
        "ident128": ident,
    }
    r3 = run("a2", [dict(commonA2, xf_T=xf[b], x1_T=x1[b]) for b in range(B)])

    counts_all = np.ascontiguousarray(
        np.concatenate([r3[b]["counts_row"] for b in range(B)], axis=0))
    iota768 = np.broadcast_to(np.arange(CAP, dtype=np.float32),
                              (128, CAP)).copy()
    tki = np.ascontiguousarray(
        np.arange(T, dtype=np.int32).reshape(16, 128).T)
    commonB = {
        "counts_all": counts_all,
        "w1_all": np.ascontiguousarray(asnp(L0_ffn_W1)),
        "b1c_all": np.ascontiguousarray(
            asnp(L0_ffn_b1).reshape(8, 32, 128).transpose(0, 2, 1)),
        "w2_all": np.ascontiguousarray(asnp(L0_ffn_W2)),
        "b2r_all": np.ascontiguousarray(asnp(L0_ffn_b2).reshape(8, 1, D)),
        "iota768": iota768, "tokidx_cols": tki, "ident128": ident,
    }
    in4 = []
    for b in range(B):
        pref = np.zeros((8, 1), np.float32)
        pref[:b] = 1.0
        in4.append(dict(commonB, prefmask=pref,
                        h2tok=r3[b]["h2tok"], base_tok=r3[b]["base_tok"],
                        gates_sm=r3[b]["gates_sm"],
                        mask_tok=r3[b]["mask_tok"],
                        pos_tok=r3[b]["pos_tok"]))
    r4 = run("b", in4)
    x2 = [np.ascontiguousarray(r4[b]["y_tok"].T) for b in range(B)]

    wqkv1 = asnp(L1_attn_W_qkv)
    bqkv1 = asnp(L1_attn_b_qkv)
    common5 = {
        "wqkv_T": np.ascontiguousarray(wqkv1.T),
        "bqkv_cols": np.ascontiguousarray(bqkv1.reshape(24, 128).T),
        "bv_row": np.ascontiguousarray(bqkv1[2048:3072].reshape(1, D)),
        "wout_T": np.ascontiguousarray(asnp(L1_attn_W_out).T),
        "bout_cols": np.ascontiguousarray(asnp(L1_attn_b_out).reshape(8, 128).T),
        "ident128": ident,
    }
    r5 = run("l1attn", [dict(common5, x_T=x2[b]) for b in range(B)])
    x3 = [r5[b]["y_T"] for b in range(B)]

    common6 = {
        "wg_T": np.ascontiguousarray(asnp(L1_ffn_w_gate).T),
        "wu_T": np.ascontiguousarray(asnp(L1_ffn_w_up).T),
        "wd_T": np.ascontiguousarray(asnp(L1_ffn_w_down).T),
    }
    r6 = run("l1ffn", [dict(common6, x_T=x3[b]) for b in range(B)])

    out = np.stack([r6[b]["y_T"].T for b in range(B)], axis=0)
    return out.astype(np.float32)


# version 08
# speedup vs baseline: 1.0033x; 1.0033x over previous
import sys as _sys
for _p in ("/opt/trn_rl_repo",):
    if _p not in _sys.path:
        _sys.path.insert(0, _p)
"""Shared raw-Bass builder helpers for the mk_seq_preln_d2 kernel.

Conventions:
- Residual stream and most activations are FEATURE-major: [d (128-chunks on
  partitions) x tokens (free)].
- Weights arrive in HBM pre-transposed to [d_in, d_out] fp32/int32 (host
  layout only); device casts matmul operands to bf16.
- Phase-A sync style: sequential nc.Block() phases (Block exit = all-engine
  barrier) + explicit semaphores inside a phase where pipelining needs them.
- PSUM pool (8 banks total), shared by every helper:
    psA, psB      [128, 512]  (1 bank each)
    psX, psY      [128, 1024] (2 banks each)
    psR1, psR2    [1, 512]    (1 bank each)
"""
import contextlib
import numpy as np
import concourse.bass as bass
import concourse.mybir as mybir
from concourse.bass import IndirectOffsetOnAxis
from concourse.bass_utils import run_bass_kernel_spmd

F32 = mybir.dt.float32
BF16 = mybir.dt.bfloat16
I32 = mybir.dt.int32
U8 = mybir.dt.uint8
AF = mybir.ActivationFunctionType
ALU = mybir.AluOpType

T = 2048
D = 1024
NCHUNK = D // 128

LN_EPS = 1e-5


class Ctx:
    def __init__(self, nc, es):
        self.nc = nc
        self.es = es
        self.n_sems = 0
        self._named = {}
        self.semc = {}

    def sbuf(self, name, shape, dtype):
        return self.es.enter_context(self.nc.sbuf_tensor(name, shape, dtype))

    def psum(self, name, shape, dtype=F32):
        return self.es.enter_context(self.nc.psum_tensor(name, shape, dtype))

    def sem(self, name=None):
        self.n_sems += 1
        return self.es.enter_context(
            self.nc.semaphore(f"{name or 's'}{self.n_sems}"))

    def named_sem(self, tag):
        if tag not in self._named:
            self._named[tag] = self.es.enter_context(self.nc.semaphore(tag))
            self.semc[tag] = 0
        return self._named[tag]


def make_ctx(dma_scratch=16384):
    nc = bass.Bass("TRN2", target_bir_lowering=False,
                   dynamic_dma_scratch_size=dma_scratch)
    es = contextlib.ExitStack()
    return Ctx(nc, es)


def load_consts(ctx, pool=True):
    nc = ctx.nc
    ctx.ones_col = ctx.sbuf("ones_col", [1, 128], F32)
    ctx.ones_col_bf = ctx.sbuf("ones_col_bf", [1, 128], BF16)
    ctx.ones128 = ctx.sbuf("ones128", [128, 1], F32)
    ctx.ones128_bf = ctx.sbuf("ones128_bf", [128, 1], BF16)
    ctx.eps_col = ctx.sbuf("eps_col", [128, 1], F32)
    if pool:
        ctx.psA = ctx.psum("psA", [128, 512])
        ctx.psB = ctx.psum("psB", [128, 512])
        ctx.psX = ctx.psum("psX", [128, 1024])
        ctx.psY = ctx.psum("psY", [128, 1024])
        ctx.psR1 = ctx.psum("psR1", [1, 512])
        ctx.psR2 = ctx.psum("psR2", [1, 512])
    with nc.Block() as block:
        @block.gpsimd
        def _(g):
            g.memset(ctx.ones_col[:], 1.0)
            g.memset(ctx.ones_col_bf[:], 1.0)
            g.memset(ctx.ones128[:], 1.0)
            g.memset(ctx.ones128_bf[:], 1.0)
            g.memset(ctx.eps_col[:], LN_EPS)


def dma_load(ctx, pairs, engines=("sync", "gpsimd", "scalar")):
    nc = ctx.nc
    per = {}
    for i, (dst, src) in enumerate(pairs):
        per.setdefault(engines[i % len(engines)], []).append((dst, src))
    waits = []
    with nc.Block() as block:
        def mk(items, sem, base):
            def f(eng):
                for dst, src in items:
                    eng.dma_start(out=dst, in_=src).then_inc(sem, 16)
            return f
        for eng_name, items in per.items():
            tag = "dma_ld_sw" if eng_name == "gpsimd" else f"dma_ld_{eng_name}"
            sem = ctx.named_sem(tag)
            base = ctx.semc[tag]
            ctx.semc[tag] = base + 16 * len(items)
            waits.append((sem, ctx.semc[tag]))
            getattr(block, eng_name)(mk(items, sem, base))
    with nc.Block() as block:
        @block.sync
        def _(s):
            for sem, val in waits:
                s.wait_ge(sem, val)


def dma_store(ctx, pairs):
    nc = ctx.nc
    sem = ctx.named_sem("dma_st")
    base = ctx.semc["dma_st"]
    n = len(pairs)
    ctx.semc["dma_st"] = base + 16 * n
    with nc.Block() as block:
        @block.gpsimd
        def _(g):
            for dst, src in pairs:
                g.dma_start(out=dst, in_=src).then_inc(sem, 16)
            g.wait_ge(sem, base + 16 * n)


def free_slices(Tn, step=512):
    return [slice(f, min(f + step, Tn)) for f in range(0, Tn, step)]


def emit_ln(ctx, x_chunks, out_chunks, aux, g_row=None, b_row=None,
            out2_chunks=None):
    nc = ctx.nc
    Tn = x_chunks[0].shape[-1]
    fsl = free_slices(Tn)
    scr = aux["scr"]
    mean, r, mr = aux["mean"], aux["r"], aux["mr"]

    for i, sl in enumerate(fsl):
        sqd = ctx.named_sem("ln_sq")
        mmd = ctx.named_sem("ln_mm")
        sqb = ctx.semc["ln_sq"]
        mmb = ctx.semc["ln_mm"]
        ctx.semc["ln_sq"] = sqb + NCHUNK
        ctx.semc["ln_mm"] = mmb + NCHUNK
        with nc.Block() as block:
            @block.scalar
            def _(s):
                for c in range(NCHUNK):
                    if c >= 2:
                        s.wait_ge(mmd, mmb + c - 1)
                    s.activation(scr[c % 2][:], x_chunks[c][:, sl], AF.Square)\
                        .then_inc(sqd, 1)

            @block.tensor
            def _(t):
                for c in range(NCHUNK):
                    t.matmul(ctx.psR1[0:1, :], ctx.ones128[:], x_chunks[c][:, sl],
                             start=(c == 0), stop=(c == NCHUNK - 1))
                    t.wait_ge(sqd, sqb + c + 1)
                    t.matmul(ctx.psR2[0:1, :], ctx.ones128[:], scr[c % 2][:],
                             start=(c == 0), stop=(c == NCHUNK - 1))\
                        .then_inc(mmd, 1)
        with nc.Block() as block:
            @block.scalar
            def _(s):
                s.activation(mean[0:1, sl], ctx.psR1[0:1, :], AF.Copy,
                             scale=1.0 / D)
        with nc.Block() as block:
            @block.scalar
            def _(s):
                s.activation(scr[0][0:1, 0:len_sl(sl)], mean[0:1, sl], AF.Square)
        with nc.Block() as block:
            @block.vector
            def _(v):
                v.scalar_tensor_tensor(scr[1][0:1, 0:len_sl(sl)],
                                       ctx.psR2[0:1, 0:len_sl(sl)], 1.0 / D,
                                       scr[0][0:1, 0:len_sl(sl)],
                                       op0=ALU.mult, op1=ALU.subtract)
        with nc.Block() as block:
            @block.scalar
            def _(s):
                s.activation(scr[0][0:1, 0:len_sl(sl)],
                             scr[1][0:1, 0:len_sl(sl)], AF.Ln, bias=ctx.eps_col[0:1, :])
        with nc.Block() as block:
            @block.scalar
            def _(s):
                s.activation(r[0:1, sl], scr[0][0:1, 0:len_sl(sl)], AF.Exp,
                             scale=-0.5)
    if g_row is not None:
        with nc.Block() as block:
            @block.vector
            def _(v):
                v.tensor_mul(mr[:], mean[0:1, :], r[0:1, :])

    for i, sl in enumerate(fsl):
        n = len_sl(sl)
        with nc.Block() as block:
            @block.tensor
            def _(t):
                if g_row is None:
                    t.matmul(ctx.psA[:, 0:n], ctx.ones_col[:], mean[0:1, sl],
                             start=True, stop=True)
                    t.matmul(ctx.psB[:, 0:n], ctx.ones_col[:], r[0:1, sl],
                             start=True, stop=True)
        for c in range(NCHUNK):
            if g_row is not None:
                gsl = g_row[0:1, c * 128:(c + 1) * 128]
                bsl = b_row[0:1, c * 128:(c + 1) * 128]
                with nc.Block() as block:
                    @block.tensor
                    def _(t):
                        t.matmul(ctx.psA[:, 0:n], gsl, r[0:1, sl],
                                 start=True, stop=True)
                        t.matmul(ctx.psB[:, 0:n], gsl, mr[0:1, sl],
                                 start=True, stop=False)
                        t.matmul(ctx.psB[:, 0:n], bsl,
                                 aux["negones"][0:1, 0:n],
                                 start=False, stop=True)
                with nc.Block() as block:
                    @block.vector
                    def _(v):
                        v.tensor_mul(scr[0][:, 0:n], x_chunks[c][:, sl],
                                     ctx.psA[:, 0:n])
                with nc.Block() as block:
                    @block.vector
                    def _(v):
                        v.tensor_sub(out_chunks[c][:, sl], scr[0][:, 0:n],
                                     ctx.psB[:, 0:n])
                        if out2_chunks is not None:
                            v.tensor_sub(out2_chunks[c][:, sl],
                                         scr[0][:, 0:n], ctx.psB[:, 0:n])
            else:
                with nc.Block() as block:
                    @block.vector
                    def _(v):
                        v.tensor_sub(scr[0][:, 0:n], x_chunks[c][:, sl],
                                     ctx.psA[:, 0:n])
                with nc.Block() as block:
                    @block.vector
                    def _(v):
                        v.tensor_mul(out_chunks[c][:, sl], scr[0][:, 0:n],
                                     ctx.psB[:, 0:n])
                        if out2_chunks is not None:
                            v.tensor_mul(out2_chunks[c][:, sl],
                                         scr[0][:, 0:n], ctx.psB[:, 0:n])


def len_sl(sl):
    return sl.stop - sl.start


def alloc_ln_aux(ctx, Tn, gb=False):
    aux = {}
    for k in ("mean", "r"):
        aux[k] = ctx.sbuf(f"ln_{k}", [1, Tn], F32)
    aux["scr"] = [ctx.sbuf(f"ln_scr{i}", [128, 512], F32) for i in range(2)]
    if gb:
        aux["mr"] = ctx.sbuf("ln_mr", [1, Tn], F32)
        aux["negones"] = ctx.sbuf("ln_negones", [1, 512], F32)
        nc = ctx.nc
        with nc.Block() as block:
            @block.gpsimd
            def _(g):
                g.memset(aux["negones"][:], -1.0)
    else:
        aux["mr"] = None
        aux["negones"] = None
    return aux

NHEADS = 16
NT = 16






def build_l0attn(ctx, x_dram, wqkv_dram, bqkv_cols_dram, bv_row_dram,
                 wout_dram, bout_cols_dram, ident_dram, y_dram):
    nc = ctx.nc
    A = [ctx.sbuf(f"A{c}", [128, T], F32) for c in range(NCHUNK)]
    q_sb = [ctx.sbuf(f"q{c}", [128, T], BF16) for c in range(NCHUNK)]
    k_sb = [ctx.sbuf(f"k{c}", [128, T], BF16) for c in range(NCHUNK)]
    v_sb = [ctx.sbuf(f"v{i}", [128, D], BF16) for i in range(NT)]
    ln_mean = ctx.sbuf("lnmean", [1, T], F32)
    ln_r = ctx.sbuf("lnr", [1, T], F32)
    lnscr = [ctx.sbuf(f"lnscr{i}", [128, 512], F32) for i in range(2)]
    bqkv_cols = ctx.sbuf("bqkv_cols_sb", [128, 24], F32)
    bout_cols = ctx.sbuf("bout_cols_sb", [128, 8], F32)
    bv_row = ctx.sbuf("bv_row_sb", [1, D], F32)
    bv_row_bf = ctx.sbuf("bv_row_bf", [1, D], BF16)
    ident = ctx.sbuf("ident_sb", [128, 128], F32)
    maxbuf = ctx.sbuf("maxbuf", [128, 16], F32)
    mrow = ctx.sbuf("mrow", [1, 130], F32)
    negM_col = ctx.sbuf("negM_col", [128, 1], F32)
    z_row = ctx.sbuf("z_row", [1, 1024], F32)
    invz_row = ctx.sbuf("invz_row", [1, 1024], F32)
    invzbc = ctx.sbuf("invzbc", [64, 1024], F32)

    aux = {"mean": ln_mean, "r": ln_r, "mr": None, "negones": None,
           "scr": lnscr}

    Abf = [a[:].bitcast(BF16) for a in A]
    Af32hi = [a[:, 1024:2048] for a in A]
    xln = [Abf[c][:, 0:2048] for c in range(NCHUNK)]
    attn = xln
    wst_f32 = Af32hi[0]
    wst_bf = Abf[1][:, 2048:3072]
    wv_bf = [Abf[4 + j][:, 2048:4096] for j in range(4)]
    p_k0 = Abf[2][:, 2048:4096]
    p_strA = Abf[3][:, 2048:4096]
    p_strB = Abf[0][:, 2048:4096]

    def p_strip(s):
        if s < 16:
            return p_strA[:, s * 128:(s + 1) * 128]
        return p_strB[:, (s - 16) * 128:(s - 15) * 128]

    def strip_kq(s):
        if s < 15:
            return (s + 1, s + 1)
        return (s - 14, 0)

    dma_load(ctx, [(A[c][:], x_dram[c * 128:(c + 1) * 128, :])
                   for c in range(NCHUNK)]
             + [(bqkv_cols[:], bqkv_cols_dram[:, :]),
                (bout_cols[:], bout_cols_dram[:, :]),
                (bv_row[:], bv_row_dram[:, :]),
                (ident[:], ident_dram[:, :])])
    with nc.Block() as block:
        @block.vector
        def _(v):
            v.tensor_copy(bv_row_bf[:], bv_row[:])

    emit_ln(ctx, [a[:] for a in A], xln, aux)

    wqkv_r = wqkv_dram.rearrange("(c p) n -> p c n", p=128)
    for qk in range(2):
        dst = q_sb if qk == 0 else k_sb
        for oc in range(NCHUNK):
            ocol = qk * 1024 + oc * 128
            dma_load(ctx, [(wst_f32.rearrange("p (c n) -> p c n", n=128),
                            wqkv_r[:, :, ocol:ocol + 128])])
            with nc.Block() as block:
                @block.vector
                def _(v):
                    v.tensor_copy(wst_bf, wst_f32)
            with nc.Block() as block:
                @block.tensor
                def _(t):
                    for dc in range(NCHUNK):
                        lhsT = wst_bf.rearrange("p (c n) -> p c n", n=128)[:, dc, :]
                        for f in range(4):
                            ps = [ctx.psX, ctx.psY][f // 2]
                            psl = slice((f % 2) * 512, (f % 2 + 1) * 512)
                            t.matmul(ps[:, psl], lhsT,
                                     xln[dc][:, f * 512:(f + 1) * 512],
                                     start=(dc == 0), stop=(dc == NCHUNK - 1))
            with nc.Block() as block:
                @block.scalar
                def _(s):
                    bcol = bqkv_cols[:, qk * 8 + oc:qk * 8 + oc + 1]
                    s.activation(dst[oc][:, 0:1024], ctx.psX[:], AF.Identity,
                                 bias=bcol)
                    s.activation(dst[oc][:, 1024:2048], ctx.psY[:], AF.Identity,
                                 bias=bcol)

    for dc in range(NCHUNK):
        dma_load(ctx, [(wst_f32, wqkv_r[:, dc, 2048:3072])])
        with nc.Block() as block:
            @block.vector
            def _(v):
                half = wv_bf[dc // 2]
                v.tensor_copy(half[:, (dc % 2) * 1024:(dc % 2 + 1) * 1024],
                              wst_f32)
    for i in range(NT):
        tsl = slice(i * 128, (i + 1) * 128)
        with nc.Block() as block:
            @block.tensor
            def _(t):
                for dc in range(NCHUNK):
                    wv_dc = wv_bf[dc // 2][:, (dc % 2) * 1024:(dc % 2 + 1) * 1024]
                    for f in range(2):
                        t.matmul(ctx.psX[:, f * 512:(f + 1) * 512],
                                 xln[dc][:, tsl],
                                 wv_dc[:, f * 512:(f + 1) * 512],
                                 start=(dc == 0), stop=False)
                for f in range(2):
                    t.matmul(ctx.psX[:, f * 512:(f + 1) * 512],
                             ctx.ones_col_bf[:],
                             bv_row_bf[0:1, f * 512:(f + 1) * 512],
                             start=False, stop=True)
        with nc.Block() as block:
            @block.scalar
            def _(s):
                s.copy(v_sb[i][:], ctx.psX[:])

    for h in range(NHEADS):
        ch, off = h // 2, (h % 2) * 64
        qh = q_sb[ch][off:off + 64, :]
        kh = k_sb[ch][off:off + 64, :]
        hsl = slice(h * 64, (h + 1) * 64)

        def emit_k0(t, f):
            ps = [ctx.psX, ctx.psY][f // 2]
            psl = slice((f % 2) * 512, (f % 2 + 1) * 512)
            t.matmul(ps[:, psl], kh[:, 0:128],
                     qh[:, f * 512:(f + 1) * 512], start=True, stop=True)

        def emit_batch(t, b):
            ps = [ctx.psA, ctx.psB][b % 2]
            for u in range(4):
                s_id = b * 4 + u
                if s_id >= 30:
                    break
                ki, qi = strip_kq(s_id)
                t.matmul(ps[:, u * 128:(u + 1) * 128],
                         kh[:, ki * 128:(ki + 1) * 128],
                         qh[:, qi * 128:(qi + 1) * 128],
                         start=True, stop=True)

        with nc.Block() as block:
            @block.tensor
            def _(t):
                for f in range(4):
                    emit_k0(t, f)
        with nc.Block() as block:
            @block.vector
            def _(v):
                for f in range(4):
                    ps = [ctx.psX, ctx.psY][f // 2]
                    psl = slice((f % 2) * 512, (f % 2 + 1) * 512)
                    v.reduce_max(maxbuf[:, f:f + 1], ps[:, psl],
                                 axis=mybir.AxisListType.X)
        for b in range(8):
            with nc.Block() as block:
                @block.tensor
                def _(t):
                    emit_batch(t, b)
            with nc.Block() as block:
                @block.vector
                def _(v):
                    ps = [ctx.psA, ctx.psB][b % 2]
                    n = min(4, 30 - b * 4) * 128
                    v.reduce_max(maxbuf[:, 4 + b:5 + b], ps[:, 0:n],
                                 axis=mybir.AxisListType.X)
        with nc.Block() as block:
            @block.vector
            def _(v):
                v.reduce_max(maxbuf[:, 12:13], maxbuf[:, 0:12],
                             axis=mybir.AxisListType.X)
        with nc.Block() as block:
            @block.tensor
            def _(t):
                t.matmul(ctx.psR1[0:1, 0:128], maxbuf[:, 12:13], ident[:],
                         start=True, stop=True)
        with nc.Block() as block:
            @block.vector
            def _(v):
                v.reduce_max(mrow[0:1, 128:129], ctx.psR1[0:1, 0:128],
                             axis=mybir.AxisListType.X)
        with nc.Block() as block:
            @block.scalar
            def _(s):
                s.activation(mrow[0:1, 129:130], mrow[0:1, 128:129],
                             AF.Copy, scale=-0.125)
        with nc.Block() as block:
            @block.tensor
            def _(t):
                t.matmul(ctx.psA[:, 0:1], ctx.ones_col[:], mrow[0:1, 129:130],
                         start=True, stop=True)
        with nc.Block() as block:
            @block.scalar
            def _(s):
                s.copy(negM_col[:], ctx.psA[:, 0:1])

        with nc.Block() as block:
            @block.tensor
            def _(t):
                for f in range(4):
                    emit_k0(t, f)
        with nc.Block() as block:
            @block.scalar
            def _(s):
                for f in range(4):
                    ps = [ctx.psX, ctx.psY][f // 2]
                    psl = slice((f % 2) * 512, (f % 2 + 1) * 512)
                    s.activation(p_k0[:, f * 512:(f + 1) * 512], ps[:, psl],
                                 AF.Exp, bias=negM_col[:], scale=0.125)
        for b in range(8):
            with nc.Block() as block:
                @block.tensor
                def _(t):
                    emit_batch(t, b)
            with nc.Block() as block:
                @block.scalar
                def _(s):
                    ps = [ctx.psA, ctx.psB][b % 2]
                    for u in range(min(4, 30 - b * 4)):
                        s.activation(p_strip(b * 4 + u),
                                     ps[:, u * 128:(u + 1) * 128],
                                     AF.Exp, bias=negM_col[:], scale=0.125)

        for half in range(2):
            for grp in range(2):
                with nc.Block() as block:
                    @block.tensor
                    def _(t):
                        for u in range(4):
                            qi = half * 8 + grp * 4 + u
                            dst = ctx.psR1[0:1, u * 128:(u + 1) * 128]
                            t.matmul(dst, ctx.ones128_bf[:],
                                     p_k0[:, qi * 128:(qi + 1) * 128],
                                     start=True, stop=False)
                            if qi >= 1:
                                t.matmul(dst, ctx.ones128_bf[:],
                                         p_strip(qi - 1),
                                         start=False, stop=True)
                            else:
                                for j in range(15):
                                    t.matmul(dst, ctx.ones128_bf[:],
                                             p_strip(15 + j),
                                             start=False, stop=(j == 14))
                with nc.Block() as block:
                    @block.scalar
                    def _(s):
                        s.copy(z_row[0:1, grp * 512:(grp + 1) * 512],
                               ctx.psR1[0:1, :])
            with nc.Block() as block:
                @block.vector
                def _(v):
                    v.reciprocal(invz_row[:], z_row[:])
            for f in range(2):
                with nc.Block() as block:
                    @block.tensor
                    def _(t):
                        t.matmul(ctx.psA[0:64, 0:512], ctx.ones_col[0:1, 0:64],
                                 invz_row[0:1, f * 512:(f + 1) * 512],
                                 start=True, stop=True)
                with nc.Block() as block:
                    @block.scalar
                    def _(s):
                        s.copy(invzbc[:, f * 512:(f + 1) * 512],
                               ctx.psA[0:64, 0:512])
            with nc.Block() as block:
                @block.tensor
                def _(t):
                    for u in range(8):
                        qi = half * 8 + u
                        ps = ctx.psX
                        dst = ps[0:64, u * 128:(u + 1) * 128]
                        t.matmul(dst, v_sb[0][:, hsl],
                                 p_k0[:, qi * 128:(qi + 1) * 128],
                                 start=True, stop=False)
                        if qi >= 1:
                            t.matmul(dst, v_sb[qi][:, hsl], p_strip(qi - 1),
                                     start=False, stop=True)
                        else:
                            for j in range(15):
                                t.matmul(dst, v_sb[1 + j][:, hsl],
                                         p_strip(15 + j),
                                         start=False, stop=(j == 14))
            with nc.Block() as block:
                @block.vector
                def _(v):
                    v.tensor_mul(attn[ch][off:off + 64,
                                          half * 1024:(half + 1) * 1024],
                                 ctx.psX[0:64, :], invzbc[:, :])

    wo_f32 = q_sb[1][:].bitcast(F32)[:, 0:1024]
    wo_bf = q_sb[2][:, 0:1024]
    xre = q_sb[0][:].bitcast(F32)[:, 0:1024]
    yst = k_sb[0][:].bitcast(F32)[:, 0:1024]
    wout_r = wout_dram.rearrange("(c p) n -> p c n", p=128)
    for oc in range(NCHUNK):
        dma_load(ctx, [(wo_f32.rearrange("p (c n) -> p c n", n=128),
                        wout_r[:, :, oc * 128:(oc + 1) * 128])])
        with nc.Block() as block:
            @block.vector
            def _(v):
                v.tensor_copy(wo_bf, wo_f32)
        with nc.Block() as block:
            @block.tensor
            def _(t):
                for dc in range(NCHUNK):
                    lhsT = wo_bf.rearrange("p (c n) -> p c n", n=128)[:, dc, :]
                    for f in range(4):
                        ps = [ctx.psX, ctx.psY][f // 2]
                        psl = slice((f % 2) * 512, (f % 2 + 1) * 512)
                        t.matmul(ps[:, psl], lhsT,
                                 attn[dc][:, f * 512:(f + 1) * 512],
                                 start=(dc == 0), stop=(dc == NCHUNK - 1))
        for half in range(2):
            tsl = slice(half * 1024, (half + 1) * 1024)
            dma_load(ctx, [(xre, x_dram[oc * 128:(oc + 1) * 128, tsl])])
            with nc.Block() as block:
                @block.vector
                def _(v):
                    bcol = bout_cols[:, oc:oc + 1]
                    v.scalar_tensor_tensor(
                        yst, [ctx.psX, ctx.psY][half][:], bcol, xre,
                        op0=ALU.add, op1=ALU.add)
            dma_store(ctx, [(y_dram[oc * 128:(oc + 1) * 128, tsl], yst)])






SCL = 0.125
CAP = 768
NF = 32
BIGI = 1.0e5
NEGBIG = -1e9


def _pieces512(a, b):
    out = []
    while a < b:
        c = min(b, (a // 512 + 1) * 512)
        out.append((a, c))
        a = c
    return out


def zav_regions(kc, h):
    qs = max(0, (kc - 8 * h) * 128)
    out = []
    if kc >= 8 * h:
        out.append((slice(qs, qs + 128), True))
        if qs + 128 < 1024:
            out.append((slice(qs + 128, 1024), False))
    else:
        out.append((slice(0, 1024), False))
    return out


def build_a1(ctx, x_dram, wq_dram, wk_dram, wv_dram, wproj_dram,
             g1_dram, b1_dram, cos_dram, sin_dram, ident_dram, maskt_dram,
             xf_dram):
    nc = ctx.nc
    xs = [ctx.sbuf(f"xs{i}", [128, 1024], F32) for i in range(2)]
    h1 = [ctx.sbuf(f"h1_{c}", [128, T], BF16) for c in range(NCHUNK)]
    attn = h1
    q_sb = [ctx.sbuf(f"q{c}", [128, T], BF16) for c in range(NCHUNK)]
    k_sb = [ctx.sbuf(f"k{c}", [128, T], BF16) for c in range(NCHUNK)]
    v_sb = [ctx.sbuf(f"v{i}", [128, D], BF16) for i in range(NT)]
    m0 = ctx.sbuf("m0", [1, T], F32)
    r0 = ctx.sbuf("r0", [1, T], F32)
    r01 = ctx.sbuf("r01", [1, T], F32)
    scr = [ctx.sbuf(f"scr{i}", [128, 512], F32) for i in range(2)]
    g1_row = ctx.sbuf("g1_row_sb", [1, D], F32)
    b1_row = ctx.sbuf("b1_row_sb", [1, D], F32)
    negones = ctx.sbuf("negones", [1, 512], F32)
    ident = ctx.sbuf("ident_sb", [128, 128], F32)
    ident_bf = ctx.sbuf("ident_bf", [128, 128], BF16)
    maskt = ctx.sbuf("maskt_sb", [128, 128], BF16)
    ctab = ctx.sbuf("ctab", [128, 1024], F32)
    stab = ctx.sbuf("stab", [128, 1024], F32)
    swb = ctx.sbuf("swb", [128, 1024], BF16)
    t1b = ctx.sbuf("t1b", [128, 1024], BF16)
    wst_f32 = ctx.sbuf("wst_f32", [128, 1024], F32)
    wst_bf = ctx.sbuf("wst_bf", [128, 1024], BF16)
    wvh_bf = ctx.sbuf("wvh_bf", [128, 4096], BF16)
    pbuf = ctx.sbuf("pbuf", [128, 1024], BF16)
    maxbuf = ctx.sbuf("maxbuf", [128, 16], F32)
    mrow = ctx.sbuf("mrow", [1, 130], F32)
    negM_col = ctx.sbuf("negM_col", [128, 1], F32)

    t2b = pbuf
    z_row = stab[0:1, 0:1024]
    invz_row = wst_f32[0:1, 0:1024]
    invzbc = wvh_bf[:].bitcast(F32)[0:64, 0:1024]
    xf_st = ctab
    fsl = free_slices(T)

    dma_load(ctx, [(g1_row[:], g1_dram[:, :]), (b1_row[:], b1_dram[:, :]),
                   (ident[:], ident_dram[:, :]),
                   (wst_f32[:, 0:128], maskt_dram[:, :])])
    with nc.Block() as block:
        @block.gpsimd
        def _(g):
            g.memset(negones[:], -1.0)
    with nc.Block() as block:
        @block.vector
        def _(v):
            v.tensor_copy(ident_bf[:], ident[:])
            v.tensor_copy(maskt[:], wst_f32[:, 0:128])

    for i, sl in enumerate(fsl):
        sqd = ctx.named_sem("ln_sq"); mmd = ctx.named_sem("ln_mm")
        sqb = ctx.semc["ln_sq"]; mmb = ctx.semc["ln_mm"]
        ctx.semc["ln_sq"] = sqb + NCHUNK
        ctx.semc["ln_mm"] = mmb + NCHUNK
        lds = [ctx.named_sem(f"xstr{c}") for c in range(NCHUNK)]
        ldb = [ctx.semc[f"xstr{c}"] for c in range(NCHUNK)]
        for c in range(NCHUNK):
            ctx.semc[f"xstr{c}"] = ldb[c] + 16
        with nc.Block() as block:
            @block.sync
            def _(sy):
                for c in range(NCHUNK):
                    if c >= 2:
                        sy.wait_ge(mmd, mmb + c - 1)
                    sy.dma_start(out=xs[c % 2][:, 0:512],
                                 in_=x_dram[c * 128:(c + 1) * 128, sl])\
                        .then_inc(lds[c], 16)

            @block.scalar
            def _(s):
                for c in range(NCHUNK):
                    s.wait_ge(lds[c], ldb[c] + 16)
                    if c >= 2:
                        s.wait_ge(mmd, mmb + c - 1)
                    s.activation(scr[c % 2][:], xs[c % 2][:, 0:512], AF.Square)\
                        .then_inc(sqd, 1)

            @block.tensor
            def _(t):
                for c in range(NCHUNK):
                    t.wait_ge(sqd, sqb + c + 1)
                    t.matmul(ctx.psR1[0:1, :], ctx.ones128[:],
                             xs[c % 2][:, 0:512],
                             start=(c == 0), stop=(c == NCHUNK - 1))
                    t.matmul(ctx.psR2[0:1, :], ctx.ones128[:], scr[c % 2][:],
                             start=(c == 0), stop=(c == NCHUNK - 1))\
                        .then_inc(mmd, 1)
        with nc.Block() as block:
            @block.scalar
            def _(s):
                s.activation(m0[0:1, sl], ctx.psR1[0:1, :], AF.Copy,
                             scale=1.0 / D)
        with nc.Block() as block:
            @block.scalar
            def _(s):
                s.activation(scr[0][0:1, 0:512], m0[0:1, sl], AF.Square)
        with nc.Block() as block:
            @block.vector
            def _(v):
                v.scalar_tensor_tensor(scr[1][0:1, 0:512],
                                       ctx.psR2[0:1, 0:512], 1.0 / D,
                                       scr[0][0:1, 0:512],
                                       op0=ALU.mult, op1=ALU.subtract)
        with nc.Block() as block:
            @block.scalar
            def _(s):
                s.activation(scr[0][0:1, 0:512], scr[1][0:1, 0:512],
                             AF.Ln, bias=ctx.eps_col[0:1, :])
        with nc.Block() as block:
            @block.scalar
            def _(s):
                s.activation(r0[0:1, sl], scr[0][0:1, 0:512], AF.Exp,
                             scale=-0.5)
        with nc.Block() as block:
            @block.vector
            def _(v):
                v.tensor_mul(scr[0][0:1, 0:512], scr[1][0:1, 0:512],
                             r0[0:1, sl])
        with nc.Block() as block:
            @block.vector
            def _(v):
                v.tensor_mul(scr[1][0:1, 0:512], scr[0][0:1, 0:512],
                             r0[0:1, sl])
        with nc.Block() as block:
            @block.scalar
            def _(s):
                s.activation(scr[0][0:1, 0:512], scr[1][0:1, 0:512],
                             AF.Ln, bias=ctx.eps_col[0:1, :])
        with nc.Block() as block:
            @block.scalar
            def _(s):
                s.activation(scr[1][0:1, 0:512], scr[0][0:1, 0:512], AF.Exp,
                             scale=-0.5)
        with nc.Block() as block:
            @block.vector
            def _(v):
                v.tensor_mul(r01[0:1, sl], scr[1][0:1, 0:512], r0[0:1, sl])

    for i, sl in enumerate(fsl):
        with nc.Block() as block:
            @block.vector
            def _(v):
                v.tensor_mul(scr[1][0:1, 0:512], m0[0:1, sl], r01[0:1, sl])
        for c in range(NCHUNK):
            gsl = g1_row[0:1, c * 128:(c + 1) * 128]
            bsl = b1_row[0:1, c * 128:(c + 1) * 128]
            dma_load(ctx, [(xs[0][:, 0:512], x_dram[c * 128:(c + 1) * 128, sl])])
            with nc.Block() as block:
                @block.tensor
                def _(t):
                    t.matmul(ctx.psA[:, :], gsl, r01[0:1, sl],
                             start=True, stop=True)
                    t.matmul(ctx.psB[:, :], gsl, scr[1][0:1, 0:512],
                             start=True, stop=False)
                    t.matmul(ctx.psB[:, :], bsl, negones[0:1, :],
                             start=False, stop=True)
            with nc.Block() as block:
                @block.vector
                def _(v):
                    v.tensor_mul(scr[0][:], xs[0][:, 0:512], ctx.psA[:, :])
            with nc.Block() as block:
                @block.vector
                def _(v):
                    v.tensor_sub(h1[c][:, sl], scr[0][:], ctx.psB[:, :])

    for which, wd, dst in ((0, wq_dram, q_sb), (1, wk_dram, k_sb)):
        w_r = wd.rearrange("(c p) n -> p c n", p=128)
        for oc in range(NCHUNK):
            dma_load(ctx, [(wst_f32[:].rearrange("p (c n) -> p c n", n=128),
                            w_r[:, :, oc * 128:(oc + 1) * 128])])
            with nc.Block() as block:
                @block.vector
                def _(v):
                    v.tensor_copy(wst_bf[:], wst_f32[:])
            with nc.Block() as block:
                @block.tensor
                def _(t):
                    for dc in range(NCHUNK):
                        lhsT = wst_bf[:].rearrange("p (c n) -> p c n", n=128)[:, dc, :]
                        for f in range(4):
                            ps = [ctx.psX, ctx.psY][f // 2]
                            psl = slice((f % 2) * 512, (f % 2 + 1) * 512)
                            t.matmul(ps[:, psl], lhsT,
                                     h1[dc][:, f * 512:(f + 1) * 512],
                                     start=(dc == 0), stop=(dc == NCHUNK - 1))
            with nc.Block() as block:
                @block.scalar
                def _(s):
                    s.copy(dst[oc][:, 0:1024], ctx.psX[:])
                    s.copy(dst[oc][:, 1024:2048], ctx.psY[:])
    wv_r = wv_dram.rearrange("(c p) n -> p c n", p=128)
    for hh in range(2):
        csl = slice(hh * 512, (hh + 1) * 512)
        for dc in range(NCHUNK):
            dma_load(ctx, [(wst_f32[:, 0:512], wv_r[:, dc, csl])])
            with nc.Block() as block:
                @block.vector
                def _(v):
                    v.tensor_copy(wvh_bf[:, dc * 512:(dc + 1) * 512],
                                  wst_f32[:, 0:512])
        for i in range(NT):
            tsl = slice(i * 128, (i + 1) * 128)
            with nc.Block() as block:
                @block.tensor
                def _(t):
                    for dc in range(NCHUNK):
                        t.matmul(ctx.psX[:, 0:512],
                                 h1[dc][:, tsl],
                                 wvh_bf[:, dc * 512:(dc + 1) * 512],
                                 start=(dc == 0), stop=(dc == NCHUNK - 1))
            with nc.Block() as block:
                @block.scalar
                def _(s):
                    s.copy(v_sb[i][:, csl], ctx.psX[:, 0:512])

    SHUF = []
    for j in range(16):
        SHUF += [2 * j + 1, 2 * j]
    for half in range(2):
        tsl = slice(half * 1024, (half + 1) * 1024)
        dma_load(ctx, [(ctab[:], cos_dram[:, tsl]), (stab[:], sin_dram[:, tsl])])
        for arr in (q_sb, k_sb):
            for c in range(NCHUNK):
                with nc.Block() as block:
                    @block.vector
                    def _(v):
                        v.stream_shuffle(swb[:], arr[c][:, tsl], SHUF)
                with nc.Block() as block:
                    @block.vector
                    def _(v):
                        v.tensor_mul(t1b[:], arr[c][:, tsl], ctab[:])
                with nc.Block() as block:
                    @block.vector
                    def _(v):
                        v.tensor_mul(t2b[:], swb[:], stab[:])
                with nc.Block() as block:
                    @block.vector
                    def _(v):
                        v.tensor_add(arr[c][:, tsl], t1b[:], t2b[:])

    for h in range(NHEADS):
        ch, off = h // 2, (h % 2) * 64
        qh = q_sb[ch][off:off + 64, :]
        kh = k_sb[ch][off:off + 64, :]
        hsl = slice(h * 64, (h + 1) * 64)
        for half in range(2):
            q0 = half * 1024
            nkc = 8 * half + 8
            for kc in range(nkc):
                qs = max(0, (kc - 8 * half) * 128)
                with nc.Block() as block:
                    @block.tensor
                    def _(t):
                        for f0, f1 in _pieces512(qs, 1024):
                            t.matmul(ctx.psX[:, f0:f1],
                                     kh[:, kc * 128:(kc + 1) * 128],
                                     qh[:, q0 + f0:q0 + f1],
                                     start=True, stop=True)
                with nc.Block() as block:
                    @block.vector
                    def _(v):
                        v.reduce_max(maxbuf[:, kc:kc + 1], ctx.psX[:, qs:1024],
                                     axis=mybir.AxisListType.X)
            with nc.Block() as block:
                @block.vector
                def _(v):
                    v.reduce_max(maxbuf[:, 0:1], maxbuf[:, 0:nkc],
                                 axis=mybir.AxisListType.X)
            with nc.Block() as block:
                @block.tensor
                def _(t):
                    t.matmul(ctx.psR1[0:1, 0:128], maxbuf[:, 0:1], ident[:],
                             start=True, stop=True)
            with nc.Block() as block:
                @block.vector
                def _(v):
                    v.reduce_max(mrow[0:1, 128:129], ctx.psR1[0:1, 0:128],
                                 axis=mybir.AxisListType.X)
            with nc.Block() as block:
                @block.scalar
                def _(s):
                    s.activation(mrow[0:1, 129:130], mrow[0:1, 128:129],
                                 AF.Copy, scale=-SCL)
            with nc.Block() as block:
                @block.tensor
                def _(t):
                    t.matmul(ctx.psA[:, 0:1], ctx.ones_col[:],
                             mrow[0:1, 129:130], start=True, stop=True)
            with nc.Block() as block:
                @block.scalar
                def _(s):
                    s.copy(negM_col[:], ctx.psA[:, 0:1])
            for kc in range(nkc):
                qs = max(0, (kc - 8 * half) * 128)
                with nc.Block() as block:
                    @block.tensor
                    def _(t):
                        if kc >= 8 * half:
                            t.matmul(ctx.psX[:, qs:qs + 128], ident_bf[:],
                                     maskt[:], start=True, stop=False)
                            t.matmul(ctx.psX[:, qs:qs + 128],
                                     kh[:, kc * 128:(kc + 1) * 128],
                                     qh[:, q0 + qs:q0 + qs + 128],
                                     start=False, stop=True)
                            rest = qs + 128
                        else:
                            rest = qs
                        for f0, f1 in _pieces512(rest, 1024):
                            t.matmul(ctx.psX[:, f0:f1],
                                     kh[:, kc * 128:(kc + 1) * 128],
                                     qh[:, q0 + f0:q0 + f1],
                                     start=True, stop=True)
                with nc.Block() as block:
                    @block.scalar
                    def _(s):
                        s.activation(pbuf[:, qs:1024], ctx.psX[:, qs:1024],
                                     AF.Exp, bias=negM_col[:], scale=SCL)
                with nc.Block() as block:
                    @block.tensor
                    def _(t):
                        if kc == 0:
                            for b0 in range(2):
                                psr = [ctx.psR1, ctx.psR2][b0]
                                t.matmul(psr[0:1, 0:512], ctx.ones128_bf[:],
                                         pbuf[:, b0 * 512:(b0 + 1) * 512],
                                         start=True,
                                         stop=(8 * half + 3 + 4 * b0 == 0))
                            for b0 in range(2):
                                t.matmul(ctx.psY[0:64, b0 * 512:(b0 + 1) * 512],
                                         v_sb[kc][:, hsl],
                                         pbuf[:, b0 * 512:(b0 + 1) * 512],
                                         start=True, stop=False)
                        else:
                            for tq in range(qs // 128, 8):
                                f0, f1 = tq * 128, (tq + 1) * 128
                                stop = (kc == 8 * half + 3 and tq == 3) or \
                                       (kc == 8 * half + 7 and tq == 7)
                                psr = ctx.psR1 if f0 < 512 else ctx.psR2
                                o0 = f0 % 512
                                t.matmul(psr[0:1, o0:o0 + 128],
                                         ctx.ones128_bf[:], pbuf[:, f0:f1],
                                         start=False, stop=stop)
                            for tq in range(qs // 128, 8):
                                f0, f1 = tq * 128, (tq + 1) * 128
                                stop = (kc == 8 * half + 3 and tq == 3) or \
                                       (kc == 8 * half + 7 and tq == 7)
                                t.matmul(ctx.psY[0:64, f0:f1],
                                         v_sb[kc][:, hsl], pbuf[:, f0:f1],
                                         start=False, stop=stop)
            with nc.Block() as block:
                @block.scalar
                def _(s):
                    s.copy(z_row[0:1, 0:512], ctx.psR1[0:1, :])
                    s.copy(z_row[0:1, 512:1024], ctx.psR2[0:1, :])
            with nc.Block() as block:
                @block.vector
                def _(v):
                    v.reciprocal(invz_row[:], z_row[:])
            for f in range(2):
                with nc.Block() as block:
                    @block.tensor
                    def _(t):
                        t.matmul(ctx.psA[0:64, 0:512], ctx.ones_col[0:1, 0:64],
                                 invz_row[0:1, f * 512:(f + 1) * 512],
                                 start=True, stop=True)
                with nc.Block() as block:
                    @block.scalar
                    def _(s):
                        s.copy(invzbc[:, f * 512:(f + 1) * 512],
                               ctx.psA[0:64, 0:512])
            with nc.Block() as block:
                @block.vector
                def _(v):
                    v.tensor_mul(attn[ch][off:off + 64, q0:q0 + 1024],
                                 ctx.psY[0:64, :], invzbc[:, :])

    wp_r = wproj_dram.rearrange("(c p) n -> p c n", p=128)
    for oc in range(NCHUNK):
        dma_load(ctx, [(wst_f32[:].rearrange("p (c n) -> p c n", n=128),
                        wp_r[:, :, oc * 128:(oc + 1) * 128])])
        with nc.Block() as block:
            @block.vector
            def _(v):
                v.tensor_copy(wst_bf[:], wst_f32[:])
        with nc.Block() as block:
            @block.tensor
            def _(t):
                for dc in range(NCHUNK):
                    lhsT = wst_bf[:].rearrange("p (c n) -> p c n", n=128)[:, dc, :]
                    for f in range(4):
                        ps = [ctx.psX, ctx.psY][f // 2]
                        psl = slice((f % 2) * 512, (f % 2 + 1) * 512)
                        t.matmul(ps[:, psl], lhsT,
                                 attn[dc][:, f * 512:(f + 1) * 512],
                                 start=(dc == 0), stop=(dc == NCHUNK - 1))
        for half in range(2):
            tsl = slice(half * 1024, (half + 1) * 1024)
            dma_load(ctx, [(xs[0][:, 0:1024], x_dram[oc * 128:(oc + 1) * 128, tsl])])
            for f in range(2):
                sl = slice(half * 1024 + f * 512, half * 1024 + (f + 1) * 512)
                lsl = slice(f * 512, (f + 1) * 512)
                with nc.Block() as block:
                    @block.tensor
                    def _(t):
                        t.matmul(ctx.psA[:, :], ctx.ones_col[:], r0[0:1, sl],
                                 start=True, stop=True)
                    @block.vector
                    def _(v):
                        v.tensor_mul(scr[1][0:1, :], m0[0:1, sl], r0[0:1, sl])
                with nc.Block() as block:
                    @block.tensor
                    def _(t):
                        t.matmul(ctx.psB[:, :], ctx.ones_col[:],
                                 scr[1][0:1, 0:512], start=True, stop=True)
                with nc.Block() as block:
                    @block.vector
                    def _(v):
                        v.tensor_mul(scr[0][:], xs[0][:, lsl], ctx.psA[:, :])
                with nc.Block() as block:
                    @block.vector
                    def _(v):
                        v.tensor_sub(xf_st[:, lsl], scr[0][:], ctx.psB[:, :])
            with nc.Block() as block:
                @block.vector
                def _(v):
                    v.tensor_add(xf_st[:, 0:1024], xf_st[:, 0:1024],
                                 [ctx.psX, ctx.psY][half][:])
            dma_store(ctx, [(xf_dram[oc * 128:(oc + 1) * 128, tsl],
                             xf_st[:, 0:1024])])



def host_prep(Wqkv, ln1_g, ln1_b):
    Dm, H, hd = 1024, 16, 64
    perm = np.empty(64, np.int64)
    perm[0::2] = np.arange(32)
    perm[1::2] = np.arange(32, 64)
    wq = np.empty((Dm, Dm), np.float32)
    wk = np.empty((Dm, Dm), np.float32)
    wv = np.empty((Dm, Dm), np.float32)
    for h in range(H):
        base = h * 192
        wq[:, h * 64:(h + 1) * 64] = Wqkv[:, base:base + 64][:, perm]
        wk[:, h * 64:(h + 1) * 64] = Wqkv[:, base + 64:base + 128][:, perm]
        wv[:, h * 64:(h + 1) * 64] = Wqkv[:, base + 128:base + 192]
    inv_freq = 1.0 / 10000.0 ** (np.arange(0, hd, 2, dtype=np.float32) / hd)
    ang = np.arange(2048, dtype=np.float32)[:, None] * inv_freq
    sin = np.sin(ang).T
    cos = np.cos(ang).T
    C = np.empty((128, 2048), np.float32)
    S = np.empty((128, 2048), np.float32)
    for g in range(2):
        o = g * 64
        C[o + 0:o + 64:2] = cos
        C[o + 1:o + 64:2] = cos
        S[o + 0:o + 64:2] = -sin
        S[o + 1:o + 64:2] = sin
    maskt = np.where(np.arange(128)[:, None] > np.arange(128)[None, :],
                     np.float32(NEGBIG * 8.0), np.float32(0.0))
    return wq, wk, wv, C, S, maskt







def build_a2(ctx, xf_dram, x1_dram, g2_dram, b2_dram, wg_dram, ident_dram,
             h2tok_dram, basetok_dram, gates_dram, mask_dram, pos_dram,
             counts_dram):
    nc = ctx.nc
    XF = [ctx.sbuf(f"XF{c}", [128, T], F32) for c in range(NCHUNK)]
    h2 = [ctx.sbuf(f"h2_{c}", [128, T], BF16) for c in range(NCHUNK)]
    g2_row = ctx.sbuf("g2_row_sb", [1, D], F32)
    b2_row = ctx.sbuf("b2_row_sb", [1, D], F32)
    negones = ctx.sbuf("negones_sb", [1, 512], F32)
    ident = ctx.sbuf("ident_f", [128, 128], F32)
    ident_bf = ctx.sbuf("ident_b", [128, 128], BF16)
    m_row = ctx.sbuf("m_row", [1, T], F32)
    r_row = ctx.sbuf("r_row", [1, T], F32)
    scr = [ctx.sbuf(f"scrA{i}", [128, 512], F32) for i in range(2)]
    h2f = ctx.sbuf("h2f", [128, T], F32)
    wg_sb = ctx.sbuf("wg_sb", [128, 64], F32)
    gatesf = ctx.sbuf("gatesf", [128, 128], F32)
    gsm = ctx.sbuf("gsm", [128, 128], F32)
    maskf = ctx.sbuf("maskf", [128, 128], F32)
    maskb = ctx.sbuf("maskb", [128, 128], BF16)
    mask_T = ctx.sbuf("mask_T", [8, T], F32)
    pos_T = ctx.sbuf("pos_T", [8, T], F32)
    postok = ctx.sbuf("postok", [128, 128], F32)
    gaux = ctx.sbuf("gaux", [128, 64], F32)
    counts_sb = ctx.sbuf("counts_sb", [1, 8], F32)
    tt_st = ctx.sbuf("tt_st", [128, 1024], F32)
    x1b = ctx.sbuf("x1b", [128, T], F32)

    aux = {"mean": m_row, "r": r_row, "mr": None, "negones": None, "scr": scr}

    dma_load(ctx, [(XF[c][:], xf_dram[c * 128:(c + 1) * 128, :])
                   for c in range(NCHUNK)]
             + [(g2_row[:], g2_dram[:, :]), (b2_row[:], b2_dram[:, :]),
                (ident[:], ident_dram[:, :]),
                (wg_sb[:].rearrange("p (c n) -> p c n", n=8),
                 wg_dram.rearrange("(c p) n -> p c n", p=128))])
    with nc.Block() as block:
        @block.gpsimd
        def _(g):
            g.memset(negones[:], -1.0)
    with nc.Block() as block:
        @block.vector
        def _(v):
            v.tensor_copy(ident_bf[:], ident[:])

    aux["mr"] = ctx.sbuf("mrA", [1, T], F32)
    aux["negones"] = negones
    emit_ln(ctx, [x[:] for x in XF], [o[:] for o in h2], aux,
            g_row=g2_row[:], b_row=b2_row[:])

    wg_bf = ctx.sbuf("wg_bf", [128, 64], BF16)
    with nc.Block() as block:
        @block.vector
        def _(v):
            v.tensor_copy(wg_bf[:], wg_sb[:])
    with nc.Block() as block:
        @block.tensor
        def _(t):
            for dc in range(NCHUNK):
                for f in range(4):
                    ps = [ctx.psX, ctx.psY][f // 2]
                    psl = slice((f % 2) * 512, (f % 2 + 1) * 512)
                    t.matmul(ps[0:8, psl], wg_bf[:, dc * 8:(dc + 1) * 8],
                             h2[dc][:, f * 512:(f + 1) * 512],
                             start=(dc == 0), stop=(dc == NCHUNK - 1))
    gates_T = mask_T
    with nc.Block() as block:
        @block.scalar
        def _(s):
            s.copy(gates_T[:, 0:1024], ctx.psX[0:8, :])
            s.copy(gates_T[:, 1024:2048], ctx.psY[0:8, :])
    for i in range(NT):
        with nc.Block() as block:
            @block.tensor
            def _(t):
                t.transpose(ctx.psA[0:128, 0:8],
                            gates_T[:, i * 128:(i + 1) * 128], ident[0:8, 0:8])
        with nc.Block() as block:
            @block.scalar
            def _(s):
                s.copy(gatesf[:, i * 8:(i + 1) * 8], ctx.psA[0:128, 0:8])

    for i in range(NT):
        gg = gatesf[:, i * 8:(i + 1) * 8]
        with nc.Block() as block:
            @block.vector
            def _(v):
                v.reduce_max(gaux[:, i:i + 1], gg, axis=mybir.AxisListType.X)
        with nc.Block() as block:
            @block.vector
            def _(v):
                v.tensor_scalar(gsm[:, i * 8:(i + 1) * 8], gg,
                                gaux[:, i:i + 1], 0.0,
                                op0=ALU.subtract, op1=ALU.add)
        with nc.Block() as block:
            @block.scalar
            def _(s):
                s.activation(gsm[:, i * 8:(i + 1) * 8],
                             gsm[:, i * 8:(i + 1) * 8], AF.Exp,
                             accum_out=gaux[:, 16 + i:17 + i])
    with nc.Block() as block:
        @block.vector
        def _(v):
            v.reciprocal(gaux[:, 32:48], gaux[:, 16:32])
    for i in range(NT):
        with nc.Block() as block:
            @block.vector
            def _(v):
                v.tensor_scalar_mul(gsm[:, i * 8:(i + 1) * 8],
                                    gsm[:, i * 8:(i + 1) * 8],
                                    gaux[:, 32 + i:33 + i])
        with nc.Block() as block:
            @block.vector
            def _(v):
                v.max(out=gaux[:, 48:56], in_=gsm[:, i * 8:(i + 1) * 8])
        with nc.Block() as block:
            @block.vector
            def _(v):
                v.tensor_scalar(maskf[:, i * 8:(i + 1) * 8],
                                gsm[:, i * 8:(i + 1) * 8],
                                gaux[:, 49:50], 0.0,
                                op0=ALU.is_ge, op1=ALU.add)
    with nc.Block() as block:
        @block.vector
        def _(v):
            v.tensor_copy(maskb[:], maskf[:])

    for i in range(NT):
        with nc.Block() as block:
            @block.tensor
            def _(t):
                t.transpose(ctx.psA[:, :].bitcast(BF16)[0:8, 0:128],
                            maskb[:, i * 8:(i + 1) * 8], ident_bf[:])
        with nc.Block() as block:
            @block.scalar
            def _(s):
                s.copy(mask_T[:, i * 128:(i + 1) * 128],
                       ctx.psA[:, :].bitcast(BF16)[0:8, 0:128])
    with nc.Block() as block:
        @block.vector
        def _(v):
            v.tensor_tensor_scan(pos_T[:], mask_T[:], mask_T[:], 0.0,
                                 op0=ALU.add, op1=ALU.bypass)
    with nc.Block() as block:
        @block.tensor
        def _(t):
            t.matmul(ctx.psR1[0:1, 0:8], pos_T[:, T - 1:T], ident[0:8, 0:8],
                     start=True, stop=True)
    with nc.Block() as block:
        @block.scalar
        def _(s):
            s.copy(counts_sb[:], ctx.psR1[0:1, 0:8])
    dma_store(ctx, [(counts_dram[:, :], counts_sb[:])])
    for i in range(NT):
        with nc.Block() as block:
            @block.tensor
            def _(t):
                t.transpose(ctx.psA[0:128, 0:8],
                            pos_T[:, i * 128:(i + 1) * 128], ident[0:8, 0:8])
        with nc.Block() as block:
            @block.scalar
            def _(s):
                s.copy(postok[:, i * 8:(i + 1) * 8], ctx.psA[0:128, 0:8])
    g_r = gates_dram.rearrange("(i p) e -> p i e", p=128)
    m_r = mask_dram.rearrange("(i p) e -> p i e", p=128)
    p_r = pos_dram.rearrange("(i p) e -> p i e", p=128)
    dma_store(ctx, [(g_r[:, i, :], gsm[:, i * 8:(i + 1) * 8])
                    for i in range(NT)])
    dma_store(ctx, [(m_r[:, i, :], maskf[:, i * 8:(i + 1) * 8])
                    for i in range(NT)])
    dma_store(ctx, [(p_r[:, i, :], postok[:, i * 8:(i + 1) * 8])
                    for i in range(NT)])

    h_r = h2tok_dram.rearrange("(i p) d -> p i d", p=128)
    for i in range(NT):
        with nc.Block() as block:
            @block.tensor
            def _(t):
                for dc in range(NCHUNK):
                    t.transpose(
                        ctx.psX[:, :].bitcast(BF16)[:, dc * 128:(dc + 1) * 128],
                        h2[dc][:, i * 128:(i + 1) * 128], ident_bf[:])
        with nc.Block() as block:
            @block.vector
            def _(v):
                v.tensor_copy(tt_st[:].bitcast(BF16)[:, 0:1024],
                              ctx.psX[:, :].bitcast(BF16)[:, 0:1024])
        dma_store(ctx, [(h_r[:, i, :], tt_st[:].bitcast(BF16)[:, 0:1024])])

    b_r = basetok_dram.rearrange("(i p) d -> p i d", p=128)
    for dc in range(NCHUNK):
        dma_load(ctx, [(x1b[:], x1_dram[dc * 128:(dc + 1) * 128, :])])
        with nc.Block() as block:
            @block.vector
            def _(v):
                v.tensor_add(XF[dc][:], XF[dc][:], x1b[:])
    for i in range(NT):
        with nc.Block() as block:
            @block.tensor
            def _(t):
                for dc in range(NCHUNK):
                    t.transpose(ctx.psX[:, dc * 128:(dc + 1) * 128],
                                XF[dc][:, i * 128:(i + 1) * 128], ident[:])
        with nc.Block() as block:
            @block.scalar
            def _(s):
                s.copy(tt_st[:, 0:1024], ctx.psX[:, :])
        dma_store(ctx, [(b_r[:, i, :], tt_st[:, 0:1024])])




def build_b(ctx, counts_dram, pref_dram, h2_dram, base_dram, gates_dram,
            mask_dram, pos_dram, w1_dram, b1c_dram, w2_dram, b2r_dram,
            iota_dram, tokidx_dram, ident_dram, y_dram):
    nc = ctx.nc
    psM = [ctx.psum(f"psM{i}", [128, 1024]) for i in range(3)]
    psS = ctx.psum("psS", [128, 1024])

    HC = CAP // 2
    midb = ctx.sbuf("midb", [128, NF * HC], BF16)
    dispT = [ctx.sbuf(f"dispT{c}", [128, CAP], BF16) for c in range(NCHUNK)]
    disp_tok = ctx.sbuf("disp_tok", [128, 1024], BF16)
    w1st_f = ctx.sbuf("w1st_f", [128, 1024], F32)
    w1st_b = ctx.sbuf("w1st_b", [128, 1024], BF16)
    w2st_f = ctx.sbuf("w2st_f", [128, 1024], F32)
    w2st_b = ctx.sbuf("w2st_b", [128, 1024], BF16)
    b1c_sb = ctx.sbuf("b1c_sb", [128, 32], F32)
    b2r_sb = ctx.sbuf("b2r_sb", [1, 1024], F32)
    b2r_bf = ctx.sbuf("b2r_bf", [1, 1024], BF16)
    ident = ctx.sbuf("ident_f", [128, 128], F32)
    ident_bf = ctx.sbuf("ident_b", [128, 128], BF16)
    iota_sb = ctx.sbuf("iota_sb", [128, CAP], F32)
    toki_sb = ctx.sbuf("toki_sb", [128, 16], I32)
    counts_sb = ctx.sbuf("counts_sb2", [8, 8], F32)
    pref_sb = ctx.sbuf("pref_sb", [8, 1], F32)
    offs_col = ctx.sbuf("offs_col", [8, 1], F32)
    offs_row = ctx.sbuf("offs_row", [1, 8], F32)
    offsbc = ctx.sbuf("offsbc", [128, 8], F32)
    iota8 = ctx.sbuf("iota8", [128, 8], F32)
    gt = ctx.sbuf("gt", [128, 8], F32)
    mt = ctx.sbuf("mt", [128, 8], F32)
    pt = ctx.sbuf("pt", [128, 8], F32)
    gp = ctx.sbuf("gp", [128, 8], F32)
    kp = ctx.sbuf("kp", [128, 8], F32)
    sv = ctx.sbuf("sv", [128, 8], F32)
    sl8 = ctx.sbuf("sl8", [128, 8], F32)
    seq = ctx.sbuf("seq", [128, 8], F32)
    seq2 = ctx.sbuf("seq2", [128, 8], F32)
    seqi = ctx.sbuf("seqi", [128, 8], I32)
    eq1 = ctx.sbuf("eq1", [128, 8], F32)
    eq2 = ctx.sbuf("eq2", [128, 8], F32)
    g2s = ctx.sbuf("g2s", [128, 8], F32)
    prod = ctx.sbuf("prod", [128, 8], F32)
    cc1 = ctx.sbuf("cc1", [128, 16], F32)
    jj = ctx.sbuf("jj", [128, 32], I32)
    gv = ctx.sbuf("gv", [128, 32], F32)
    jf = ctx.sbuf("jf", [128, 2], F32)
    seqeq_all = ctx.sbuf("seqeq_all", [128, 128], F32)
    tokf_sb = ctx.sbuf("tokf_sb", [128, 16], F32)
    eqb = None
    idxrow = ctx.sbuf("idxrow", [1, CAP], F32)
    idxrow_i = ctx.sbuf("idxrow_i", [1, CAP], I32)
    gA = ctx.sbuf("gA", [128, 1024], BF16)
    gB = ctx.sbuf("gB", [128, 1024], BF16)
    yst = ctx.sbuf("yst", [128, 1024], F32)
    idxcol = ctx.sbuf("idxcol", [128, 1], I32)
    seq1 = ctx.sbuf("seq1", [128, 1], I32)
    tok1 = ctx.sbuf("tok1", [128, 1], I32)

    idx_dram = nc.dram_tensor("idx_scratch", [8 * CAP, 1], I32).ap()
    exp_dram = nc.dram_tensor("exp_scratch", [8 * CAP, D], BF16).ap()

    eqb = w1st_f[:, 0:CAP]
    dma_load(ctx, [(counts_sb[:], counts_dram[:, :]),
                   (pref_sb[:], pref_dram[:, :]),
                   (ident[:], ident_dram[:, :]),
                   (iota_sb[:], iota_dram[:, :]),
                   (toki_sb[:], tokidx_dram[:, :])])
    with nc.Block() as block:
        @block.gpsimd
        def _(g):
            g.memset(gA[:], 0.0)
            g.memset(gB[:], 0.0)
    with nc.Block() as block:
        @block.vector
        def _(v):
            v.tensor_copy(ident_bf[:], ident[:])
            v.tensor_copy(iota8[:], iota_sb[:, 0:8])
            v.tensor_copy(tokf_sb[:], toki_sb[:])

    with nc.Block() as block:
        @block.tensor
        def _(t):
            t.matmul(psM[0][0:8, 0:1], counts_sb[:], pref_sb[:],
                     start=True, stop=True)
    with nc.Block() as block:
        @block.scalar
        def _(s):
            s.copy(offs_col[:], psM[0][0:8, 0:1])
    with nc.Block() as block:
        @block.tensor
        def _(t):
            t.matmul(psM[0][0:1, 0:8], offs_col[:], ident[0:8, 0:8],
                     start=True, stop=True)
    with nc.Block() as block:
        @block.scalar
        def _(s):
            s.copy(offs_row[:], psM[0][0:1, 0:8])
    with nc.Block() as block:
        @block.tensor
        def _(t):
            t.matmul(psM[0][0:128, 0:8], ctx.ones_col[:], offs_row[:],
                     start=True, stop=True)
    with nc.Block() as block:
        @block.scalar
        def _(s):
            s.copy(offsbc[:], psM[0][0:128, 0:8])

    for i in range(NT):
        rs = slice(i * 128, (i + 1) * 128)
        dma_load(ctx, [(gt[:], gates_dram[rs, :]), (mt[:], mask_dram[rs, :]),
                       (pt[:], pos_dram[rs, :])])
        with nc.Block() as block:
            @block.vector
            def _(v):
                v.tensor_add(gp[:], pt[:], offsbc[:])
                v.tensor_scalar_add(sl8[:], pt[:], -1.0)
        with nc.Block() as block:
            @block.vector
            def _(v):
                v.tensor_single_scalar(kp[:], gp[:], 4097.0, op=ALU.is_lt)
                v.tensor_single_scalar(sv[:], sl8[:], float(CAP), op=ALU.is_lt)
        with nc.Block() as block:
            @block.vector
            def _(v):
                v.tensor_mul(kp[:], kp[:], mt[:])
        with nc.Block() as block:
            @block.vector
            def _(v):
                v.tensor_mul(kp[:], kp[:], sv[:])
        with nc.Block() as block:
            @block.vector
            def _(v):
                v.scalar_tensor_tensor(seq[:], sl8[:], 5000.0, kp[:],
                                       op0=ALU.add, op1=ALU.mult)
        with nc.Block() as block:
            @block.vector
            def _(v):
                v.tensor_scalar_add(seqeq_all[:, i * 8:(i + 1) * 8], seq[:],
                                    -5000.0)
        with nc.Block() as block:
            @block.vector
            def _(v):
                v.reduce_max(cc1[:, 0:1], gt[:], axis=mybir.AxisListType.X)
        with nc.Block() as block:
            @block.vector
            def _(v):
                v.tensor_scalar(eq1[:], gt[:], cc1[:, 0:1], 0.0,
                                op0=ALU.is_equal, op1=ALU.add)
        with nc.Block() as block:
            @block.vector
            def _(v):
                v.tensor_mul(prod[:], eq1[:], iota8[:])
        with nc.Block() as block:
            @block.vector
            def _(v):
                v.reduce_sum(cc1[:, 1:2], prod[:], axis=mybir.AxisListType.X)
        with nc.Block() as block:
            @block.vector
            def _(v):
                v.scalar_tensor_tensor(g2s[:], eq1[:], -10.0, gt[:],
                                       op0=ALU.mult, op1=ALU.add)
        with nc.Block() as block:
            @block.vector
            def _(v):
                v.reduce_max(cc1[:, 2:3], g2s[:], axis=mybir.AxisListType.X)
        with nc.Block() as block:
            @block.vector
            def _(v):
                v.tensor_scalar(eq2[:], g2s[:], cc1[:, 2:3], 0.0,
                                op0=ALU.is_equal, op1=ALU.add)
        for (eqm, o) in ((eq1, 0), (eq2, 1)):
            with nc.Block() as block:
                @block.vector
                def _(v):
                    v.tensor_mul(prod[:], eqm[:], sl8[:])
            with nc.Block() as block:
                @block.vector
                def _(v):
                    v.reduce_sum(cc1[:, 4 + o:5 + o], prod[:],
                                 axis=mybir.AxisListType.X)
            with nc.Block() as block:
                @block.vector
                def _(v):
                    v.tensor_mul(prod[:], eqm[:], kp[:])
            with nc.Block() as block:
                @block.vector
                def _(v):
                    v.reduce_sum(cc1[:, 6 + o:7 + o], prod[:],
                                 axis=mybir.AxisListType.X)
        if True:
            with nc.Block() as block:
                @block.vector
                def _(v):
                    v.tensor_mul(prod[:], eq2[:], iota8[:])
            with nc.Block() as block:
                @block.vector
                def _(v):
                    v.reduce_sum(cc1[:, 3:4], prod[:],
                                 axis=mybir.AxisListType.X)
        with nc.Block() as block:
            @block.vector
            def _(v):
                v.tensor_mul(gv[:, 2 * i:2 * i + 1], cc1[:, 0:1], cc1[:, 6:7])
        with nc.Block() as block:
            @block.vector
            def _(v):
                v.tensor_mul(gv[:, 2 * i + 1:2 * i + 2], cc1[:, 2:3],
                             cc1[:, 7:8])
        for o in range(2):
            with nc.Block() as block:
                @block.vector
                def _(v):
                    v.scalar_tensor_tensor(jf[:, o:o + 1],
                                           cc1[:, 1 + 2 * o:2 + 2 * o],
                                           float(CAP),
                                           cc1[:, 4 + o:5 + o],
                                           op0=ALU.mult, op1=ALU.add)
            with nc.Block() as block:
                @block.vector
                def _(v):
                    v.scalar_tensor_tensor(jf[:, o:o + 1], jf[:, o:o + 1],
                                           -BIGI, cc1[:, 6 + o:7 + o],
                                           op0=ALU.add, op1=ALU.mult)
            with nc.Block() as block:
                @block.vector
                def _(v):
                    v.tensor_scalar_add(jf[:, o:o + 1], jf[:, o:o + 1], BIGI)
        with nc.Block() as block:
            @block.vector
            def _(v):
                v.tensor_copy(jj[:, 2 * i:2 * i + 2], jf[:, 0:2])

    for e in range(8):
        for i in range(NT):
            with nc.Block() as block:
                @block.vector
                def _(v):
                    v.tensor_scalar(eqb, iota_sb[:],
                                    seqeq_all[:, i * 8 + e:i * 8 + e + 1],
                                    0.0, op0=ALU.is_equal, op1=ALU.add)
            with nc.Block() as block:
                @block.tensor
                def _(t):
                    t.matmul(psM[0][0:1, 0:512], tokf_sb[:, i:i + 1],
                             eqb[:, 0:512], start=(i == 0),
                             stop=(i == NT - 1))
                    t.matmul(psM[0][0:1, 512:768], tokf_sb[:, i:i + 1],
                             eqb[:, 512:768], start=(i == 0),
                             stop=(i == NT - 1))
        with nc.Block() as block:
            @block.scalar
            def _(s):
                s.copy(idxrow[:], psM[0][0:1, 0:768])
        with nc.Block() as block:
            @block.vector
            def _(v):
                v.tensor_copy(idxrow_i[:], idxrow[:])
        dma_store(ctx, [(idx_dram[e * CAP:(e + 1) * CAP, :]
                         .rearrange("r one -> one r"), idxrow_i[:])])

    for e in range(8):
        dma_load(ctx, [(b1c_sb[:], b1c_dram[e]),
                       (b2r_sb[:], b2r_dram[e])])
        with nc.Block() as block:
            @block.vector
            def _(v):
                v.tensor_copy(b2r_bf[:], b2r_sb[:])
        for st in range(6):
            r0 = e * CAP + st * 128
            dma_load(ctx, [(idxcol[:], idx_dram[r0:r0 + 128, :])])
            sem_g = ctx.named_sem("gath")
            base_g = ctx.semc["gath"]
            ctx.semc["gath"] = base_g + 16
            with nc.Block() as block:
                @block.gpsimd
                def _(g):
                    sem, base = sem_g, base_g
                    g.indirect_dma_start(
                        out=disp_tok[:],
                        out_offset=None,
                        in_=h2_dram[:, :],
                        in_offset=IndirectOffsetOnAxis(idxcol[:], 0))\
                        .then_inc(sem, 16)
                    g.wait_ge(sem, base + 16)
            with nc.Block() as block:
                @block.tensor
                def _(t):
                    for dc in range(NCHUNK):
                        t.transpose(
                            psM[0][:, :].bitcast(BF16)[:, dc * 128:(dc + 1) * 128],
                            disp_tok[:, dc * 128:(dc + 1) * 128], ident_bf[:])
            with nc.Block() as block:
                @block.vector
                def _(v):
                    for dc in range(NCHUNK):
                        v.tensor_copy(dispT[dc][:, st * 128:(st + 1) * 128],
                                      psM[0][:, :].bitcast(BF16)
                                      [:, dc * 128:(dc + 1) * 128])
        w1_r = w1_dram[e].rearrange("(c p) n -> p c n", p=128)
        for hc in range(2):
            ssl = slice(hc * HC, (hc + 1) * HC)
            for fc in range(NF):
                dma_load(ctx, [(w1st_f[:].rearrange("p (c n) -> p c n", n=128),
                                w1_r[:, :, fc * 128:(fc + 1) * 128])])
                with nc.Block() as block:
                    @block.vector
                    def _(v):
                        v.tensor_copy(w1st_b[:], w1st_f[:])
                with nc.Block() as block:
                    @block.tensor
                    def _(t):
                        for dc in range(NCHUNK):
                            lhsT = w1st_b[:].rearrange("p (c n) -> p c n",
                                                       n=128)[:, dc, :]
                            t.matmul(psS[:, 0:HC], lhsT, dispT[dc][:, ssl],
                                     start=(dc == 0), stop=(dc == NCHUNK - 1))
                with nc.Block() as block:
                    @block.scalar
                    def _(s):
                        s.activation(midb[:, fc * HC:(fc + 1) * HC],
                                     psS[:, 0:HC], AF.Gelu,
                                     bias=b1c_sb[:, fc:fc + 1])
            for fc in range(NF):
                r0 = fc * 128
                dma_load(ctx, [(w2st_f[:], w2_dram[e][r0:r0 + 128, :])])
                with nc.Block() as block:
                    @block.vector
                    def _(v):
                        v.tensor_copy(w2st_b[:], w2st_f[:])
                with nc.Block() as block:
                    @block.tensor
                    def _(t):
                        for st3 in range(3):
                            lhsT = midb[:, fc * HC + st3 * 128:
                                        fc * HC + (st3 + 1) * 128]
                            t.matmul(psM[st3][:, 0:512], lhsT,
                                     w2st_b[:, 0:512],
                                     start=(fc == 0), stop=False)
                            t.matmul(psM[st3][:, 512:1024], lhsT,
                                     w2st_b[:, 512:1024],
                                     start=(fc == 0), stop=False)
            with nc.Block() as block:
                @block.tensor
                def _(t):
                    for st3 in range(3):
                        t.matmul(psM[st3][:, 0:512], ctx.ones_col_bf[:],
                                 b2r_bf[0:1, 0:512], start=False, stop=True)
                        t.matmul(psM[st3][:, 512:1024], ctx.ones_col_bf[:],
                                 b2r_bf[0:1, 512:1024], start=False, stop=True)
            for st3 in range(3):
                st = hc * 3 + st3
                with nc.Block() as block:
                    @block.scalar
                    def _(s):
                        s.copy(disp_tok[:], psM[st3][:, :])
                r0 = e * CAP + st * 128
                dma_store(ctx, [(exp_dram[r0:r0 + 128, :], disp_tok[:])])

    for i in range(NT):
        rs = slice(i * 128, (i + 1) * 128)
        baset = w1st_f
        t1 = w2st_f
        dma_load(ctx, [(baset[:], base_dram[rs, :])])
        sem_g2 = ctx.named_sem("gath2")
        base_g2 = ctx.semc["gath2"]
        ctx.semc["gath2"] = base_g2 + 32
        with nc.Block() as block:
            @block.gpsimd
            def _(g):
                sem, base = sem_g2, base_g2
                g.indirect_dma_start(
                    out=gA[:], out_offset=None, in_=exp_dram[:, :],
                    in_offset=IndirectOffsetOnAxis(jj[:, 2 * i:2 * i + 1], 0),
                    bounds_check=8 * CAP - 1,
                    oob_is_err=False).then_inc(sem, 16)
                g.indirect_dma_start(
                    out=gB[:], out_offset=None, in_=exp_dram[:, :],
                    in_offset=IndirectOffsetOnAxis(jj[:, 2 * i + 1:2 * i + 2], 0),
                    bounds_check=8 * CAP - 1,
                    oob_is_err=False).then_inc(sem, 16)
                g.wait_ge(sem, base + 32)
        with nc.Block() as block:
            @block.vector
            def _(v):
                v.scalar_tensor_tensor(t1[:], gA[:], gv[:, 2 * i:2 * i + 1],
                                       baset[:], op0=ALU.mult, op1=ALU.add)
        with nc.Block() as block:
            @block.vector
            def _(v):
                v.scalar_tensor_tensor(yst[:], gB[:],
                                       gv[:, 2 * i + 1:2 * i + 2],
                                       t1[:], op0=ALU.mult, op1=ALU.add)
        dma_store(ctx, [(y_dram[rs, :], yst[:])])



def host_consts_b(core):
    pref = np.zeros((8, 1), np.float32)
    pref[:core] = 1.0
    iota = np.broadcast_to(np.arange(CAP, dtype=np.float32), (128, CAP)).copy()
    tki = (np.arange(T, dtype=np.int32).reshape(16, 128).T).copy()
    return pref, iota, tki






TOPK = 614
SIG_THR = float(2 * TOPK - 2048)
NITER = 36
INV32 = 1.0 / 32.0


def build_l1attn(ctx, x_dram, wqkv_dram, bqkv_cols_dram, bv_row_dram,
                 wout_dram, bout_cols_dram, ident_dram, y_dram):
    nc = ctx.nc
    A = [ctx.sbuf(f"A{c}", [128, T], F32) for c in range(NCHUNK)]
    q_sb = [ctx.sbuf(f"q{c}", [128, T], BF16) for c in range(NCHUNK)]
    k_sb = [ctx.sbuf(f"k{c}", [128, T], BF16) for c in range(NCHUNK)]
    v_sb = [ctx.sbuf(f"v{i}", [128, D], BF16) for i in range(NT)]
    ln_mean = ctx.sbuf("lnmean", [1, T], F32)
    ln_r = ctx.sbuf("lnr", [1, T], F32)
    lnscr = [ctx.sbuf(f"lnscr{i}", [128, 512], F32) for i in range(2)]
    bqkv_cols = ctx.sbuf("bqkv_cols_sb", [128, 24], F32)
    bout_cols = ctx.sbuf("bout_cols_sb", [128, 8], F32)
    bv_row = ctx.sbuf("bv_row_sb", [1, D], F32)
    bv_row_bf = ctx.sbuf("bv_row_bf", [1, D], BF16)
    ident = ctx.sbuf("ident_sb", [128, 128], F32)
    z_row = ctx.sbuf("z_row", [1, 128], F32)
    invz_row = ctx.sbuf("invz_row", [1, 128], F32)
    invzbc = ctx.sbuf("invzbc", [64, 128], F32)
    lo_c = ctx.sbuf("lo_c", [128, 4], F32)
    hi_c = ctx.sbuf("hi_c", [128, 4], F32)
    t_c = ctx.sbuf("t_c", [128, 4], F32)
    nt_c = ctx.sbuf("nt_c", [128, 4], F32)
    sig2_c = ctx.sbuf("sig2_c", [128, 8], F32)
    sig_c = ctx.sbuf("sig_c", [128, 4], F32)
    mge_c = ctx.sbuf("mge_c", [128, 4], U8)
    mlt_c = ctx.sbuf("mlt_c", [128, 4], U8)
    nlo_c = ctx.sbuf("nlo_c", [128, 4], F32)
    nt32_c = ctx.sbuf("nt32_c", [128, 4], F32)
    z_cols = ctx.sbuf("z_cols", [128, 16], F32)
    zh_c = ctx.sbuf("zh_c", [128, 2], F32)
    sgnbuf = ctx.sbuf("sgnbuf", [128, 1024], BF16)
    ebuf = ctx.sbuf("ebuf", [128, 1024], BF16)
    pmbuf = ctx.sbuf("pmbuf", [128, 1024], BF16)
    p_sb = ctx.sbuf("p_sb", [128, T], BF16)

    aux = {"mean": ln_mean, "r": ln_r, "mr": None, "negones": None,
           "scr": lnscr}

    Abf = [a[:].bitcast(BF16) for a in A]
    Af32hi = [a[:, 1024:2048] for a in A]
    xln = [Abf[c][:, 0:2048] for c in range(NCHUNK)]
    attn = xln
    wst_f32 = Af32hi[0]
    wst_bf = Abf[1][:, 2048:3072]
    wv_bf = [Abf[4 + j][:, 2048:4096] for j in range(4)]

    def sc_half(tq, hf):
        return Af32hi[2 * tq + hf]

    dma_load(ctx, [(A[c][:], x_dram[c * 128:(c + 1) * 128, :])
                   for c in range(NCHUNK)]
             + [(bqkv_cols[:], bqkv_cols_dram[:, :]),
                (bout_cols[:], bout_cols_dram[:, :]),
                (bv_row[:], bv_row_dram[:, :]),
                (ident[:], ident_dram[:, :])])
    with nc.Block() as block:
        @block.vector
        def _(v):
            v.tensor_copy(bv_row_bf[:], bv_row[:])
            v.tensor_copy(ctx.ident_bf[:], ident[:])

    emit_ln(ctx, [a[:] for a in A], xln, aux)

    wqkv_r = wqkv_dram.rearrange("(c p) n -> p c n", p=128)
    for qk in range(2):
        dst = q_sb if qk == 0 else k_sb
        for oc in range(NCHUNK):
            ocol = qk * 1024 + oc * 128
            dma_load(ctx, [(wst_f32.rearrange("p (c n) -> p c n", n=128),
                            wqkv_r[:, :, ocol:ocol + 128])])
            with nc.Block() as block:
                @block.vector
                def _(v):
                    v.tensor_copy(wst_bf, wst_f32)
            with nc.Block() as block:
                @block.tensor
                def _(t):
                    for dc in range(NCHUNK):
                        lhsT = wst_bf.rearrange("p (c n) -> p c n", n=128)[:, dc, :]
                        for f in range(4):
                            ps = [ctx.psX, ctx.psY][f // 2]
                            psl = slice((f % 2) * 512, (f % 2 + 1) * 512)
                            t.matmul(ps[:, psl], lhsT,
                                     xln[dc][:, f * 512:(f + 1) * 512],
                                     start=(dc == 0), stop=(dc == NCHUNK - 1))
            with nc.Block() as block:
                @block.scalar
                def _(s):
                    bcol = bqkv_cols[:, qk * 8 + oc:qk * 8 + oc + 1]
                    s.activation(dst[oc][:, 0:1024], ctx.psX[:], AF.Identity,
                                 bias=bcol)
                    s.activation(dst[oc][:, 1024:2048], ctx.psY[:], AF.Identity,
                                 bias=bcol)

    for dc in range(NCHUNK):
        dma_load(ctx, [(wst_f32, wqkv_r[:, dc, 2048:3072])])
        with nc.Block() as block:
            @block.vector
            def _(v):
                half = wv_bf[dc // 2]
                v.tensor_copy(half[:, (dc % 2) * 1024:(dc % 2 + 1) * 1024],
                              wst_f32)
    for i in range(NT):
        tsl = slice(i * 128, (i + 1) * 128)
        with nc.Block() as block:
            @block.tensor
            def _(t):
                for dc in range(NCHUNK):
                    wv_dc = wv_bf[dc // 2][:, (dc % 2) * 1024:(dc % 2 + 1) * 1024]
                    for f in range(2):
                        t.matmul(ctx.psX[:, f * 512:(f + 1) * 512],
                                 xln[dc][:, tsl],
                                 wv_dc[:, f * 512:(f + 1) * 512],
                                 start=(dc == 0), stop=False)
                for f in range(2):
                    t.matmul(ctx.psX[:, f * 512:(f + 1) * 512],
                             ctx.ones_col_bf[:],
                             bv_row_bf[0:1, f * 512:(f + 1) * 512],
                             start=False, stop=True)
        with nc.Block() as block:
            @block.scalar
            def _(s):
                s.copy(v_sb[i][:], ctx.psX[:])

    for h in range(NHEADS):
        ch, off = h // 2, (h % 2) * 64
        qh = q_sb[ch][off:off + 64, :]
        kh = k_sb[ch][off:off + 64, :]
        hsl = slice(h * 64, (h + 1) * 64)

        for grp in range(4):
            for tq in range(4):
                qt = grp * 4 + tq
                with nc.Block() as block:
                    @block.tensor
                    def _(t):
                        for f in range(4):
                            ps = [ctx.psX, ctx.psY][f // 2]
                            psl = slice((f % 2) * 512, (f % 2 + 1) * 512)
                            t.matmul(ps[:, psl],
                                     qh[:, qt * 128:(qt + 1) * 128],
                                     kh[:, f * 512:(f + 1) * 512],
                                     start=True, stop=True)
                with nc.Block() as block:
                    @block.scalar
                    def _(s):
                        s.copy(sc_half(tq, 0), ctx.psX[:])
                        s.copy(sc_half(tq, 1), ctx.psY[:])
            with nc.Block() as block:
                @block.vector
                def _(v):
                    for tq in range(4):
                        v.reduce_max(sig2_c[:, 2 * tq:2 * tq + 1],
                                     sc_half(tq, 0), axis=mybir.AxisListType.X)
                        v.reduce_max(sig2_c[:, 2 * tq + 1:2 * tq + 2],
                                     sc_half(tq, 1), axis=mybir.AxisListType.X)
            with nc.Block() as block:
                @block.vector
                def _(v):
                    v.tensor_max(hi_c[:], sig2_c[:, 0:8:2], sig2_c[:, 1:8:2])
            with nc.Block() as block:
                @block.vector
                def _(v):
                    for tq in range(4):
                        v.tensor_reduce(sig2_c[:, 2 * tq:2 * tq + 1],
                                        sc_half(tq, 0),
                                        axis=mybir.AxisListType.X, op=ALU.min)
                        v.tensor_reduce(sig2_c[:, 2 * tq + 1:2 * tq + 2],
                                        sc_half(tq, 1),
                                        axis=mybir.AxisListType.X, op=ALU.min)
            with nc.Block() as block:
                @block.vector
                def _(v):
                    v.tensor_tensor(t_c[:], sig2_c[:, 0:8:2], sig2_c[:, 1:8:2],
                                    op=ALU.min)
            with nc.Block() as block:
                @block.vector
                def _(v):
                    v.tensor_scalar_add(lo_c[:], t_c[:], -1.0)
            VA = ctx.named_sem("bis_va"); SA = ctx.named_sem("bis_sa")
            VS = ctx.named_sem("bis_vs"); SS = ctx.named_sem("bis_ss")
            vab = ctx.semc["bis_va"]; sab = ctx.semc["bis_sa"]
            vsb = ctx.semc["bis_vs"]; ssb = ctx.semc["bis_ss"]
            ctx.semc["bis_va"] = vab + 5 * NITER
            ctx.semc["bis_sa"] = sab + NITER
            ctx.semc["bis_vs"] = vsb + NITER
            ctx.semc["bis_ss"] = ssb + NITER
            with nc.Block() as block:
                @block.vector
                def _(v):
                    for it in range(NITER):
                        A0 = vab + 5 * it
                        v.tensor_add(sig_c[:], lo_c[:], hi_c[:])\
                            .then_inc(VA, 1)
                        v.wait_ge(VA, A0 + 1)
                        v.tensor_scalar_mul(t_c[:], sig_c[:], 0.5)\
                            .then_inc(VS, 1)
                        v.wait_ge(SS, ssb + it + 1)
                        v.tensor_add(sig_c[:], sig2_c[:, 0:8:2],
                                     sig2_c[:, 1:8:2]).then_inc(VA, 1)
                        v.wait_ge(VA, A0 + 2)
                        v.tensor_single_scalar(mge_c[:], sig_c[:], SIG_THR,
                                               op=ALU.is_ge).then_inc(VA, 1)
                        v.tensor_single_scalar(mlt_c[:], sig_c[:], SIG_THR,
                                               op=ALU.is_lt).then_inc(VA, 1)
                        v.wait_ge(VA, A0 + 4)
                        v.copy_predicated(lo_c[:], mge_c[:], t_c[:])
                        v.copy_predicated(hi_c[:], mlt_c[:], t_c[:])\
                            .then_inc(VA, 1)
                        v.wait_ge(VA, A0 + 5)

                @block.scalar
                def _(s):
                    for it in range(NITER):
                        s.wait_ge(VS, vsb + it + 1)
                        s.mul(nt_c[:], t_c[:], -1.0).then_inc(SA, 1)
                        s.wait_ge(SA, sab + it + 1)
                        for tq in range(4):
                            for hf in range(2):
                                ins = s.activation(
                                    pmbuf[:], sc_half(tq, hf), AF.Sign,
                                    bias=nt_c[:, tq:tq + 1],
                                    accum_out=sig2_c[:, 2 * tq + hf:
                                                     2 * tq + hf + 1])
                        ins.then_inc(SS, 1)
            with nc.Block() as block:
                @block.scalar
                def _(s):
                    s.mul(nlo_c[:], lo_c[:], -1.0)
                    s.mul(nt32_c[:], lo_c[:], -INV32)
            for tq in range(4):
                qt = grp * 4 + tq
                for hf in range(2):
                    with nc.Block() as block:
                        @block.scalar
                        def _(s):
                            s.activation(sgnbuf[:], sc_half(tq, hf), AF.Sign,
                                         bias=nlo_c[:, tq:tq + 1])
                            s.activation(ebuf[:], sc_half(tq, hf), AF.Exp,
                                         bias=nt32_c[:, tq:tq + 1],
                                         scale=INV32)
                    with nc.Block() as block:
                        @block.vector
                        def _(v):
                            v.tensor_mul(pmbuf[:], ebuf[:], sgnbuf[:])
                    with nc.Block() as block:
                        @block.scalar
                        def _(s):
                            s.activation(p_sb[:, hf * 1024:(hf + 1) * 1024],
                                         pmbuf[:], AF.Relu,
                                         accum_out=zh_c[:, hf:hf + 1])
                with nc.Block() as block:
                    @block.vector
                    def _(v):
                        v.tensor_add(z_cols[:, qt:qt + 1], zh_c[:, 0:1],
                                     zh_c[:, 1:2])
                psAb = ctx.psA[:, :].bitcast(BF16)
                psBb = ctx.psB[:, :].bitcast(BF16)
                with nc.Block() as block:
                    @block.tensor
                    def _(t):
                        for kc in range(NT):
                            dstp = [psAb, psBb][kc % 2]
                            t.transpose(dstp[:, (kc // 2) * 128:
                                             (kc // 2 + 1) * 128],
                                        p_sb[:, kc * 128:(kc + 1) * 128],
                                        ctx.ident_bf[:])
                with nc.Block() as block:
                    @block.vector
                    def _(v):
                        v.tensor_copy(p_sb[:, 0:1024], psAb[:, :])
                        v.tensor_copy(p_sb[:, 1024:2048], psBb[:, :])
                with nc.Block() as block:
                    @block.tensor
                    def _(t):
                        for kc in range(NT):
                            rhs = p_sb[:, (kc % 2) * 1024 + (kc // 2) * 128:
                                       (kc % 2) * 1024 + (kc // 2) * 128 + 128]
                            t.matmul(ctx.psX[0:64, 0:128], v_sb[kc][:, hsl],
                                     rhs, start=(kc == 0), stop=(kc == NT - 1))
                    @block.tensor
                    def _(t2):
                        pass
                with nc.Block() as block:
                    @block.tensor
                    def _(t):
                        t.matmul(ctx.psR1[0:1, 0:128], z_cols[:, qt:qt + 1],
                                 ident[:], start=True, stop=True)
                with nc.Block() as block:
                    @block.scalar
                    def _(s):
                        s.copy(z_row[0:1, 0:128], ctx.psR1[0:1, 0:128])
                with nc.Block() as block:
                    @block.vector
                    def _(v):
                        v.reciprocal(invz_row[0:1, 0:128], z_row[0:1, 0:128])
                with nc.Block() as block:
                    @block.tensor
                    def _(t):
                        t.matmul(ctx.psB[0:64, 0:128], ctx.ones_col[0:1, 0:64],
                                 invz_row[0:1, 0:128], start=True, stop=True)
                with nc.Block() as block:
                    @block.scalar
                    def _(s):
                        s.copy(invzbc[:, :], ctx.psB[0:64, 0:128])
                with nc.Block() as block:
                    @block.vector
                    def _(v):
                        v.tensor_mul(attn[ch][off:off + 64,
                                              qt * 128:(qt + 1) * 128],
                                     ctx.psX[0:64, 0:128], invzbc[:, :])

    wo_f32 = q_sb[1][:].bitcast(F32)[:, 0:1024]
    wo_bf = q_sb[2][:, 0:1024]
    xre = q_sb[0][:].bitcast(F32)[:, 0:1024]
    yst = k_sb[0][:].bitcast(F32)[:, 0:1024]
    wout_r = wout_dram.rearrange("(c p) n -> p c n", p=128)
    for oc in range(NCHUNK):
        dma_load(ctx, [(wo_f32.rearrange("p (c n) -> p c n", n=128),
                        wout_r[:, :, oc * 128:(oc + 1) * 128])])
        with nc.Block() as block:
            @block.vector
            def _(v):
                v.tensor_copy(wo_bf, wo_f32)
        with nc.Block() as block:
            @block.tensor
            def _(t):
                for dc in range(NCHUNK):
                    lhsT = wo_bf.rearrange("p (c n) -> p c n", n=128)[:, dc, :]
                    for f in range(4):
                        ps = [ctx.psX, ctx.psY][f // 2]
                        psl = slice((f % 2) * 512, (f % 2 + 1) * 512)
                        t.matmul(ps[:, psl], lhsT,
                                 attn[dc][:, f * 512:(f + 1) * 512],
                                 start=(dc == 0), stop=(dc == NCHUNK - 1))
        for half in range(2):
            tsl = slice(half * 1024, (half + 1) * 1024)
            dma_load(ctx, [(xre, x_dram[oc * 128:(oc + 1) * 128, tsl])])
            with nc.Block() as block:
                @block.vector
                def _(v):
                    bcol = bout_cols[:, oc:oc + 1]
                    v.scalar_tensor_tensor(
                        yst, [ctx.psX, ctx.psY][half][:], bcol, xre,
                        op0=ALU.add, op1=ALU.add)
            dma_store(ctx, [(y_dram[oc * 128:(oc + 1) * 128, tsl], yst)])




SCALE = 1.0 / 127.0
HID = 3072
NH = HID // 128


def build_l1ffn(ctx, x_dram, wg_dram, wu_dram, wd_dram, y_dram):
    nc = ctx.nc
    x_sb = [ctx.sbuf(f"x{c}", [128, T], F32) for c in range(NCHUNK)]
    xln = [ctx.sbuf(f"xln{c}", [128, T], BF16) for c in range(NCHUNK)]
    h_all = [ctx.sbuf(f"h{c}", [128, 1024], BF16) for c in range(NH)]
    wi_buf = ctx.sbuf("wi_buf", [128, HID], I32)
    wb_buf = ctx.sbuf("wb_buf", [128, HID], BF16)
    y_sb = [ctx.sbuf("y0", [128, 1024], F32)] * 2
    aux = alloc_ln_aux(ctx, T)
    sg_sb = ctx.sbuf("sg_sb", [128, 1024], F32)

    wg_r = wg_dram.rearrange("(c p) n -> p c n", p=128)
    wu_r = wu_dram.rearrange("(c p) n -> p c n", p=128)
    wd_r = wd_dram.rearrange("(c p) n -> p c n", p=128)

    dma_load(ctx, [(x_sb[c][:], x_dram[c * 128:(c + 1) * 128, :])
                   for c in range(NCHUNK)])
    emit_ln(ctx, [x[:] for x in x_sb], [o[:] for o in xln], aux)

    wiv = wi_buf[:].rearrange("p (c n) -> p c n", n=128)
    wbv = wb_buf[:].rearrange("p (c n) -> p c n", n=128)

    for half in range(2):
        tsl = slice(half * 1024, (half + 1) * 1024)
        for oc in range(NH):
            osl = slice(oc * 128, (oc + 1) * 128)
            dma_load(ctx, [
                (wiv[:, 0:8, :], wg_r[:, :, osl]),
                (wiv[:, 8:16, :], wu_r[:, :, osl]),
            ])
            with nc.Block() as block:
                @block.vector
                def _(v):
                    v.tensor_copy(wbv[:, 0:16, :], wiv[:, 0:16, :])
            with nc.Block() as block:
                @block.tensor
                def _(t):
                    for dc in range(NCHUNK):
                        for f in range(2):
                            qs = slice(half * 1024 + f * 512,
                                       half * 1024 + (f + 1) * 512)
                            ps = slice(f * 512, (f + 1) * 512)
                            t.matmul(ctx.psX[:, ps], wbv[:, dc, :],
                                     xln[dc][:, qs],
                                     start=(dc == 0), stop=(dc == NCHUNK - 1))
                            t.matmul(ctx.psY[:, ps], wbv[:, 8 + dc, :],
                                     xln[dc][:, qs],
                                     start=(dc == 0), stop=(dc == NCHUNK - 1))
            with nc.Block() as block:
                @block.scalar
                def _(s):
                    s.activation(sg_sb[:], ctx.psX[:], AF.Silu,
                                 scale=SCALE)
            with nc.Block() as block:
                @block.vector
                def _(v):
                    v.scalar_tensor_tensor(h_all[oc][:], ctx.psY[:], SCALE,
                                           sg_sb[:],
                                           op0=ALU.mult, op1=ALU.mult)
        for o in range(NCHUNK):
            osl = slice(o * 128, (o + 1) * 128)
            dma_load(ctx, [(wiv[:, 0:24, :], wd_r[:, :, osl])])
            with nc.Block() as block:
                @block.vector
                def _(v):
                    v.tensor_copy(wbv[:, 0:24, :], wiv[:, 0:24, :])
            with nc.Block() as block:
                @block.tensor
                def _(t):
                    for hc in range(NH):
                        for f in range(2):
                            ps = slice(f * 512, (f + 1) * 512)
                            t.matmul(ctx.psX[:, ps], wbv[:, hc, :],
                                     h_all[hc][:, ps],
                                     start=(hc == 0), stop=(hc == NH - 1))
            with nc.Block() as block:
                @block.vector
                def _(v):
                    v.scalar_tensor_tensor(y_sb[half][:], ctx.psX[:],
                                           SCALE, x_sb[o][:, tsl],
                                           op0=ALU.mult, op1=ALU.add)
            dma_store(ctx, [(y_dram[o * 128:(o + 1) * 128, tsl],
                             y_sb[half][:])])





_PROG_CACHE = {}


def _prog_l0attn():
    ctx = make_ctx()
    nc = ctx.nc
    x = nc.dram_tensor("x_T", [D, T], F32, kind="ExternalInput").ap()
    wqkv = nc.dram_tensor("wqkv_T", [D, 3 * D], F32, kind="ExternalInput").ap()
    bqkv = nc.dram_tensor("bqkv_cols", [128, 24], F32, kind="ExternalInput").ap()
    bvr = nc.dram_tensor("bv_row", [1, D], F32, kind="ExternalInput").ap()
    wout = nc.dram_tensor("wout_T", [D, D], F32, kind="ExternalInput").ap()
    bout = nc.dram_tensor("bout_cols", [128, 8], F32, kind="ExternalInput").ap()
    ident = nc.dram_tensor("ident128", [128, 128], F32, kind="ExternalInput").ap()
    y = nc.dram_tensor("y_T", [D, T], F32, kind="ExternalOutput").ap()
    with ctx.es:
        load_consts(ctx)
        build_l0attn(ctx, x, wqkv, bqkv, bvr, wout, bout, ident, y)
    return ctx.nc


def _prog_a1():
    ctx = make_ctx()
    nc = ctx.nc
    x = nc.dram_tensor("x_T", [D, T], F32, kind="ExternalInput").ap()
    wq = nc.dram_tensor("wq_i", [D, D], F32, kind="ExternalInput").ap()
    wk = nc.dram_tensor("wk_i", [D, D], F32, kind="ExternalInput").ap()
    wv = nc.dram_tensor("wv_i", [D, D], F32, kind="ExternalInput").ap()
    wp = nc.dram_tensor("wproj", [D, D], F32, kind="ExternalInput").ap()
    g1 = nc.dram_tensor("g1_row", [1, D], F32, kind="ExternalInput").ap()
    b1 = nc.dram_tensor("b1_row", [1, D], F32, kind="ExternalInput").ap()
    cs = nc.dram_tensor("cos_tab", [128, T], F32, kind="ExternalInput").ap()
    sn = nc.dram_tensor("sin_tab", [128, T], F32, kind="ExternalInput").ap()
    ident = nc.dram_tensor("ident128", [128, 128], F32, kind="ExternalInput").ap()
    mkt = nc.dram_tensor("maskt", [128, 128], F32, kind="ExternalInput").ap()
    xf = nc.dram_tensor("xf_T", [D, T], F32, kind="ExternalOutput").ap()
    with ctx.es:
        load_consts(ctx)
        build_a1(ctx, x, wq, wk, wv, wp, g1, b1, cs, sn, ident, mkt, xf)
    return ctx.nc


def _prog_a2():
    ctx = make_ctx()
    nc = ctx.nc
    xf = nc.dram_tensor("xf_T", [D, T], F32, kind="ExternalInput").ap()
    x1 = nc.dram_tensor("x1_T", [D, T], F32, kind="ExternalInput").ap()
    g2 = nc.dram_tensor("g2_row", [1, D], F32, kind="ExternalInput").ap()
    b2 = nc.dram_tensor("b2_row", [1, D], F32, kind="ExternalInput").ap()
    wg = nc.dram_tensor("wgate", [D, 8], F32, kind="ExternalInput").ap()
    ident = nc.dram_tensor("ident128", [128, 128], F32, kind="ExternalInput").ap()
    h2t = nc.dram_tensor("h2tok", [T, D], BF16, kind="ExternalOutput").ap()
    bt = nc.dram_tensor("base_tok", [T, D], F32, kind="ExternalOutput").ap()
    gd = nc.dram_tensor("gates_sm", [T, 8], F32, kind="ExternalOutput").ap()
    md = nc.dram_tensor("mask_tok", [T, 8], F32, kind="ExternalOutput").ap()
    pd = nc.dram_tensor("pos_tok", [T, 8], F32, kind="ExternalOutput").ap()
    cd = nc.dram_tensor("counts_row", [1, 8], F32, kind="ExternalOutput").ap()
    with ctx.es:
        load_consts(ctx)
        build_a2(ctx, xf, x1, g2, b2, wg, ident, h2t, bt, gd, md, pd, cd)
    return ctx.nc


def _prog_b():
    ctx = make_ctx(dma_scratch=32768)
    nc = ctx.nc
    cnt = nc.dram_tensor("counts_all", [8, 8], F32, kind="ExternalInput").ap()
    prf = nc.dram_tensor("prefmask", [8, 1], F32, kind="ExternalInput").ap()
    h2 = nc.dram_tensor("h2tok", [T, D], BF16, kind="ExternalInput").ap()
    bs = nc.dram_tensor("base_tok", [T, D], F32, kind="ExternalInput").ap()
    gd = nc.dram_tensor("gates_sm", [T, 8], F32, kind="ExternalInput").ap()
    md = nc.dram_tensor("mask_tok", [T, 8], F32, kind="ExternalInput").ap()
    pd = nc.dram_tensor("pos_tok", [T, 8], F32, kind="ExternalInput").ap()
    w1 = nc.dram_tensor("w1_all", [8, D, 4096], F32, kind="ExternalInput").ap()
    b1c = nc.dram_tensor("b1c_all", [8, 128, 32], F32, kind="ExternalInput").ap()
    w2 = nc.dram_tensor("w2_all", [8, 4096, D], F32, kind="ExternalInput").ap()
    b2r = nc.dram_tensor("b2r_all", [8, 1, D], F32, kind="ExternalInput").ap()
    iot = nc.dram_tensor("iota768", [128, CAP], F32, kind="ExternalInput").ap()
    tki = nc.dram_tensor("tokidx_cols", [128, 16], I32, kind="ExternalInput").ap()
    ident = nc.dram_tensor("ident128", [128, 128], F32, kind="ExternalInput").ap()
    y = nc.dram_tensor("y_tok", [T, D], F32, kind="ExternalOutput").ap()
    with ctx.es:
        load_consts(ctx, pool=False)
        build_b(ctx, cnt, prf, h2, bs, gd, md, pd, w1, b1c, w2, b2r,
                iot, tki, ident, y)
    return ctx.nc


def _prog_l1attn():
    ctx = make_ctx()
    nc = ctx.nc
    x = nc.dram_tensor("x_T", [D, T], F32, kind="ExternalInput").ap()
    wqkv = nc.dram_tensor("wqkv_T", [D, 3 * D], F32, kind="ExternalInput").ap()
    bqkv = nc.dram_tensor("bqkv_cols", [128, 24], F32, kind="ExternalInput").ap()
    bvr = nc.dram_tensor("bv_row", [1, D], F32, kind="ExternalInput").ap()
    wout = nc.dram_tensor("wout_T", [D, D], F32, kind="ExternalInput").ap()
    bout = nc.dram_tensor("bout_cols", [128, 8], F32, kind="ExternalInput").ap()
    ident = nc.dram_tensor("ident128", [128, 128], F32, kind="ExternalInput").ap()
    y = nc.dram_tensor("y_T", [D, T], F32, kind="ExternalOutput").ap()
    with ctx.es:
        load_consts(ctx)
        ctx.ident_bf = ctx.sbuf("ident_bf", [128, 128], BF16)
        build_l1attn(ctx, x, wqkv, bqkv, bvr, wout, bout, ident, y)
    return ctx.nc


def _prog_l1ffn():
    ctx = make_ctx()
    nc = ctx.nc
    x = nc.dram_tensor("x_T", [D, T], F32, kind="ExternalInput").ap()
    wg = nc.dram_tensor("wg_T", [D, HID], I32, kind="ExternalInput").ap()
    wu = nc.dram_tensor("wu_T", [D, HID], I32, kind="ExternalInput").ap()
    wd = nc.dram_tensor("wd_T", [HID, D], I32, kind="ExternalInput").ap()
    y = nc.dram_tensor("y_T", [D, T], F32, kind="ExternalOutput").ap()
    with ctx.es:
        load_consts(ctx)
        build_l1ffn(ctx, x, wg, wu, wd, y)
    return ctx.nc


def _get_prog(name):
    if name not in _PROG_CACHE:
        _PROG_CACHE[name] = {
            "l0attn": _prog_l0attn, "a1": _prog_a1, "a2": _prog_a2,
            "b": _prog_b, "l1attn": _prog_l1attn, "l1ffn": _prog_l1ffn,
        }[name]()
    return _PROG_CACHE[name]



def _rope_tables():
    hd = 64
    inv_freq = 1.0 / 10000.0 ** (np.arange(0, hd, 2, dtype=np.float32) / hd)
    ang = np.arange(T, dtype=np.float32)[:, None] * inv_freq
    sin = np.sin(ang).astype(np.float32).T
    cos = np.cos(ang).astype(np.float32).T
    C = np.empty((128, T), np.float32)
    S = np.empty((128, T), np.float32)
    for g in range(2):
        o = g * 64
        C[o + 0:o + 64:2] = cos
        C[o + 1:o + 64:2] = cos
        S[o + 0:o + 64:2] = -sin
        S[o + 1:o + 64:2] = sin
    return C, S


def _perm_inner_qkv(Wqkv):
    perm = np.empty(64, np.int64)
    perm[0::2] = np.arange(32)
    perm[1::2] = np.arange(32, 64)
    wq = np.empty((D, D), np.float32)
    wk = np.empty((D, D), np.float32)
    wv = np.empty((D, D), np.float32)
    for h in range(16):
        base = h * 192
        wq[:, h * 64:(h + 1) * 64] = Wqkv[:, base:base + 64][:, perm]
        wk[:, h * 64:(h + 1) * 64] = Wqkv[:, base + 64:base + 128][:, perm]
        wv[:, h * 64:(h + 1) * 64] = Wqkv[:, base + 128:base + 192]
    return wq, wk, wv



def kernel(L0_attn_x, L0_attn_w_qkv, L0_attn_b_qkv, L0_attn_w_out,
           L0_attn_b_out, L0_ffn_ln1_g, L0_ffn_ln1_b, L0_ffn_Wqkv,
           L0_ffn_Wproj, L0_ffn_ln2_g, L0_ffn_ln2_b, L0_ffn_Wgate,
           L0_ffn_W1, L0_ffn_b1, L0_ffn_W2, L0_ffn_b2,
           L1_attn_W_qkv, L1_attn_b_qkv, L1_attn_W_out, L1_attn_b_out,
           L1_ffn_w_gate, L1_ffn_w_up, L1_ffn_w_down):
    B = 8
    asnp = np.asarray
    ident = np.eye(128, dtype=np.float32)
    cores = list(range(B))

    def run(name, in_maps):
        nc = _get_prog(name)
        return run_bass_kernel_spmd(nc, in_maps, core_ids=cores).results

    x0 = asnp(L0_attn_x)
    wqkv0 = asnp(L0_attn_w_qkv)
    bqkv0 = asnp(L0_attn_b_qkv)
    common0 = {
        "wqkv_T": np.ascontiguousarray(wqkv0.T),
        "bqkv_cols": np.ascontiguousarray(bqkv0.reshape(24, 128).T),
        "bv_row": np.ascontiguousarray(bqkv0[2048:3072].reshape(1, D)),
        "wout_T": np.ascontiguousarray(asnp(L0_attn_w_out).T),
        "bout_cols": np.ascontiguousarray(asnp(L0_attn_b_out).reshape(8, 128).T),
        "ident128": ident,
    }
    r1 = run("l0attn", [dict(common0, x_T=np.ascontiguousarray(x0[b].T))
                        for b in range(B)])
    x1 = [r1[b]["y_T"] for b in range(B)]

    wq_i, wk_i, wv_i = _perm_inner_qkv(asnp(L0_ffn_Wqkv))
    C, S = _rope_tables()
    maskt = np.where(np.arange(128)[:, None] > np.arange(128)[None, :],
                     np.float32(-1e9 * 8.0), np.float32(0.0))
    commonA1 = {
        "wq_i": wq_i, "wk_i": wk_i, "wv_i": wv_i,
        "wproj": np.ascontiguousarray(asnp(L0_ffn_Wproj)),
        "g1_row": asnp(L0_ffn_ln1_g).reshape(1, D),
        "b1_row": asnp(L0_ffn_ln1_b).reshape(1, D),
        "cos_tab": C, "sin_tab": S, "ident128": ident, "maskt": maskt,
    }
    r2 = run("a1", [dict(commonA1, x_T=x1[b]) for b in range(B)])
    xf = [r2[b]["xf_T"] for b in range(B)]

    commonA2 = {
        "g2_row": asnp(L0_ffn_ln2_g).reshape(1, D),
        "b2_row": asnp(L0_ffn_ln2_b).reshape(1, D),
        "wgate": np.ascontiguousarray(asnp(L0_ffn_Wgate)),
        "ident128": ident,
    }
    r3 = run("a2", [dict(commonA2, xf_T=xf[b], x1_T=x1[b]) for b in range(B)])

    counts_all = np.ascontiguousarray(
        np.concatenate([r3[b]["counts_row"] for b in range(B)], axis=0))
    iota768 = np.broadcast_to(np.arange(CAP, dtype=np.float32),
                              (128, CAP)).copy()
    tki = np.ascontiguousarray(
        np.arange(T, dtype=np.int32).reshape(16, 128).T)
    commonB = {
        "counts_all": counts_all,
        "w1_all": np.ascontiguousarray(asnp(L0_ffn_W1)),
        "b1c_all": np.ascontiguousarray(
            asnp(L0_ffn_b1).reshape(8, 32, 128).transpose(0, 2, 1)),
        "w2_all": np.ascontiguousarray(asnp(L0_ffn_W2)),
        "b2r_all": np.ascontiguousarray(asnp(L0_ffn_b2).reshape(8, 1, D)),
        "iota768": iota768, "tokidx_cols": tki, "ident128": ident,
    }
    in4 = []
    for b in range(B):
        pref = np.zeros((8, 1), np.float32)
        pref[:b] = 1.0
        in4.append(dict(commonB, prefmask=pref,
                        h2tok=r3[b]["h2tok"], base_tok=r3[b]["base_tok"],
                        gates_sm=r3[b]["gates_sm"],
                        mask_tok=r3[b]["mask_tok"],
                        pos_tok=r3[b]["pos_tok"]))
    r4 = run("b", in4)
    x2 = [np.ascontiguousarray(r4[b]["y_tok"].T) for b in range(B)]

    wqkv1 = asnp(L1_attn_W_qkv)
    bqkv1 = asnp(L1_attn_b_qkv)
    common5 = {
        "wqkv_T": np.ascontiguousarray(wqkv1.T),
        "bqkv_cols": np.ascontiguousarray(bqkv1.reshape(24, 128).T),
        "bv_row": np.ascontiguousarray(bqkv1[2048:3072].reshape(1, D)),
        "wout_T": np.ascontiguousarray(asnp(L1_attn_W_out).T),
        "bout_cols": np.ascontiguousarray(asnp(L1_attn_b_out).reshape(8, 128).T),
        "ident128": ident,
    }
    r5 = run("l1attn", [dict(common5, x_T=x2[b]) for b in range(B)])
    x3 = [r5[b]["y_T"] for b in range(B)]

    common6 = {
        "wg_T": np.ascontiguousarray(asnp(L1_ffn_w_gate).T),
        "wu_T": np.ascontiguousarray(asnp(L1_ffn_w_up).T),
        "wd_T": np.ascontiguousarray(asnp(L1_ffn_w_down).T),
    }
    r6 = run("l1ffn", [dict(common6, x_T=x3[b]) for b in range(B)])

    out = np.stack([r6[b]["y_T"].T for b in range(B)], axis=0)
    return out.astype(np.float32)
